# version 03
# baseline (speedup 1.0000x reference)
import math
import numpy as np
import ml_dtypes
import concourse.bass as bass
import concourse.mybir as mybir
from concourse.ap import AP
from concourse.bass_utils import run_bass_kernel_spmd

F32 = mybir.dt.float32
BF16 = mybir.dt.bfloat16
I32 = mybir.dt.int32
AF = mybir.ActivationFunctionType
ALU = mybir.AluOpType
AX = mybir.AxisListType

T = 2048
D = 1024
NT = 16
NC = 8
PAD = 8
XTW = PAD + T
BF = ml_dtypes.bfloat16


def _bf(x):
    return np.ascontiguousarray(x, dtype=BF)


def _f32(x):
    return np.ascontiguousarray(np.asarray(x), dtype=np.float32)


def host_constants():
    return {
        "ident": np.eye(128, dtype=BF),
        "identf": np.eye(128, dtype=np.float32),
    }


def _ap_span(ap):
    t = ap.tensor
    name = t.name
    kind = type(t).__name__
    if "PSum" in kind:
        return (name, 0, 1 << 40)
    esz = mybir.dt.sizeof(ap.dtype) if hasattr(mybir.dt, "sizeof") else {
        F32: 4, BF16: 2, I32: 4}.get(ap.dtype, 4)
    dims = list(ap.ap)
    is_dram = "DRam" in kind
    off = ap.offset
    if not is_dram and dims:
        pitch = abs(dims[0][0])
        if pitch > 0:
            off = off % pitch
    lo = off * esz
    reach = 0
    use = dims if is_dram else dims[1:]
    for step, cnt in use:
        if cnt > 1:
            reach += abs(step) * (cnt - 1)
    return (name, lo, lo + (reach + 1) * esz)


class Prog:

    def __init__(self, name="prog"):
        self.nc = bass.Bass("TRN2", target_bir_lowering=False)
        self.name = name
        self.inputs = {}
        self.outputs = {}
        self._sb = []
        self._psum = []

    def dram_in(self, name, shape, dtype=F32):
        ap = self.nc.dram_tensor(name, list(shape), dtype, kind="ExternalInput").ap()
        self.inputs[name] = ap
        return ap

    def dram_out(self, name, shape, dtype=F32):
        ap = self.nc.dram_tensor(name, list(shape), dtype, kind="ExternalOutput").ap()
        self.outputs[name] = ap
        return ap

    def dram_tmp(self, name, shape, dtype=F32, addr_space="Local"):
        return self.nc.dram_tensor(name, list(shape), dtype, kind="Internal",
                                   addr_space=addr_space).ap()

    def sb(self, name, shape, dtype):
        h = self.nc.sbuf_tensor(name, list(shape), dtype).__enter__()
        self._sb.append(h)
        return h

    def ps(self, name, shape, dtype=F32):
        h = self.nc.psum_tensor(name, list(shape), dtype).__enter__()
        self._psum.append(h)
        return h

    def begin(self):
        self.sems = {}
        qs = [f"q_{q}{i}" for q in ("sp", "gp", "act") for i in range(4)]
        for s in ["pe", "dve", "act", "pool"] + qs:
            self.sems[s] = [self.nc.semaphore(s).__enter__(), 0]
        self.X = self.sb("X", [128, NT * D], F32)
        self.XT = self.sb("XT", [128, NC * XTW], BF16)
        self.TMPB = self.sb("TMPB", [128, 2 * D], BF16)
        self.WSA = self.sb("WSA", [128, 16384], BF16)
        self.WSB = self.sb("WSB", [128, 16384], BF16)
        self.WSC = self.sb("WSC", [128, 8192], BF16)
        self.WSF = self.sb("WSF", [128, 3072], F32)
        self.ident = self.sb("identc", [128, 128], BF16)
        self.identf = self.sb("identfc", [128, 128], F32)
        self.stats = self.sb("statsb", [128, NT * 8 + 16], F32)
        self.PP = [self.ps(f"pp{i}", [128, 512], F32) for i in range(4)]
        self.PTA = self.ps("pta", [128, 1024], BF16)
        self.PTB = self.ps("ptb", [128, 1024], BF16)
        self.PF = self.ps("pfb", [128, 512], F32)
        self.PTF = self.ps("ptf", [128, 512], F32)
        self._mm_pend = {}
        self._wr = {}
        self._rd = {}
        self._waited = {}
        self._qstate = {}
        self._eng_sem = {"tensor": "pe", "vector": "dve", "scalar": "act",
                         "gpsimd": "pool"}
        self.block = self.nc.Block().__enter__()
        return self

    def end(self):
        self.block.__exit__(None, None, None)

    def inc(self, instr, sname, k=1):
        s = self.sems[sname]
        s[1] += k
        instr.then_inc(s[0], k)
        return s[1]

    def _deps_for(self, reads, writes, own_sem=None):
        deps = {}

        def scan(ap, lists, skip_own=False):
            name, lo, hi = _ap_span(ap)
            for li, lst in enumerate(lists):
                for (l, h, sem, val) in lst.get(name, ()):
                    if skip_own and li == 0 and sem == own_sem:
                        continue
                    if l < hi and lo < h and val > deps.get(sem, 0):
                        deps[sem] = val

        for ap in reads:
            scan(ap, (self._wr,))
        for ap in writes:
            scan(ap, (self._wr, self._rd))
        return deps

    def _apply_waits(self, eng, ename, deps):
        waited = self._waited.setdefault(ename, {})
        for sem, val in sorted(deps.items()):
            if waited.get(sem, 0) >= val:
                continue
            eng.wait_ge(self.sems[sem][0], val)
            waited[sem] = val

    def _record(self, reads, writes, sem, val):
        for ap in reads:
            name, lo, hi = _ap_span(ap)
            self._rd.setdefault(name, []).append((lo, hi, sem, val))
        for ap in writes:
            name, lo, hi = _ap_span(ap)
            wl = self._wr.setdefault(name, [])
            wl[:] = [e for e in wl if not (lo <= e[0] and e[1] <= hi)]
            wl.append((lo, hi, sem, val))
            rl = self._rd.get(name)
            if rl:
                rl[:] = [e for e in rl if not (lo <= e[0] and e[1] <= hi)]
        for d in (self._wr, self._rd):
            for ap in writes:
                name = _ap_span(ap)[0]
                l = d.get(name)
                if l and len(l) > 192:
                    best = {}
                    lo = min(e[0] for e in l)
                    hi = max(e[1] for e in l)
                    for (_, _, sem, val) in l:
                        best[sem] = max(best.get(sem, 0), val)
                    d[name] = [(lo, hi, sem, val) for sem, val in best.items()]

    def emit(self, ename, method, *args, xreads=(), xwrites=(), **kw):
        eng = getattr(self.nc, ename)
        esem = self._eng_sem[ename]
        writes = list(xwrites)
        reads = list(xreads)
        seen_out = False
        for a in args:
            if isinstance(a, AP):
                if not seen_out:
                    writes.append(a)
                    seen_out = True
                else:
                    reads.append(a)
        for k, v in kw.items():
            if isinstance(v, AP):
                if k in ("accum_out",):
                    writes.append(v)
                else:
                    reads.append(v)
        deps = self._deps_for(reads, writes)
        self._apply_waits(eng, ename, deps)
        getattr(eng, method)(*args, **kw)
        ins = eng
        raise RuntimeError("use _emit")

    def _emit(self, ename, method, args, kw, xreads=(), xwrites=()):
        eng = getattr(self.nc, ename)
        esem = self._eng_sem[ename]
        writes = list(xwrites)
        reads = list(xreads)
        seen_out = False
        for a in args:
            if isinstance(a, AP):
                if not seen_out:
                    writes.append(a)
                    seen_out = True
                else:
                    reads.append(a)
        for k, v in kw.items():
            if isinstance(v, AP):
                if k in ("accum_out",):
                    writes.append(v)
                else:
                    reads.append(v)
        deps = self._deps_for(reads, writes, own_sem=esem)
        self._apply_waits(eng, ename, deps)
        ins = getattr(eng, method)(*args, **kw)
        val = self.inc(ins, esem)
        self._record(reads, writes, esem, val)
        return (esem, val)

    def V(self, method, *args, **kw):
        return self._emit("vector", method, args, kw)

    def S(self, method, *args, **kw):
        return self._emit("scalar", method, args, kw)

    def G(self, method, *args, **kw):
        return self._emit("gpsimd", method, args, kw)

    def mm(self, out, lhsT, rhs, start, stop):
        eng = self.nc.tensor
        reads = [lhsT, rhs]
        deps = self._deps_for(reads, [out] if start else [], own_sem="pe")
        self._apply_waits(eng, "tensor", deps)
        ins = eng.matmul(out, lhsT, rhs, start=start, stop=stop)
        key = _ap_span(out)[0]
        pend = self._mm_pend.setdefault(key, [])
        pend.extend(reads)
        if stop:
            val = self.inc(ins, "pe")
            self._record(pend, [out], "pe", val)
            self._mm_pend[key] = []
            return ("pe", val)
        return None

    def transpose(self, out_psum, src, identity):
        eng = self.nc.tensor
        reads = [src, identity]
        deps = self._deps_for(reads, [out_psum], own_sem="pe")
        self._apply_waits(eng, "tensor", deps)
        ins = eng.transpose(out_psum, src, identity)
        val = self.inc(ins, "pe")
        self._record(reads, [out_psum], "pe", val)
        return ("pe", val)

    def _qname(self, qeng):
        return {id(self.nc.sync): "sp", id(self.nc.gpsimd): "gp",
                id(self.nc.scalar): "act"}[id(qeng)]

    def dma(self, qeng, out, in_, **kw):
        qn = self._qname(qeng)
        st = self._qstate.setdefault(qn, {"ctr": 0})
        sname = f"q_{qn}{st['ctr'] % 4}"
        st["ctr"] += 1
        deps = self._deps_for([in_], [out])
        prev = self.sems[sname][1]
        if prev > 0:
            deps[sname] = max(deps.get(sname, 0), prev)
        ename = f"q_{qn}"
        waited = self._waited.setdefault(ename, {})
        for sem, val in sorted(deps.items()):
            if waited.get(sem, 0) >= val:
                continue
            qeng.wait_ge(self.sems[sem][0], val)
            waited[sem] = val
        ins = qeng.dma_start(out=out, in_=in_, **kw)
        val = self.inc(ins, sname, 16)
        self._record([in_], [out], sname, val)
        return (sname, val)

    def wait_tok(self, eng, tok):
        if tok is not None:
            eng.wait_ge(self.sems[tok[0]][0], tok[1])

    def x_tile(self, tt):
        return self.X[:, tt * D:(tt + 1) * D]

    def tmpb(self, i):
        return self.TMPB[:, (i % 2) * D:(i % 2) * D + D]

    def xt_chunk(self, c, t0=0, tn=T):
        return self.XT[:, c * XTW + PAD + t0: c * XTW + PAD + t0 + tn]

    def pe_transpose(self, jobs):
        pend = []
        for i, (src, dst) in enumerate(jobs):
            bank = (self.PTA, self.PTB)[(i // 8) % 2]
            s = i % 8
            pt = bank[:, s * 128:(s + 1) * 128]
            self.transpose(pt, src, self.ident[:, :])
            pend.append((pt, dst))
            if len(pend) == 8:
                for (p, d) in pend:
                    self.V("tensor_copy", d, p)
                pend = []
        for (p, d) in pend:
            self.V("tensor_copy", d, p)

    def pe_transpose_f32(self, jobs, rows=128):
        for i, (src, dst) in enumerate(jobs):
            bank = (self.PTF, self.PF)[i % 2]
            ncols = src.shape[-1]
            pt = bank[0:ncols, 0:rows]
            self.transpose(pt, src, self.identf[0:rows, 0:rows])
            self.V("tensor_copy", dst, pt)


def emit_load_stream(P, x_dram, build_xt=True):
    nc = P.nc
    xr = x_dram.rearrange("(tt p) d -> p tt d", p=128)
    P.dma(nc.sync, P.X[:, :].rearrange("p (tt d) -> p tt d", tt=NT), xr)
    P.dma(nc.sync, P.ident[:, :], P.inputs["ident"])
    P.dma(nc.sync, P.identf[:, :], P.inputs["identf"])
    for c in range(NC):
        P.V("memset", P.XT[:, c * XTW: c * XTW + PAD], 0.0)
    if build_xt:
        emit_rebuild_xt(P)


def emit_rebuild_xt(P):
    for tt in range(NT):
        tb = P.tmpb(tt)
        P.V("tensor_copy", tb, P.x_tile(tt))
        jobs = []
        for c in range(NC):
            src = tb[:, c * 128:(c + 1) * 128]
            dst = P.XT[:, c * XTW + PAD + tt * 128: c * XTW + PAD + (tt + 1) * 128]
            jobs.append((src, dst))
        P.pe_transpose(jobs)


def emit_ln_stats(P):
    st = P.stats
    scr = NT * 8
    for tt in range(NT):
        xt = P.x_tile(tt)
        o = tt * 8
        P.V("bn_stats", st[:, scr:scr + 6], xt[:, 0:512])
        P.V("bn_stats", st[:, scr + 6:scr + 12], xt[:, 512:1024])
        P.V("bn_aggr", st[:, o:o + 2], st[:, scr:scr + 12])
        P.V("tensor_scalar", st[:, o + 2:o + 3], st[:, o + 1:o + 2], 1e-5, None,
            ALU.add)
    for tt in range(NT):
        o = tt * 8
        P.S("activation", st[:, o + 3:o + 4], st[:, o + 2:o + 3], AF.Sqrt)
    for tt in range(NT):
        o = tt * 8
        P.V("reciprocal", st[:, o + 4:o + 5], st[:, o + 3:o + 4])
        P.V("tensor_tensor", st[:, o + 6:o + 7], st[:, o:o + 1],
            st[:, o + 4:o + 5], ALU.mult)
        P.V("tensor_scalar", st[:, o + 5:o + 6], st[:, o + 6:o + 7], -1.0, None,
            ALU.mult)


def emit_boundary_ln(P, rebuild=True):
    st = P.stats
    emit_ln_stats(P)
    for tt in range(NT):
        o = tt * 8
        P.V("tensor_scalar", P.x_tile(tt), P.x_tile(tt),
            st[:, o + 4:o + 5], st[:, o + 5:o + 6], ALU.mult, ALU.add)
    if rebuild:
        emit_rebuild_xt(P)


def emit_store_stream(P, x_out_dram):
    nc = P.nc
    xr = x_out_dram.rearrange("(tt p) d -> p tt d", p=128)
    tok = P.dma(nc.sync, xr, P.X[:, :].rearrange("p (tt d) -> p tt d", tt=NT))
    P.wait_tok(nc.sync, tok)
    return tok


def emit_router_logits(P, rw_dram, E, lgts, tag, pre_scale=None, rwf_hook=None,
                       wsf_overlay=False):
    nc = P.nc
    if wsf_overlay:
        rwf = P.WSF[:, 640:640 + NC * E]
        rwh = P.WSF[:, 704:704 + NC * E // 2].bitcast(BF16)
        rwl = P.WSF[:, 736:736 + NC * E // 2].bitcast(BF16)
        rowbuf = P.WSF[0:E, 128:640]
    else:
        rwf = P.sb(f"rwf_{tag}", [128, NC * E], F32)
        rwh = P.sb(f"rwh_{tag}", [128, NC * E], BF16)
        rwl = P.sb(f"rwl_{tag}", [128, NC * E], BF16)
        rowbuf = P.sb(f"rowb_{tag}", [E, 512], F32)
    xlotf = P.WSA[:, 0:NC * T]

    P.dma(nc.sync, rwf[:, :].rearrange("p (c e) -> p c e", c=NC),
          rw_dram.rearrange("(c p) e -> p c e", p=128))
    if rwf_hook is not None:
        rwf_hook(rwf)
    if pre_scale is not None:
        for c in range(NC):
            P.V("tensor_scalar", rwf[:, c * E:(c + 1) * E],
                rwf[:, c * E:(c + 1) * E], pre_scale[:, c:c + 1], None, ALU.mult)
    P.V("tensor_copy", rwh[:, :], rwf[:, :])
    P.V("tensor_tensor", rwl[:, :], rwf[:, :], rwh[:, :], ALU.subtract)

    for tt in range(NT):
        tb = P.tmpb(tt)
        P.V("tensor_copy", tb, P.x_tile(tt))
        P.V("tensor_tensor", tb, P.x_tile(tt), tb, ALU.subtract)
        jobs = []
        for c in range(NC):
            jobs.append((tb[:, c * 128:(c + 1) * 128],
                         xlotf[:, c * T + tt * 128: c * T + (tt + 1) * 128]))
        P.pe_transpose(jobs)

    plv = P.PF[0:E, 0:512]
    for ts in range(T // 512):
        for c in range(NC):
            hiT = P.xt_chunk(c, ts * 512, 512)
            loT = xlotf[:, c * T + ts * 512: c * T + ts * 512 + 512]
            rh = rwh[:, c * E:(c + 1) * E]
            rl = rwl[:, c * E:(c + 1) * E]
            P.mm(plv, rh, hiT, start=(c == 0), stop=False)
            P.mm(plv, rh, loT, start=False, stop=False)
            P.mm(plv, rl, hiT, start=False, stop=(c == NC - 1))
        P.V("tensor_copy", rowbuf[:, :], plv)
        jobs = []
        for q in range(4):
            tt = ts * 4 + q
            jobs.append((rowbuf[0:E, q * 128:(q + 1) * 128],
                         lgts[:, tt * E:(tt + 1) * E]))
        P.pe_transpose_f32(jobs, rows=E)


def emit_dense_moe(P, sel, w1_dram, w2_dram, E, I, tag, hidden_act=AF.Silu,
                   up_off=None, b1_sb=None, xt=None):
    nc = P.nc
    xt = xt or P.xt_chunk
    NIC = I // 128
    gate_up = up_off is not None
    nreg = 4 if gate_up else 2
    wq = P.WSB[:, 0:nreg * NC * 256 * 2]
    w2s = P.WSC[:, 0:8 * 1024]
    HT = P.WSA[:, 0:8 * T]
    sg = P.sb(f"{tag}_sg", [128, 512], BF16)
    NSEC = I // 1024

    for e in range(E):
        for sec in range(NSEC):
            P.dma(nc.gpsimd, w2s.rearrange("p (i n) -> p i n", i=8),
                  w2_dram[e, sec * 1024:(sec + 1) * 1024]
                  .rearrange("(i p) n -> p i n", p=128))
            for q in range(2):
                qq = sec * 2 + q
                b = qq % 2
                wqv = wq[:, b * nreg * NC * 256:(b + 1) * nreg * NC * 256]
                g0 = 512 * qq
                srcs = [g0, g0 + 256]
                if gate_up:
                    srcs += [up_off + g0, up_off + g0 + 256]
                for r, base in enumerate(srcs):
                    P.dma(nc.gpsimd,
                          wqv[:, r * NC * 256:(r + 1) * NC * 256]
                          .rearrange("p (c n) -> p c n", c=NC),
                          w1_dram[e, :, base:base + 256]
                          .rearrange("(c p) n -> p c n", p=128))
                for ic in range(4):
                    reg = ic // 2
                    off = (ic % 2) * 128
                    for ts in range(4):
                        it = (ic * 4 + ts) % 2
                        pg = P.PP[2 * it][:, :]
                        pu = P.PP[2 * it + 1][:, :]
                        for c in range(NC):
                            gw = wqv[:, reg * NC * 256 + c * 256 + off:
                                     reg * NC * 256 + c * 256 + off + 128]
                            hiT = xt(c, ts * 512, 512)
                            P.mm(pg, gw, hiT, start=(c == 0), stop=(c == NC - 1))
                            if gate_up:
                                uw = wqv[:, (2 + reg) * NC * 256 + c * 256 + off:
                                         (2 + reg) * NC * 256 + c * 256 + off + 128]
                                P.mm(pu, uw, hiT, start=(c == 0),
                                     stop=(c == NC - 1))
                        hc = q * 4 + ic
                        ichunk_g = sec * 8 + hc
                        hdst = HT[:, hc * T + ts * 512: hc * T + ts * 512 + 512]
                        bias = (b1_sb[:, e * NIC + ichunk_g: e * NIC + ichunk_g + 1]
                                if b1_sb is not None else 0.0)
                        if gate_up:
                            P.S("activation", sg[:, :], pg, hidden_act, bias=bias)
                            P.V("tensor_tensor", hdst, sg[:, :], pu, ALU.mult)
                        else:
                            P.S("activation", hdst, pg, hidden_act, bias=bias)
            for tt in range(NT):
                for dh in range(2):
                    it = (tt * 2 + dh) % 2
                    py = P.PP[2 * it][:, :]
                    for hc in range(8):
                        lh = HT[:, hc * T + tt * 128: hc * T + (tt + 1) * 128]
                        rw2 = w2s[:, hc * 1024 + dh * 512:
                                  hc * 1024 + dh * 512 + 512]
                        P.mm(py, lh, rw2, start=(hc == 0), stop=(hc == 7))
                    xdst = P.X[:, tt * D + dh * 512: tt * D + dh * 512 + 512]
                    if sel is not None:
                        P.V("scalar_tensor_tensor", xdst, py,
                            sel[:, tt * E + e: tt * E + e + 1], xdst,
                            ALU.mult, ALU.add)
                    else:
                        P.V("tensor_tensor", xdst, xdst, py, ALU.add)


def emit_L3_ffn(P, rw_dram, w13_dram, w2_dram):
    E = 8
    lgts = P.sb("l3f_lgts", [128, NT * E], F32)
    sel = P.sb("l3f_sel", [128, NT * E], F32)
    probs = P.sb("l3f_probs", [128, NT * E], F32)
    scr = P.sb("l3f_scr", [128, NT * 8], F32)
    emit_router_logits(P, rw_dram, E, lgts, "l3f")

    for tt in range(NT):
        lg = lgts[:, tt * E:(tt + 1) * E]
        pr = probs[:, tt * E:(tt + 1) * E]
        t8 = scr[:, tt * 8: tt * 8 + 8]
        P.V("tensor_reduce", t8[:, 0:1], lg, AX.X, ALU.max, negate=True)
        P.S("activation", pr, lg, AF.Exp, bias=t8[:, 0:1], scale=1.0,
            accum_out=t8[:, 1:2])
        P.V("reciprocal", t8[:, 2:3], t8[:, 1:2])
        P.V("tensor_scalar", pr, pr, t8[:, 2:3], None, ALU.mult)
        P.V("max", t8[:, 0:8], pr)
        P.V("tensor_tensor", t8[:, 6:7], t8[:, 0:1], t8[:, 1:2], ALU.add)
        P.V("tensor_scalar", t8[:, 6:7], t8[:, 6:7], 1e-6, None, ALU.add)
        P.V("reciprocal", t8[:, 7:8], t8[:, 6:7])
        sl = sel[:, tt * E:(tt + 1) * E]
        sc1 = lg
        P.V("tensor_scalar", sc1, pr, t8[:, 0:1], None, ALU.is_equal)
        P.V("tensor_scalar", sc1, sc1, t8[:, 0:1], None, ALU.mult)
        P.V("tensor_scalar", sl, pr, t8[:, 1:2], None, ALU.is_equal)
        P.V("tensor_scalar", sl, sl, t8[:, 1:2], None, ALU.mult)
        P.V("tensor_tensor", sl, sl, sc1, ALU.add)
        P.V("tensor_scalar", sl, sl, t8[:, 7:8], None, ALU.mult)

    emit_dense_moe(P, sel, w13_dram, w2_dram, E=8, I=2048, tag="l3f", up_off=2048)
    emit_boundary_ln(P)


def build_l3_ffn_prog():
    P = Prog("l3ffn")
    x_in = P.dram_in("x_in", [T, D])
    rw = P.dram_in("rw_t", [D, 8])
    w13 = P.dram_in("w13", [8, 1024, 4096])
    w2 = P.dram_in("w2", [8, 2048, 1024])
    P.dram_in("ident", [128, 128], BF16)
    P.dram_in("identf", [128, 128], F32)
    x_out = P.dram_out("x_out", [T, D])
    P.begin()
    emit_load_stream(P, x_in)
    emit_L3_ffn(P, rw, w13, w2)
    emit_store_stream(P, x_out)
    P.end()
    return P


def l3_ffn_in_maps(inputs, x_full):
    c = host_constants()
    rw_t = _f32(np.asarray(inputs["L3_ffn_router_w"]).T)
    w13 = _f32(inputs["L3_ffn_w13"])
    w2 = _f32(inputs["L3_ffn_w2"])
    maps = []
    for core in range(8):
        maps.append({
            "x_in": _f32(x_full[core]),
            "rw_t": rw_t, "w13": w13, "w2": w2,
            "ident": c["ident"], "identf": c["identf"],
        })
    return maps




def emit_kth_bisect(P, alg, K, kcol, ones_row, ones_col, kA, kB, iters=44):
    nc = P.nc
    sc = P.sb("kth_sc", [1, 16], F32)
    scb = P.sb("kth_scb", [128, 4], F32)
    lo, hi, mid = sc[0:1, 0:1], sc[0:1, 1:2], sc[0:1, 2:3]
    cnt, pp, qq, t1 = sc[0:1, 3:4], sc[0:1, 4:5], sc[0:1, 5:6], sc[0:1, 6:7]
    P.V("tensor_reduce", kA[:, 0:1], alg[:, :], AX.X, ALU.max)
    P.V("tensor_reduce", kA[:, 1:2], alg[:, :], AX.X, ALU.min)
    P.pe_transpose_f32([(kA[:, 0:1], kB[0:1, 0:128]),
                        (kA[:, 1:2], kB[0:1, 128:256])], rows=128)
    P.V("tensor_reduce", hi, kB[0:1, 0:128], AX.X, ALU.max)
    P.V("tensor_reduce", lo, kB[0:1, 128:256], AX.X, ALU.min)
    P.V("tensor_scalar", hi, hi, 1e-3, None, ALU.add)
    P.V("tensor_scalar", lo, lo, -1e-3, None, ALU.add)
    for i in range(iters):
        P.V("tensor_tensor", mid, lo, hi, ALU.add)
        P.V("tensor_scalar", mid, mid, 0.5, None, ALU.mult)
        pb = (P.PF, P.PTF)[i % 2][:, 0:1]
        P.mm(pb, ones_row, mid, start=True, stop=True)
        P.V("tensor_copy", scb[:, 0:1], pb)
        P.V("tensor_scalar", kA[:, :], alg[:, :], scb[:, 0:1], 0.0, ALU.is_lt,
            ALU.add, accum_out=kB[:, 0:1])
        ps = (P.PTF, P.PF)[i % 2][0:1, 0:1]
        P.mm(ps, ones_col, kB[:, 0:1], start=True, stop=True)
        P.V("tensor_copy", cnt, ps)
        P.V("tensor_scalar", pp, cnt, float(K), None, ALU.is_ge)
        P.V("tensor_scalar", qq, pp, -1.0, 1.0, ALU.mult, ALU.add)
        P.V("tensor_tensor", t1, mid, hi, ALU.subtract)
        P.V("tensor_tensor", t1, t1, pp, ALU.mult)
        P.V("tensor_tensor", hi, hi, t1, ALU.add)
        P.V("tensor_tensor", t1, mid, lo, ALU.subtract)
        P.V("tensor_tensor", t1, t1, qq, ALU.mult)
        P.V("tensor_tensor", lo, lo, t1, ALU.add)
    pb = P.PF[:, 0:1]
    P.mm(pb, ones_row, hi, start=True, stop=True)
    P.V("tensor_copy", scb[:, 1:2], pb)
    P.V("tensor_scalar", kA[:, :], alg[:, :], scb[:, 1:2], None, ALU.is_lt)
    P.V("tensor_tensor", kB[:, :], kA[:, :], alg[:, :], ALU.mult)
    P.V("tensor_scalar", kA[:, :], kA[:, :], 1e30, -1e30, ALU.mult, ALU.add)
    P.V("tensor_tensor", kB[:, :], kB[:, :], kA[:, :], ALU.add)
    P.V("tensor_reduce", kA[:, 0:1], kB[:, :], AX.X, ALU.max)
    P.pe_transpose_f32([(kA[:, 0:1], kB[0:1, 0:128])], rows=128)
    P.V("tensor_reduce", mid, kB[0:1, 0:128], AX.X, ALU.max)
    pb = P.PTF[:, 0:1]
    P.mm(pb, ones_row, mid, start=True, stop=True)
    P.V("tensor_copy", kcol, pb)


def emit_L1_ffn(P, all_lg_dram, own_lg_dram, w1t_dram, b1f_dram, w2t_dram,
                b2_dram):
    nc = P.nc
    E, I = 8, 4096
    alg = P.WSF[:, 0:1024]
    kA = P.WSF[:, 1024:2048]
    kB = P.WSF[:, 2048:3072]
    kcol = P.sb("l1f_kcol", [128, 1], F32)
    lgown = P.sb("l1f_lgo", [128, NT * E], F32)
    gates = P.sb("l1f_gates", [128, NT * E], F32)
    gatesT = P.WSF[0:8, 0:T]
    b1f = P.sb("l1f_b1f", [128, E * 32], F32)
    b2sb = P.WSF[0:8, 2048:3072]
    ones_row = P.sb("l1f_or", [1, 128], F32)[:, :]
    ones_col = P.sb("l1f_oc", [128, 1], F32)[:, :]
    kcol = kcol[:, :]
    P.V("memset", ones_row, 1.0)
    P.V("memset", ones_col, 1.0)
    P.dma(nc.sync, alg[:, :].rearrange("p (a e) -> p a e", a=128),
          all_lg_dram.rearrange("(a p) e -> p a e", p=128))
    P.dma(nc.sync, lgown[:, :].rearrange("p (tt e) -> p tt e", tt=NT),
          own_lg_dram.rearrange("(tt p) e -> p tt e", p=128))
    P.dma(nc.sync, b1f[:, :], b1f_dram)
    emit_kth_bisect(P, alg, 24576, kcol, ones_row, ones_col, kA, kB)
    P.dma(nc.sync, b2sb, b2_dram)
    P.V("tensor_scalar", gates[:, :], lgown[:, :], kcol[:, 0:1], None, ALU.is_ge)
    P.V("tensor_tensor", gates[:, :], gates[:, :], lgown[:, :], ALU.mult)
    jobs = []
    for tt in range(NT):
        jobs.append((gates[:, tt * E:(tt + 1) * E],
                     gatesT[0:E, tt * 128:(tt + 1) * 128]))
    P.pe_transpose_f32(jobs, rows=128)
    emit_dense_moe(P, gates, w1t_dram, w2t_dram, E=8, I=4096, tag="l1f",
                   hidden_act=AF.Gelu_apprx_tanh, up_off=None, b1_sb=b1f)
    for tt in range(NT):
        for dh in range(2):
            py = (P.PF, P.PTF)[(tt * 2 + dh) % 2][:, :]
            P.mm(py, gatesT[0:E, tt * 128:(tt + 1) * 128],
                 b2sb[0:E, dh * 512:(dh + 1) * 512], start=True, stop=True)
            xdst = P.X[:, tt * D + dh * 512: tt * D + dh * 512 + 512]
            P.V("tensor_tensor", xdst, xdst, py, ALU.add)
    emit_boundary_ln(P)


def build_l1_logits_prog():
    P = Prog("l1lg")
    x_in = P.dram_in("x_in", [T, D])
    gw = P.dram_in("gw_t", [D, 8])
    P.dram_in("ident", [128, 128], BF16)
    P.dram_in("identf", [128, 128], F32)
    lg_out = P.dram_out("lg_out", [T, 8])
    P.begin()
    emit_load_stream(P, x_in)
    lgts = P.sb("l1lg_lg", [128, NT * 8], F32)
    emit_router_logits(P, gw, 8, lgts, "l1lg")
    tok = P.dma(P.nc.sync, lg_out.rearrange("(tt p) e -> p tt e", p=128),
                lgts[:, :].rearrange("p (tt e) -> p tt e", tt=NT))
    P.wait_tok(P.nc.sync, tok)
    P.end()
    return P


def build_l1_ffn_prog():
    P = Prog("l1ffn")
    x_in = P.dram_in("x_in", [T, D])
    all_lg = P.dram_in("all_lg", [8 * T, 8])
    own_lg = P.dram_in("own_lg", [T, 8])
    w1t = P.dram_in("w1t", [8, 1024, 4096])
    b1f = P.dram_in("b1f", [128, 8 * 32])
    w2t = P.dram_in("w2t", [8, 4096, 1024])
    b2 = P.dram_in("b2", [8, 1024])
    P.dram_in("ident", [128, 128], BF16)
    P.dram_in("identf", [128, 128], F32)
    x_out = P.dram_out("x_out", [T, D])
    P.begin()
    emit_load_stream(P, x_in)
    emit_L1_ffn(P, all_lg, own_lg, w1t, b1f, w2t, b2)
    emit_store_stream(P, x_out)
    P.end()
    return P


def l1_logits_in_maps(inputs, x_full):
    c = host_constants()
    gw_t = _f32(np.asarray(inputs["L1_ffn_gate_weight"]).T)
    return [{"x_in": _f32(x_full[core]), "gw_t": gw_t,
             "ident": c["ident"], "identf": c["identf"]} for core in range(8)]


def l1_ffn_in_maps(inputs, x_full, all_logits):
    c = host_constants()
    w1 = np.asarray(inputs["L1_ffn_w1"])
    w2 = np.asarray(inputs["L1_ffn_w2"])
    b1 = np.asarray(inputs["L1_ffn_b1"])
    w1t = _f32(w1.transpose(0, 2, 1))
    w2t = _f32(w2.transpose(0, 2, 1))
    b1f = _f32(b1.reshape(8, 32, 128).transpose(2, 0, 1).reshape(128, 256))
    b2 = _f32(inputs["L1_ffn_b2"])
    maps = []
    for core in range(8):
        maps.append({
            "x_in": _f32(x_full[core]),
            "all_lg": _f32(all_logits.reshape(8 * T, 8)),
            "own_lg": _f32(all_logits[core]),
            "w1t": w1t, "b1f": b1f, "w2t": w2t, "b2": b2,
            "ident": c["ident"], "identf": c["identf"],
        })
    return maps




def emit_l2_expert(P, gate_src, up_src, down_src, xt, selrow_bf, sg, wsbst,
                   ones_bf):
    nc = P.nc
    wq = P.WSB[:, 0:4 * NC * 256]
    w2s = P.WSC[:, 0:8 * 1024]
    HT = P.WSB[:, 4 * NC * 256: 4 * NC * 256 + 4 * T]
    for sec in range(4):
        P.dma(nc.gpsimd, w2s[:, 0:4 * 1024].rearrange("p (i n) -> p i n", i=4),
              down_src[sec * 512:(sec + 1) * 512]
              .rearrange("(i p) n -> p i n", p=128))
        wqv = wq
        g0 = 512 * sec
        for r, (srcd, base) in enumerate(
                [(gate_src, g0), (gate_src, g0 + 256),
                 (up_src, g0), (up_src, g0 + 256)]):
            P.dma(nc.gpsimd,
                  wqv[:, r * NC * 256:(r + 1) * NC * 256]
                  .rearrange("p (c n) -> p c n", c=NC),
                  srcd[:, base:base + 256]
                  .rearrange("(c p) n -> p c n", p=128))
        for ts in range(4):
            if selrow_bf is not None:
                pb = P.PF[:, :]
                P.mm(pb, ones_bf, selrow_bf[0:1, ts * 512:(ts + 1) * 512],
                     start=True, stop=True)
                P.V("tensor_copy", wsbst[:, :], pb)
            for ic in range(4):
                reg = ic // 2
                off = (ic % 2) * 128
                it = ic % 2
                pg = P.PP[2 * it][:, :]
                pu = P.PP[2 * it + 1][:, :]
                for c in range(NC):
                    gw = wqv[:, reg * NC * 256 + c * 256 + off:
                             reg * NC * 256 + c * 256 + off + 128]
                    uw = wqv[:, (2 + reg) * NC * 256 + c * 256 + off:
                             (2 + reg) * NC * 256 + c * 256 + off + 128]
                    hiT = xt(c, ts * 512, 512)
                    P.mm(pg, gw, hiT, start=(c == 0), stop=(c == NC - 1))
                    P.mm(pu, uw, hiT, start=(c == 0), stop=(c == NC - 1))
                hdst = HT[:, ic * T + ts * 512: ic * T + ts * 512 + 512]
                if selrow_bf is not None:
                    P.V("tensor_tensor", sg[:, :], pg, wsbst[:, :], ALU.mult)
                    P.S("activation", sg[:, :], sg[:, :], AF.Silu)
                    P.V("tensor_tensor", hdst, pu, wsbst[:, :], ALU.mult)
                    P.V("tensor_tensor", hdst, hdst, sg[:, :], ALU.mult)
                else:
                    P.S("activation", sg[:, :], pg, AF.Silu)
                    P.V("tensor_tensor", hdst, sg[:, :], pu, ALU.mult)
        for tt in range(NT):
            for dh in range(2):
                it = (tt * 2 + dh) % 2
                py = P.PP[2 * it][:, :]
                for hc in range(4):
                    lh = HT[:, hc * T + tt * 128: hc * T + (tt + 1) * 128]
                    rw2 = w2s[:, hc * 1024 + dh * 512: hc * 1024 + dh * 512 + 512]
                    P.mm(py, lh, rw2, start=(hc == 0), stop=(hc == 3))
                xdst = P.X[:, tt * D + dh * 512: tt * D + dh * 512 + 512]
                P.V("tensor_tensor", xdst, xdst, py, ALU.add)


def emit_L2_ffn(P, rw_dram, wrow_dram, brow_dram, wfeat_dram, bfeat_dram,
                gate_dram, up_dram, down_dram, shg_dram, shu_dram, shd_dram):
    nc = P.nc
    E = 8
    lgts = P.sb("l2f_lgts", [128, NT * E], F32)
    sel = P.sb("l2f_sel", [128, NT * E], F32)
    scr = P.sb("l2f_scr", [128, NT * 4], F32)
    csb = P.sb("l2f_csb", [128, E], F32)
    bvb = P.sb("l2f_bvb", [128, E], F32)
    wfeat = P.sb("l2f_wf", [128, NC], F32)
    bfeat = P.sb("l2f_bf", [128, NC], F32)
    sg = P.sb("l2f_sg", [128, 512], BF16)
    wsbst = P.sb("l2f_wsb", [128, 512], BF16)
    sm1 = P.sb("l2f_sm1", [1, 8], F32)
    cs_row = P.sb("l2f_cs", [1, 8], F32)
    rwf2 = P.sb("l2f_rwf2", [128, NC * E], F32)
    onescol = P.sb("l2f_oc", [128, 1], F32)[:, :]
    ones_row = P.sb("l2f_or", [1, 128], F32)[:, :]
    ones_bf = P.sb("l2f_orb", [1, 128], BF16)[:, :]
    P.V("memset", onescol, 1.0)
    P.V("memset", ones_row, 1.0)
    P.V("memset", ones_bf, 1.0)
    P.dma(nc.sync, wfeat[:, :], wfeat_dram)
    P.dma(nc.sync, bfeat[:, :], bfeat_dram)
    wrow_sb = P.WSF[0:1, 0:1024]
    brow_sb = P.WSF[0:1, 1024:2048]
    P.dma(nc.sync, wrow_sb, wrow_dram)
    P.dma(nc.sync, brow_sb, brow_dram)
    emit_ln_stats(P)

    def hook(rwf):
        pb = P.PF[0:1, 0:E]
        for c in range(NC):
            P.mm(pb, bfeat[:, c:c + 1], rwf[:, c * E:(c + 1) * E],
                 start=(c == 0), stop=(c == NC - 1))
        P.V("tensor_copy", sm1[0:1, 0:E], pb)

    emit_router_logits(P, rw_dram, E, lgts, "l2f", pre_scale=wfeat,
                       rwf_hook=hook)
    P.dma(nc.sync, rwf2[:, :].rearrange("p (c e) -> p c e", c=NC),
          rw_dram.rearrange("(c p) e -> p c e", p=128))
    for c in range(NC):
        P.V("tensor_scalar", rwf2[:, c * E:(c + 1) * E],
            rwf2[:, c * E:(c + 1) * E], wfeat[:, c:c + 1], None, ALU.mult)
    pcs = P.PTF[0:1, 0:E]
    for c in range(NC):
        P.mm(pcs, onescol, rwf2[:, c * E:(c + 1) * E],
             start=(c == 0), stop=(c == NC - 1))
    P.V("tensor_copy", cs_row[0:1, 0:E], pcs)
    pb = P.PF[:, 0:E]
    P.mm(pb, ones_row, cs_row[0:1, 0:E], start=True, stop=True)
    P.V("tensor_copy", csb[:, :], pb)
    pb2 = P.PTF[:, 0:E]
    P.mm(pb2, ones_row, sm1[0:1, 0:E], start=True, stop=True)
    P.V("tensor_copy", bvb[:, :], pb2)

    st = P.stats
    for tt in range(NT):
        o = tt * 8
        lg = lgts[:, tt * E:(tt + 1) * E]
        P.V("tensor_scalar", lg, lg, st[:, o + 4:o + 5], None, ALU.mult)
        P.V("scalar_tensor_tensor", lg, csb[:, :], st[:, o + 5:o + 6], lg,
            ALU.mult, ALU.add)
        P.V("tensor_tensor", lg, lg, bvb[:, :], ALU.add)
        P.V("tensor_reduce", scr[:, tt * 4:tt * 4 + 1], lg, AX.X, ALU.max)
        P.S("activation", scr[:, tt * 4 + 1:tt * 4 + 2],
            scr[:, tt * 4:tt * 4 + 1], AF.Sigmoid)
        sl = sel[:, tt * E:(tt + 1) * E]
        P.V("tensor_scalar", sl, lg, scr[:, tt * 4:tt * 4 + 1], None,
            ALU.is_equal)
        P.V("tensor_scalar", sl, sl, scr[:, tt * 4 + 1:tt * 4 + 2], None,
            ALU.mult)

    wb = P.WSF[:, 0:1024]
    bb = P.WSF[:, 1024:2048]
    for half in range(2):
        pbq = P.PP[half][:, :]
        P.mm(pbq, ones_row, wrow_sb[0:1, half * 512:(half + 1) * 512],
             start=True, stop=True)
        pbb = P.PP[2 + half][:, :]
        P.mm(pbb, ones_row, brow_sb[0:1, half * 512:(half + 1) * 512],
             start=True, stop=True)
        P.V("tensor_copy", wb[:, half * 512:(half + 1) * 512], pbq)
        P.V("tensor_copy", bb[:, half * 512:(half + 1) * 512], pbb)

    xnT = P.WSA[:, 0:NC * T]
    for tt in range(NT):
        tb = P.tmpb(tt)
        o = tt * 8
        P.V("tensor_scalar", tb, P.x_tile(tt), st[:, o + 4:o + 5],
            st[:, o + 5:o + 6], ALU.mult, ALU.add)
        P.V("tensor_tensor", tb, tb, wb, ALU.mult)
        P.V("tensor_tensor", tb, tb, bb, ALU.add)
        jobs = []
        for c in range(NC):
            jobs.append((tb[:, c * 128:(c + 1) * 128],
                         xnT[:, c * T + tt * 128: c * T + (tt + 1) * 128]))
        P.pe_transpose(jobs)

    def xt(c, t0, tn):
        return xnT[:, c * T + t0: c * T + t0 + tn]

    for tt in range(NT):
        P.V("tensor_scalar", P.x_tile(tt), P.x_tile(tt), 2.0, None, ALU.mult)

    selrow_bf = P.TMPB[0:1, 0:T]
    for e in range(E):
        jobs = []
        for tt in range(NT):
            jobs.append((sel[:, tt * E + e: tt * E + e + 1],
                         selrow_bf[0:1, tt * 128:(tt + 1) * 128]))
        P.pe_transpose_f32(jobs, rows=128)
        emit_l2_expert(P, gate_dram[e], up_dram[e], down_dram[e], xt,
                       selrow_bf, sg, wsbst, ones_bf)
    emit_l2_expert(P, shg_dram, shu_dram, shd_dram, xt, None, sg, wsbst,
                   ones_bf)
    emit_boundary_ln(P)


def build_l2_ffn_prog():
    P = Prog("l2ffn")
    x_in = P.dram_in("x_in", [T, D])
    rw = P.dram_in("rw", [D, 8])
    wrow = P.dram_in("wrow", [1, D])
    brow = P.dram_in("brow", [1, D])
    wfeat = P.dram_in("wfeat", [128, NC])
    bfeat = P.dram_in("bfeat", [128, NC])
    gate = P.dram_in("gate_w", [8, 1024, 2048])
    up = P.dram_in("up_w", [8, 1024, 2048])
    down = P.dram_in("down_w", [8, 2048, 1024])
    shg = P.dram_in("sh_gate", [1024, 2048])
    shu = P.dram_in("sh_up", [1024, 2048])
    shd = P.dram_in("sh_down", [2048, 1024])
    P.dram_in("ident", [128, 128], BF16)
    P.dram_in("identf", [128, 128], F32)
    x_out = P.dram_out("x_out", [T, D])
    P.begin()
    emit_load_stream(P, x_in)
    emit_L2_ffn(P, rw, wrow, brow, wfeat, bfeat, gate, up, down, shg, shu, shd)
    emit_store_stream(P, x_out)
    P.end()
    return P


def l2_ffn_in_maps(inputs, x_full):
    c = host_constants()
    lw = _f32(inputs["L2_ffn_ln_w"])
    lb = _f32(inputs["L2_ffn_ln_b"])
    maps = []
    for core in range(8):
        maps.append({
            "x_in": _f32(x_full[core]),
            "rw": _f32(inputs["L2_ffn_router_w"]),
            "wrow": lw.reshape(1, D), "brow": lb.reshape(1, D),
            "wfeat": _f32(lw.reshape(NC, 128).T),
            "bfeat": _f32(lb.reshape(NC, 128).T),
            "gate_w": _f32(inputs["L2_ffn_gate_w"]),
            "up_w": _f32(inputs["L2_ffn_up_w"]),
            "down_w": _f32(inputs["L2_ffn_down_w"]),
            "sh_gate": _f32(inputs["L2_ffn_sh_gate"]),
            "sh_up": _f32(inputs["L2_ffn_sh_up"]),
            "sh_down": _f32(inputs["L2_ffn_sh_down"]),
            "ident": c["ident"], "identf": c["identf"],
        })
    return maps




def emit_L3_attn(P, wup_dram, bup_dram, convw_dram, convb_dram, wdel_dram,
                 bdel_dram, wB_dram, bB_dram, wC_dram, bC_dram, alog_dram,
                 dskip_dram, wdown_dram, bdownrow_dram):
    nc = P.nc
    N = 64
    XSf = P.WSA[:, 0:4096].bitcast(F32)
    DLT = P.WSA[:, 4096:8192].bitcast(F32)
    LPT = P.WSA[:, 8192:10244]
    XSb = P.WSA[:, 10244:12292]
    WT = P.WSA[:, 12292:14340]
    GT = P.WSA[:, 12292:14340]
    SRT = P.WSC[:, 0:2048]
    BT = P.WSC[:, 2048:4096]
    CT = P.WSC[:, 4096:6144]
    YSB = P.WSC[:, 6144:8192]
    cvs = P.WSF[:, 0:2048]
    wdsb = P.sb("l3a_wd", [128, 1024], BF16)
    wupsb = P.sb("l3a_wup", [128, 2048], BF16)
    wdelsb = P.sb("l3a_wdel", [128, 128], BF16)
    wBsb = P.sb("l3a_wB", [128, 64], BF16)
    wCsb = P.sb("l3a_wC", [128, 64], BF16)
    smalls = P.sb("l3a_small", [128, 16], F32)
    aL = P.sb("l3a_alog", [128, N], F32)
    acols = P.sb("l3a_acols", [128, N], F32)
    bBc = P.sb("l3a_bB", [64, 2], F32)
    selJ = P.sb("l3a_selJ", [64, 128], F32)
    selE = P.sb("l3a_selE", [64, 128], BF16)
    bdrow = P.sb("l3a_bdrow", [1, 1024], F32)
    bdb = P.WSF[:, 2048:3072]
    onr = P.sb("l3a_or", [1, 128], F32)[:, :]
    P.V("memset", onr, 1.0)

    P.dma(nc.gpsimd, wupsb[:, :].rearrange("p (c u) -> p c u", c=NC),
          wup_dram.rearrange("(c p) u -> p c u", p=128))
    P.dma(nc.sync, smalls[:, 0:2], bup_dram)
    P.dma(nc.sync, smalls[:, 2:5], convw_dram)
    P.dma(nc.sync, smalls[:, 5:6], convb_dram)
    P.dma(nc.gpsimd, wdelsb[:, :], wdel_dram)
    P.dma(nc.sync, smalls[:, 6:7], bdel_dram)
    P.dma(nc.gpsimd, wBsb[:, :], wB_dram)
    P.dma(nc.sync, bBc[:, 0:1], bB_dram)
    P.dma(nc.gpsimd, wCsb[:, :], wC_dram)
    P.dma(nc.sync, bBc[:, 1:2], bC_dram)
    P.dma(nc.sync, aL[:, :], alog_dram)
    P.dma(nc.sync, smalls[:, 7:8], dskip_dram)
    P.dma(nc.gpsimd, wdsb[:, :], wdown_dram)
    P.dma(nc.sync, bdrow[:, :], bdownrow_dram)
    P.S("activation", acols[:, :], aL[:, :], AF.Exp)
    P.V("tensor_scalar", acols[:, :], acols[:, :], -1.0, None, ALU.mult)
    for half in range(2):
        pb = P.PP[half][:, :]
        P.mm(pb, onr, bdrow[0:1, half * 512:(half + 1) * 512], start=True,
             stop=True)
        P.V("tensor_copy", bdb.bitcast(F32)[:, half * 512:(half + 1) * 512]
            if False else bdb[:, half * 512:(half + 1) * 512], pb)

    P.V("memset", LPT[:, 0:4], 0.0)
    for half, dst in ((0, None), (1, None)):
        for ts in range(4):
            pu = P.PP[2 + (ts % 2)][:, :]
            for c in range(NC):
                P.mm(pu, wupsb[:, c * 256 + half * 128: c * 256 + half * 128 + 128],
                     P.xt_chunk(c, ts * 512, 512),
                     start=(c == 0), stop=(c == NC - 1))
            if half == 0:
                P.V("tensor_scalar", LPT[:, 4 + ts * 512: 4 + ts * 512 + 512],
                    pu, smalls[:, 0:1], None, ALU.add)
            else:
                P.S("activation", SRT[:, ts * 512:(ts + 1) * 512], pu, AF.Silu,
                    bias=smalls[:, 1:2])

    P.V("tensor_scalar", cvs[:, :], LPT[:, 2:2 + 2048], smalls[:, 2:3], None,
        ALU.mult)
    P.V("scalar_tensor_tensor", cvs[:, :], LPT[:, 3:3 + 2048], smalls[:, 3:4],
        cvs[:, :], ALU.mult, ALU.add)
    P.V("scalar_tensor_tensor", cvs[:, :], LPT[:, 4:4 + 2048], smalls[:, 4:5],
        cvs[:, :], ALU.mult, ALU.add)
    P.S("activation", XSf[:, :], cvs[:, :], AF.Silu, bias=smalls[:, 5:6])
    P.V("tensor_copy", XSb[:, :], XSf[:, :])

    for ts in range(4):
        pz = P.PP[ts % 2][:, :]
        P.mm(pz, wdelsb[:, :], XSb[:, ts * 512:(ts + 1) * 512], start=True,
             stop=True)
        z = cvs[:, ts * 512:(ts + 1) * 512]
        P.V("tensor_scalar", z, pz, smalls[:, 6:7], None, ALU.add)
        za = P.PP[2 + ts % 2]
        dz = DLT[:, ts * 512:(ts + 1) * 512]
        P.S("activation", dz, z, AF.Abs)
        P.S("activation", dz, dz, AF.Exp, scale=-1.0)
        P.S("activation", dz, dz, AF.Ln, bias=1.0)
        P.V("tensor_scalar", z, z, 0.0, None, ALU.max)
        P.V("tensor_tensor", dz, dz, z, ALU.add)
    for ts in range(4):
        pb_ = P.PP[ts % 2][0:64, :]
        P.mm(pb_, wBsb[:, :], XSb[:, ts * 512:(ts + 1) * 512], start=True,
             stop=True)
        P.V("tensor_scalar", BT[0:64, ts * 512:(ts + 1) * 512], pb_,
            bBc[:, 0:1], None, ALU.add)
        pc_ = P.PP[2 + ts % 2][0:64, :]
        P.mm(pc_, wCsb[:, :], XSb[:, ts * 512:(ts + 1) * 512], start=True,
             stop=True)
        P.V("tensor_scalar", CT[0:64, ts * 512:(ts + 1) * 512], pc_,
            bBc[:, 1:2], None, ALU.add)
    P.V("tensor_tensor", WT[:, :], DLT[:, :], XSf[:, :], ALU.mult)

    for n in range(N):
        o = (n % 2) * 8192
        dAn = P.WSB[:, o:o + 2048]
        dBn = P.WSB[:, o + 2048:o + 4096]
        hn = P.WSB[:, o + 4096:o + 6144]
        un = P.WSB[:, o + 6144:o + 8192]
        P.G("iota", selJ[:, :], [[0, 128]], base=-n, channel_multiplier=1,
            allow_small_or_imprecise_dtypes=True)
        P.V("tensor_scalar", selE[:, :], selJ[:, :], 0.0, None, ALU.is_equal)
        P.S("activation", dAn, DLT[:, :], AF.Exp, scale=acols[:, n:n + 1])
        for ts in range(4):
            pbb = (P.PF, P.PTF)[ts % 2][:, :]
            P.mm(pbb, selE[:, :], BT[0:64, ts * 512:(ts + 1) * 512],
                 start=True, stop=True)
            P.V("tensor_tensor", dBn[:, ts * 512:(ts + 1) * 512],
                WT[:, ts * 512:(ts + 1) * 512], pbb, ALU.mult)
        P.V("tensor_tensor_scan", hn, dAn, dBn, 0.0, ALU.mult, ALU.add)
        for ts in range(4):
            pcc = (P.PTF, P.PF)[ts % 2][:, :]
            P.mm(pcc, selE[:, :], CT[0:64, ts * 512:(ts + 1) * 512],
                 start=True, stop=True)
            P.V("tensor_tensor", un[:, ts * 512:(ts + 1) * 512],
                hn[:, ts * 512:(ts + 1) * 512], pcc, ALU.mult)
        for ts in range(4):
            P.mm(P.PP[ts][:, :], P.ident[:, :], un[:, ts * 512:(ts + 1) * 512],
                 start=(n == 0), stop=(n == N - 1))
    for ts in range(4):
        P.V("scalar_tensor_tensor", YSB[:, ts * 512:(ts + 1) * 512],
            XSf[:, ts * 512:(ts + 1) * 512], smalls[:, 7:8], P.PP[ts][:, :],
            ALU.mult, ALU.add)
    P.V("tensor_tensor", GT[:, :], YSB[:, :], SRT[:, :], ALU.mult)
    for tt in range(NT):
        for dh in range(2):
            po = P.PP[(tt * 2 + dh) % 4][:, :]
            P.mm(po, GT[:, tt * 128:(tt + 1) * 128],
                 wdsb[:, dh * 512:(dh + 1) * 512], start=True, stop=True)
            xdst = P.X[:, tt * D + dh * 512: tt * D + dh * 512 + 512]
            P.V("tensor_tensor", xdst, xdst, po, ALU.add)
            P.V("tensor_tensor", xdst, xdst,
                bdb[:, dh * 512:(dh + 1) * 512], ALU.add)
    emit_boundary_ln(P)


def build_l3_attn_prog():
    P = Prog("l3attn")
    x_in = P.dram_in("x_in", [T, D])
    wup = P.dram_in("wup", [1024, 256])
    bup = P.dram_in("bup", [128, 2])
    convw = P.dram_in("convw", [128, 3])
    convb = P.dram_in("convb", [128, 1])
    wdel = P.dram_in("wdel", [128, 128])
    bdel = P.dram_in("bdel", [128, 1])
    wB = P.dram_in("wB", [128, 64])
    bB = P.dram_in("bB", [64, 1])
    wC = P.dram_in("wC", [128, 64])
    bC = P.dram_in("bC", [64, 1])
    alog = P.dram_in("alog", [128, 64])
    dskip = P.dram_in("dskip", [128, 1])
    wdown = P.dram_in("wdown", [128, 1024])
    bdown = P.dram_in("bdown", [1, 1024])
    P.dram_in("ident", [128, 128], BF16)
    P.dram_in("identf", [128, 128], F32)
    x_out = P.dram_out("x_out", [T, D])
    P.begin()
    emit_load_stream(P, x_in)
    emit_L3_attn(P, wup, bup, convw, convb, wdel, bdel, wB, bB, wC, bC, alog,
                 dskip, wdown, bdown)
    emit_store_stream(P, x_out)
    P.end()
    return P


def l3_attn_in_maps(inputs, x_full):
    c = host_constants()
    g = lambda k: _f32(inputs[k])
    base = {
        "wup": g("L3_attn_W_up"),
        "bup": _f32(np.stack([np.asarray(inputs["L3_attn_b_up"])[0:128],
                              np.asarray(inputs["L3_attn_b_up"])[128:256]], 1)),
        "convw": g("L3_attn_conv_w"),
        "convb": _f32(np.asarray(inputs["L3_attn_conv_b"]).reshape(128, 1)),
        "wdel": g("L3_attn_W_delta"),
        "bdel": _f32(np.asarray(inputs["L3_attn_b_delta"]).reshape(128, 1)),
        "wB": g("L3_attn_W_B"),
        "bB": _f32(np.asarray(inputs["L3_attn_b_B"]).reshape(64, 1)),
        "wC": g("L3_attn_W_C"),
        "bC": _f32(np.asarray(inputs["L3_attn_b_C"]).reshape(64, 1)),
        "alog": g("L3_attn_A_log"),
        "dskip": _f32(np.asarray(inputs["L3_attn_D_skip"]).reshape(128, 1)),
        "wdown": g("L3_attn_W_down"),
        "bdown": _f32(np.asarray(inputs["L3_attn_b_down"]).reshape(1, 1024)),
        "ident": c["ident"], "identf": c["identf"],
    }
    maps = []
    for core in range(8):
        m = dict(base)
        m["x_in"] = _f32(x_full[core])
        maps.append(m)
    return maps




def _alibi_slopes16():
    import math
    start = 2.0 ** (-2.0 ** (-(math.log2(16) - 3)))
    return [start * start ** i for i in range(16)]


def emit_L0_attn(P, wqt, bqt, aqt, wkt, bkt, akt, wvt, bvt, avt, wpt,
                 prec_dram, msk_dram):
    nc = P.nc
    H, dh = 16, 64
    QT = P.WSA[:, 0:NC * T]
    KT = P.WSB[:, 0:NC * T]
    VH = P.WSC[:, 0:16 * 512]
    Ssb = P.WSF[:, 0:2048]
    prec = P.WSF[:, 2048:2560]
    mskw = P.sb("l0a_mskw", [128, 896], BF16)
    ws = P.sb("l0a_ws", [128, 4096], BF16)
    lora1 = P.TMPB[0:8, 0:T]
    lora1v = P.sb("l0a_lora1v", [8, T], BF16)
    amt = P.WSF[0:8, 2560:3072].bitcast(BF16)
    cols = P.stats[:, NT * 8: NT * 8 + 16]
    pts = P.stats[:, 0:128].bitcast(BF16)
    bmt2 = P.sb("l0a_bmt2", [128, 3 * 64], BF16)
    P.dma(nc.sync, prec, prec_dram)
    P.dma(nc.sync, mskw[:, :], msk_dram)
    for i, bd in enumerate([bqt, bkt, bvt]):
        P.dma(nc.gpsimd, bmt2[:, i * 64:(i + 1) * 64]
              .rearrange("p (c e) -> p c e", c=NC),
              bd.rearrange("(c p) e -> p c e", p=128))
    slopes = _alibi_slopes16()

    def lora_mid(i, dstb):
        for ts in range(4):
            pl = (P.PF, P.PTF)[ts % 2][0:8, :]
            for c in range(NC):
                P.mm(pl, bmt2[:, i * 64 + c * 8: i * 64 + (c + 1) * 8],
                     P.xt_chunk(c, ts * 512, 512), start=(c == 0),
                     stop=(c == NC - 1))
            P.V("tensor_copy", dstb[0:8, ts * 512: ts * 512 + 512], pl)

    def load_amt(at_dram):
        P.dma(nc.gpsimd, amt[0:8, :], at_dram)
        P.V("tensor_scalar", amt[0:8, :], amt[0:8, :], 2.0, None, ALU.mult)

    def proj_feat(dst, wt_dram, at_dram, i):
        load_amt(at_dram)
        lora_mid(i, lora1)
        for ph in range(2):
            P.dma(nc.gpsimd, ws[:, :].rearrange("p (c e) -> p c e", c=4),
                  wt_dram[ph * 512:(ph + 1) * 512]
                  .rearrange("(c p) e -> p c e", p=128))
            for ec in range(NC):
                for ts in range(4):
                    pq = P.PP[(ec * 4 + ts) % 2][:, :]
                    for c in range(4):
                        P.mm(pq, ws[:, c * 1024 + ec * 128:
                                    c * 1024 + (ec + 1) * 128],
                             P.xt_chunk(ph * 4 + c, ts * 512, 512),
                             start=(c == 0), stop=(ph == 0 and c == 3))
                    ddst = dst[:, ec * T + ts * 512: ec * T + ts * 512 + 512]
                    if ph == 0:
                        P.V("tensor_copy", ddst, pq)
                    else:
                        P.mm(pq, amt[0:8, ec * 128:(ec + 1) * 128],
                             lora1[0:8, ts * 512: ts * 512 + 512],
                             start=False, stop=True)
                        P.V("tensor_tensor", ddst, ddst, pq, ALU.add)

    proj_feat(QT, wqt, aqt, 0)
    proj_feat(KT, wkt, akt, 1)
    load_amt(avt)
    lora_mid(2, lora1v)

    for hg in range(2):
        for ph in range(2):
            P.dma(nc.gpsimd, ws[:, 0:2048].rearrange("p (c e) -> p c e", c=4),
                  wvt[ph * 512:(ph + 1) * 512, hg * 512:(hg + 1) * 512]
                  .rearrange("(c p) e -> p c e", p=128))
            for tt in range(NT):
                pv = P.PP[2 + tt % 2][:, :]
                for c in range(4):
                    P.mm(pv, P.xt_chunk(ph * 4 + c, tt * 128, 128),
                         ws[:, c * 512:(c + 1) * 512], start=(c == 0),
                         stop=(ph == 0 and c == 3))
                vdst = VH[:, tt * 512: tt * 512 + 512]
                if ph == 0:
                    P.V("tensor_copy", vdst, pv)
                else:
                    P.mm(pv, lora1v[0:8, tt * 128:(tt + 1) * 128],
                         amt[0:8, hg * 512:(hg + 1) * 512],
                         start=False, stop=True)
                    P.V("tensor_tensor", vdst, vdst, pv, ALU.add)
        for hh in range(8):
            h = hg * 8 + hh
            ec = h // 2
            eo = (h % 2) * 64
            slope = slopes[h]
            for tq in range(NT):
                nstr = tq // 4 + 1
                diag = tq % 4
                for st_ in range(nstr):
                    ps = P.PP[st_ % 2][:, :]
                    qsl = QT[eo:eo + 64,
                             ec * T + tq * 128: ec * T + (tq + 1) * 128]
                    ksl = KT[eo:eo + 64,
                             ec * T + st_ * 512: ec * T + st_ * 512 + 512]
                    P.mm(ps, qsl, ksl, start=True, stop=True)
                    sdst = Ssb[:, st_ * 512:(st_ + 1) * 512]
                    cst = slope * (tq * 128 - st_ * 512)
                    P.V("tensor_scalar", sdst, ps, 0.125, None, ALU.mult)
                    P.V("scalar_tensor_tensor", sdst, prec, slope, sdst,
                        ALU.mult, ALU.add)
                    if st_ == nstr - 1:
                        P.V("tensor_tensor", sdst, sdst,
                            mskw[:, 384 - 128 * diag: 896 - 128 * diag],
                            ALU.add)
                    P.V("tensor_reduce", cols[:, st_:st_ + 1], sdst, AX.X,
                        ALU.max)
                    P.V("tensor_scalar", cols[:, st_:st_ + 1],
                        cols[:, st_:st_ + 1], cst, None, ALU.add)
                P.V("tensor_reduce", cols[:, 8:9], cols[:, 0:nstr], AX.X,
                    ALU.max)
                for st_ in range(nstr):
                    cst = slope * (tq * 128 - st_ * 512)
                    P.V("tensor_scalar", cols[:, 9:10], cols[:, 8:9], -1.0,
                        cst, ALU.mult, ALU.add)
                    P.S("activation", P.TMPB[:, st_ * 512:(st_ + 1) * 512],
                        Ssb[:, st_ * 512:(st_ + 1) * 512], AF.Exp,
                        bias=cols[:, 9:10],
                        accum_out=cols[:, 10 + st_:11 + st_])
                P.V("tensor_reduce", cols[:, 14:15], cols[:, 10:10 + nstr],
                    AX.X, ALU.add)
                P.V("reciprocal", cols[:, 15:16], cols[:, 14:15])
                pav = P.PP[2 + (tq % 2)][0:64, 0:128]
                for st_ in range(nstr):
                    P.V("tensor_scalar", P.TMPB[:, st_ * 512:(st_ + 1) * 512],
                        P.TMPB[:, st_ * 512:(st_ + 1) * 512], cols[:, 15:16],
                        None, ALU.mult)
                nkt = tq + 1
                for kt in range(nkt):
                    ptile = (P.PTA, P.PTB)[kt % 2][:, 0:128]
                    P.transpose(ptile, P.TMPB[:, kt * 128:(kt + 1) * 128],
                                P.ident[:, :])
                    stg = pts[:, (kt % 2) * 128:(kt % 2) * 128 + 128]
                    P.V("tensor_copy", stg, ptile)
                    P.mm(pav, VH[:, kt * 512 + hh * 64: kt * 512 + hh * 64 + 64],
                         stg, start=(kt == 0), stop=(kt == nkt - 1))
                P.V("tensor_copy",
                    QT[eo:eo + 64, ec * T + tq * 128: ec * T + (tq + 1) * 128],
                    pav)
    for ph in range(2):
        P.dma(nc.gpsimd, ws[:, :].rearrange("p (c e) -> p c e", c=4),
              wpt[ph * 512:(ph + 1) * 512].rearrange("(c p) e -> p c e", p=128))
        for tt in range(NT):
            for dhh in range(2):
                po = P.PP[(tt * 2 + dhh) % 2][:, :]
                for c in range(4):
                    P.mm(po, QT[:, (ph * 4 + c) * T + tt * 128:
                                (ph * 4 + c) * T + (tt + 1) * 128],
                         ws[:, c * 1024 + dhh * 512: c * 1024 + dhh * 512 + 512],
                         start=(c == 0), stop=(c == 3))
                xdst = P.X[:, tt * D + dhh * 512: tt * D + dhh * 512 + 512]
                P.V("tensor_tensor", xdst, xdst, po, ALU.add)
    emit_boundary_ln(P)


def build_l0_attn_prog():
    P = Prog("l0attn")
    x_in = P.dram_in("x_in", [T, D])
    names = ["wqt", "bqt", "aqt", "wkt", "bkt", "akt", "wvt", "bvt", "avt", "wpt"]
    shapes = [[1024, 1024], [1024, 8], [8, 1024]] * 3 + [[1024, 1024]]
    aps = {}
    for nm, sh in zip(names, shapes):
        aps[nm] = P.dram_in(nm, sh)
    prec = P.dram_in("prec", [128, 512])
    msk = P.dram_in("msk", [128, 896], BF16)
    P.dram_in("ident", [128, 128], BF16)
    P.dram_in("identf", [128, 128], F32)
    x_out = P.dram_out("x_out", [T, D])
    P.begin()
    emit_load_stream(P, x_in)
    emit_L0_attn(P, aps["wqt"], aps["bqt"], aps["aqt"], aps["wkt"], aps["bkt"],
                 aps["akt"], aps["wvt"], aps["bvt"], aps["avt"], aps["wpt"],
                 prec, msk)
    emit_store_stream(P, x_out)
    P.end()
    return P


def l0_attn_consts():
    p = np.arange(128)[:, None]
    f = np.arange(512)[None, :]
    prec = (p - f).astype(np.float32)
    g = np.arange(896)[None, :]
    mskw = np.where(p + 384 - g >= 0, 0.0, -1e30)
    return prec, np.ascontiguousarray(mskw, dtype=BF)


def l0_attn_in_maps(inputs, x_full):
    c = host_constants()
    prec, msk = l0_attn_consts()
    base = {
        "wqt": _f32(np.asarray(inputs["L0_attn_Wq"]).T),
        "bqt": _f32(np.asarray(inputs["L0_attn_Bq"]).T),
        "aqt": _f32(np.asarray(inputs["L0_attn_Aq"]).T),
        "wkt": _f32(np.asarray(inputs["L0_attn_Wk"]).T),
        "bkt": _f32(np.asarray(inputs["L0_attn_Bk"]).T),
        "akt": _f32(np.asarray(inputs["L0_attn_Ak"]).T),
        "wvt": _f32(np.asarray(inputs["L0_attn_Wv"]).T),
        "bvt": _f32(np.asarray(inputs["L0_attn_Bv"]).T),
        "avt": _f32(np.asarray(inputs["L0_attn_Av"]).T),
        "wpt": _f32(np.asarray(inputs["L0_attn_Wp"]).T),
        "prec": prec, "msk": msk,
        "ident": c["ident"], "identf": c["identf"],
    }
    maps = []
    for core in range(8):
        m = dict(base)
        m["x_in"] = _f32(x_full[core])
        maps.append(m)
    return maps




def emit_L0_ffn(P, x_in_dram, wqkv_dram, qsrows_dram, rot_dram, wo_dram,
                gbf_dram, w1a_dram, w1g_dram, b1f_dram, w2_dram, b2rows_dram,
                m2_dram):
    nc = P.nc
    HD = 8
    NU = 22
    QK = P.WSA[:, 0:NT * 1024]
    VTOK = P.WSA[:, 0:NT * 512]
    OT = P.WSA[:, NT * 512:NT * 512 + 4 * T]
    XNT = P.WSA[:, 0:NC * T]
    QKT = P.WSB[:, 0:8 * T]
    HMT = P.WSB[:, 0:4 * T]
    RBT = P.WSB[:, 4 * T:6 * T]
    Ssb = P.WSF[:, 0:1024]
    SqS = P.WSF[:, 1024:1536]
    h1S = P.WSF[:, 1536:2048]
    ws = P.WSC[:, 0:8192]
    rot = P.sb("l0f_rot", [128, 4 * 1024], BF16)
    m2 = P.sb("l0f_m2", [128, 1920], BF16)
    qsb = P.sb("l0f_qsb", [128, 4 * 64], F32)
    qsrows = P.WSF[0:1, 1536:2048]
    b2rows = P.WSF[0:1, 0:1024]
    gbf = P.sb("l0f_gbf", [128, 2 * 16], F32)
    b1f = P.sb("l0f_b1f", [128, 2 * 2 * NU], F32)
    dtile = P.sb("l0f_dt", [128, 2 * 64], BF16)[:, :]
    ocb = P.sb("l0f_ocb", [128, 1], BF16)[:, :]
    onr = P.sb("l0f_onr", [1, 128], F32)[:, :]
    cols = P.stats[:, NT * 8: NT * 8 + 16]
    pts = P.stats[:, 0:128].bitcast(BF16)
    gsb = P.TMPB[:, 1024:1536]
    P.V("memset", ocb, 1.0)
    P.V("memset", onr, 1.0)
    P.dma(nc.sync, rot[:, :], rot_dram)
    P.dma(nc.sync, m2[:, :], m2_dram)
    P.dma(nc.sync, gbf[:, :], gbf_dram)
    P.dma(nc.sync, b1f[:, :], b1f_dram)

    def xt_shift(c, t0, tn):
        if c < 4:
            return P.xt_chunk(c, t0, tn)
        return P.XT[:, c * XTW + PAD + t0 - 1: c * XTW + PAD + t0 - 1 + tn]

    for l in range(2):
        P.dma(nc.sync, qsrows[0:1, 0:256], qsrows_dram[0:1, l * 256:(l + 1) * 256])
        for j in range(4):
            pb = P.PP[j % 2][:, 0:64]
            P.mm(pb, onr, qsrows[0:1, j * 64:(j + 1) * 64],
                 start=True, stop=True)
            P.V("tensor_copy", qsb[:, j * 64:(j + 1) * 64], pb)
        for ph in range(2):
            P.dma(nc.gpsimd, ws[:, 0:4 * 1024].rearrange("p (c e) -> p c e", c=4),
                  wqkv_dram[l, ph * 512:(ph + 1) * 512, 0:1024]
                  .rearrange("(c p) e -> p c e", p=128))
            for tt in range(NT):
                for fs in range(2):
                    pq = P.PP[(tt * 2 + fs) % 2][:, :]
                    for c in range(4):
                        P.mm(pq, xt_shift(ph * 4 + c, tt * 128, 128),
                             ws[:, c * 1024 + fs * 512: c * 1024 + fs * 512 + 512],
                             start=(c == 0), stop=(c == 3))
                    qdst = QK[:, tt * 1024 + fs * 512: tt * 1024 + fs * 512 + 512]
                    if ph == 0:
                        P.V("tensor_copy", qdst, pq)
                    else:
                        P.V("tensor_tensor", qdst, qdst, pq, ALU.add)
        for tt in range(NT):
            for pj in range(2):
                seg = QK[:, tt * 1024 + pj * 512: tt * 1024 + pj * 512 + 512]
                seg3 = seg.rearrange("p (h d) -> p h d", h=8)
                P.S("activation", SqS[:, 0:512], seg, AF.Square)
                P.V("tensor_reduce", cols[:, 0:8],
                    SqS[:, 0:512].rearrange("p (h d) -> p h d", h=8), AX.X,
                    ALU.add)
                P.S("activation", cols[:, 8:16], cols[:, 0:8], AF.Sqrt)
                P.V("tensor_scalar", cols[:, 8:16], cols[:, 8:16], 1e-12, None,
                    ALU.max)
                P.V("reciprocal", cols[:, 0:8], cols[:, 8:16])
                sview = bass.AP(cols.tensor, cols.offset,
                                [list(cols.ap[0]), [1, 8], [0, 64]])
                P.V("tensor_tensor", seg3, seg3, sview, ALU.mult)
                c1 = rot[:, (2 * pj) * 1024 + tt * 64:
                         (2 * pj) * 1024 + tt * 64 + 64]
                c2 = rot[:, (2 * pj + 1) * 1024 + tt * 64:
                         (2 * pj + 1) * 1024 + tt * 64 + 64]
                P.V("tensor_tensor", dtile[:, 0:64], c1,
                    qsb[:, (2 * pj) * 64:(2 * pj) * 64 + 64], ALU.mult)
                P.V("tensor_tensor", dtile[:, 64:128], c2,
                    qsb[:, (2 * pj + 1) * 64:(2 * pj + 1) * 64 + 64], ALU.mult)
                d1b = bass.AP(dtile.tensor, dtile.offset,
                              [list(dtile.ap[0]), [0, 8], [1, 64]])
                d2b4 = bass.AP(dtile.tensor, dtile.offset + 64,
                               [list(dtile.ap[0]), [0, 8], [32, 2], [1, 32]])
                aswap = bass.AP(seg.tensor, seg.offset + 32,
                                [list(seg.ap[0]), [64, 8], [-32, 2], [1, 32]])
                tb = P.tmpb(0)[:, 0:512]
                tb3 = tb.rearrange("p (h d) -> p h d", h=8)
                P.V("tensor_tensor", tb3, seg3, d1b, ALU.mult)
                tc = P.tmpb(1)[:, 0:512]
                tc4 = tc.rearrange("p (h e d) -> p h e d", h=8, e=2)
                P.V("tensor_tensor", tc4, aswap, d2b4, ALU.mult)
                P.V("tensor_tensor", seg, tb, tc, ALU.add)
        jobs = []
        for tt in range(NT):
            for c in range(8):
                jobs.append((QK[:, tt * 1024 + c * 128: tt * 1024 + (c + 1) * 128],
                             QKT[:, c * T + tt * 128: c * T + (tt + 1) * 128]))
        P.pe_transpose(jobs)
        for ph in range(2):
            P.dma(nc.gpsimd, ws[:, 0:4 * 512].rearrange("p (c e) -> p c e", c=4),
                  wqkv_dram[l, ph * 512:(ph + 1) * 512, 1024:1536]
                  .rearrange("(c p) e -> p c e", p=128))
            for tt in range(NT):
                pv = P.PP[2 + tt % 2][:, :]
                for c in range(4):
                    P.mm(pv, xt_shift(ph * 4 + c, tt * 128, 128),
                         ws[:, c * 512:(c + 1) * 512],
                         start=(c == 0), stop=(c == 3))
                vdst = VTOK[:, tt * 512:(tt + 1) * 512]
                if ph == 0:
                    P.V("tensor_copy", vdst, pv)
                else:
                    P.V("tensor_tensor", vdst, vdst, pv, ALU.add)
        for h in range(HD):
            qc = h // 2
            eo = (h % 2) * 64
            for w in range(4):
                for q4 in range(4):
                    tq = w * 4 + q4
                    strips = [s_ for s_ in range(2)
                              if not (w == 0 and s_ == 0)]
                    co = (q4 % 2) * 8
                    for s_ in strips:
                        kt0 = (w - 1 + s_) * 4
                        ps = P.PP[s_ % 2][:, :]
                        qsl = QKT[eo:eo + 64,
                                  qc * T + tq * 128:qc * T + (tq + 1) * 128]
                        ksl = QKT[eo:eo + 64,
                                  (4 + qc) * T + kt0 * 128:
                                  (4 + qc) * T + kt0 * 128 + 512]
                        P.mm(ps, qsl, ksl, start=True, stop=True)
                        sdst = Ssb[:, s_ * 512:(s_ + 1) * 512]
                        c2m = q4 * 128 + 512 - s_ * 512
                        P.V("tensor_scalar", sdst, ps, 8.0, None, ALU.mult)
                        P.V("tensor_tensor", sdst, sdst,
                            m2[:, 1024 - c2m:1536 - c2m], ALU.add)
                    s0 = strips[0] * 512
                    klen = len(strips) * 512
                    P.V("tensor_reduce", cols[:, co:co + 1],
                        Ssb[:, s0:s0 + klen], AX.X, ALU.max, negate=True)
                    P.S("activation", P.TMPB[:, s0:s0 + klen],
                        Ssb[:, s0:s0 + klen], AF.Exp,
                        bias=cols[:, co:co + 1],
                        accum_out=cols[:, co + 1:co + 2])
                    P.V("reciprocal", cols[:, co + 2:co + 3],
                        cols[:, co + 1:co + 2])
                    P.V("tensor_scalar", P.TMPB[:, s0:s0 + klen],
                        P.TMPB[:, s0:s0 + klen], cols[:, co + 2:co + 3], None,
                        ALU.mult)
                    pav = P.PP[2 + (q4 % 2)][0:64, 0:128]
                    kts = []
                    for s_ in strips:
                        for k4 in range(4):
                            kts.append(((w - 1 + s_) * 4 + k4,
                                        s_ * 512 + k4 * 128))
                    for ki, (ktile, pcol) in enumerate(kts):
                        ptile = (P.PTA, P.PTB)[ki % 2][:, 0:128]
                        P.transpose(ptile, P.TMPB[:, pcol:pcol + 128],
                                    P.ident[:, :])
                        stg = pts[:, (ki % 2) * 128:(ki % 2) * 128 + 128]
                        P.V("tensor_copy", stg, ptile)
                        P.mm(pav, VTOK[:, ktile * 512 + h * 64:
                                       ktile * 512 + h * 64 + 64],
                             stg, start=(ki == 0), stop=(ki == len(kts) - 1))
                    P.V("tensor_copy",
                        OT[eo:eo + 64,
                           qc * T + tq * 128: qc * T + (tq + 1) * 128], pav)
        P.dma(nc.gpsimd, ws[:, 0:4 * 1024].rearrange("p (c e) -> p c e", c=4),
              wo_dram[l].rearrange("(c p) e -> p c e", p=128))
        for tt in range(NT):
            for dhh in range(2):
                po = P.PP[(tt * 2 + dhh) % 2][:, :]
                for c in range(4):
                    P.mm(po, OT[:, c * T + tt * 128: c * T + (tt + 1) * 128],
                         ws[:, c * 1024 + dhh * 512: c * 1024 + dhh * 512 + 512],
                         start=(c == 0), stop=(c == 3))
                xdst = P.X[:, tt * D + dhh * 512: tt * D + dhh * 512 + 512]
                P.V("tensor_tensor", xdst, xdst, po, ALU.add)
        emit_rebuild_xt(P)
        for st_ in range(4):
            pm = P.PF[0:1, :]
            for c in range(NC):
                P.mm(pm, ocb, xt_shift(c, st_ * 512, 512), start=(c == 0),
                     stop=(c == NC - 1))
            sacc = Ssb[0:1, 512:1024]
            P.V("memset", sacc, 0.0)
            for c in range(NC):
                sqb = P.tmpb(c)[:, 0:512]
                P.S("activation", sqb, xt_shift(c, st_ * 512, 512), AF.Square)
                pq2 = P.PTF[0:1, :] if c % 2 else P.PG[0:1, :] \
                    if False else P.PTF[0:1, :]
                P.mm(pq2, ocb, sqb, start=True, stop=True)
                P.V("tensor_tensor", sacc, sacc, pq2, ALU.add)
            mrow = h1S[0:1, 0:512]
            P.V("tensor_scalar", mrow, pm, 1.0 / 1024, None, ALU.mult)
            vrow = SqS[0:1, 0:512]
            P.V("tensor_scalar", vrow, sacc, 1.0 / 1024, None, ALU.mult)
            t2 = Ssb[0:1, 0:512]
            P.V("tensor_tensor", t2, mrow, mrow, ALU.mult)
            P.V("tensor_tensor", vrow, vrow, t2, ALU.subtract)
            P.V("tensor_scalar", vrow, vrow, 1e-5, None, ALU.add)
            P.S("activation", vrow, vrow, AF.Sqrt)
            P.V("reciprocal", vrow, vrow)
            rrow = P.TMPB[0:1, 0:512]
            nrow = P.TMPB[0:1, 512:1024]
            P.V("tensor_copy", rrow, vrow)
            P.V("tensor_tensor", t2, mrow, vrow, ALU.mult)
            P.V("tensor_scalar", t2, t2, -1.0, None, ALU.mult)
            P.V("tensor_copy", nrow, t2)
            P.V("memset", pts[0:1, 0:128], 1.0)
            pb = P.PP[st_ % 2][:, :]
            P.mm(pb, pts[0:1, 0:128], rrow, start=True, stop=True)
            P.V("tensor_copy", RBT[:, st_ * 512: st_ * 512 + 512], pb)
            pb2 = P.PP[2 + st_ % 2][:, :]
            P.mm(pb2, pts[0:1, 0:128], nrow, start=True, stop=True)
            P.V("tensor_copy", RBT[:, T + st_ * 512: T + st_ * 512 + 512], pb2)
        for c in range(NC):
            gcol = gbf[:, l * 16 + c: l * 16 + c + 1]
            bcol = gbf[:, l * 16 + 8 + c: l * 16 + 8 + c + 1]
            xc = XNT[:, c * T:(c + 1) * T]
            P.V("tensor_tensor", xc, xt_shift(c, 0, T), RBT[:, 0:T], ALU.mult)
            P.V("tensor_scalar", xc, xc, gcol, None, ALU.mult)
            P.V("scalar_tensor_tensor", xc, RBT[:, T:2 * T], gcol, xc,
                ALU.mult, ALU.add)
            P.V("tensor_scalar", xc, xc, bcol, None, ALU.add)
        for sec in range(6):
            u0 = sec * 4
            nch = min(4, NU - u0)
            if nch <= 0:
                break
            wsa_ = ws[:, 0:NC * nch * 128]
            wsg_ = ws[:, 4096:4096 + NC * nch * 128]
            P.dma(nc.gpsimd, wsa_.rearrange("p (c j) -> p c j", c=NC),
                  w1a_dram[l, :, u0 * 128:(u0 + nch) * 128]
                  .rearrange("(c p) j -> p c j", p=128))
            P.dma(nc.gpsimd, wsg_.rearrange("p (c j) -> p c j", c=NC),
                  w1g_dram[l, :, u0 * 128:(u0 + nch) * 128]
                  .rearrange("(c p) j -> p c j", p=128))
            for j in range(nch):
                for ts in range(4):
                    it = (j * 4 + ts) % 2
                    pa = P.PP[2 * it][:, :]
                    pg = P.PP[2 * it + 1][:, :]
                    for c in range(NC):
                        P.mm(pa, wsa_[:, c * nch * 128 + j * 128:
                                      c * nch * 128 + (j + 1) * 128],
                             XNT[:, c * T + ts * 512: c * T + ts * 512 + 512],
                             start=(c == 0), stop=(c == NC - 1))
                        P.mm(pg, wsg_[:, c * nch * 128 + j * 128:
                                      c * nch * 128 + (j + 1) * 128],
                             XNT[:, c * T + ts * 512: c * T + ts * 512 + 512],
                             start=(c == 0), stop=(c == NC - 1))
                    jj = u0 + j
                    acol = b1f[:, l * 2 * NU + jj: l * 2 * NU + jj + 1]
                    gcol2 = b1f[:, l * 2 * NU + NU + jj: l * 2 * NU + NU + jj + 1]
                    P.S("activation", gsb[:, :], pg, AF.Gelu, bias=gcol2)
                    P.V("tensor_scalar", h1S[:, 0:512], pa, acol, None, ALU.add)
                    P.V("tensor_tensor",
                        HMT[:, j * T + ts * 512: j * T + ts * 512 + 512],
                        h1S[:, 0:512], gsb[:, :], ALU.mult)
            P.dma(nc.gpsimd,
                  ws[:, 0:nch * 1024].rearrange("p (c n) -> p c n", c=nch),
                  w2_dram[l, u0 * 128:(u0 + nch) * 128]
                  .rearrange("(c p) n -> p c n", p=128))
            for tt in range(NT):
                for dhh in range(2):
                    pd = P.PP[2 + (tt * 2 + dhh) % 2][:, :]
                    for j in range(nch):
                        P.mm(pd, HMT[:, j * T + tt * 128: j * T + (tt + 1) * 128],
                             ws[:, j * 1024 + dhh * 512: j * 1024 + dhh * 512 + 512],
                             start=(j == 0), stop=(j == nch - 1))
                    xdst = P.X[:, tt * D + dhh * 512: tt * D + dhh * 512 + 512]
                    P.V("tensor_tensor", xdst, xdst, pd, ALU.add)
        P.dma(nc.sync, b2rows, b2rows_dram[0:1, l * 1024:(l + 1) * 1024])
        for dhh in range(2):
            pb = P.PP[dhh][:, :]
            P.mm(pb, onr, b2rows[0:1, dhh * 512: dhh * 512 + 512], start=True,
                 stop=True)
            bslab = P.WSF[:, 2048 + dhh * 512: 2048 + dhh * 512 + 512]
            P.V("tensor_copy", bslab, pb)
            for tt in range(NT):
                xdst = P.X[:, tt * D + dhh * 512: tt * D + dhh * 512 + 512]
                P.V("tensor_tensor", xdst, xdst, bslab, ALU.add)
        emit_rebuild_xt(P)
    P.dma(nc.gpsimd, P.X[:, :].rearrange("p (tt d) -> p tt d", tt=NT),
          x_in_dram.rearrange("(tt p) d -> p tt d", p=128),
          accum_op=ALU.add)
    emit_boundary_ln(P)


def build_l0_ffn_prog():
    P = Prog("l0ffn")
    x_in = P.dram_in("x_in", [T, D])
    wqkv = P.dram_in("wqkv", [2, 1024, 1536])
    qsrows = P.dram_in("qsrows", [1, 512])
    rot = P.dram_in("rot", [128, 4096], BF16)
    wo = P.dram_in("wo", [2, 512, 1024])
    gbf = P.dram_in("gbf", [128, 32])
    w1a = P.dram_in("w1a", [2, 1024, 2816])
    w1g = P.dram_in("w1g", [2, 1024, 2816])
    b1f = P.dram_in("b1f", [128, 88])
    w2p = P.dram_in("w2p", [2, 2816, 1024])
    b2rows = P.dram_in("b2rows", [1, 2048])
    m2 = P.dram_in("m2", [128, 1920], BF16)
    P.dram_in("ident", [128, 128], BF16)
    P.dram_in("identf", [128, 128], F32)
    x_out = P.dram_out("x_out", [T, D])
    P.begin()
    emit_load_stream(P, x_in)
    emit_L0_ffn(P, x_in, wqkv, qsrows, rot, wo, gbf, w1a, w1g, b1f, w2p,
                b2rows, m2)
    emit_store_stream(P, x_out)
    P.end()
    return P


def l0_ffn_consts():
    DIM_HEAD, WINDOW, n = 64, 512, 2048
    half_idx = np.arange(0, DIM_HEAD, 2, dtype=np.float64)
    inv_freq = 1.0 / 10000.0 ** (half_idx / DIM_HEAD)
    t = np.arange(n, dtype=np.float64)
    freqs = t[:, None] * inv_freq[None, :]
    freqs = np.concatenate([freqs, freqs], axis=-1)
    base = (half_idx + 0.4 * DIM_HEAD) / (1.4 * DIM_HEAD)
    power = (t - n // 2) / float(WINDOW // 2)
    scale = base[None, :] ** power[:, None]
    scale = np.concatenate([scale, scale], axis=-1)
    cos, sin = np.cos(freqs), np.sin(freqs)
    sgn = np.where(np.arange(64) < 32, -1.0, 1.0)[None, :]
    c1q = cos * scale
    c2q = sin * scale * sgn
    c1k = cos / scale
    c2k = sin / scale * sgn
    rot = np.zeros((128, 4, 16, 64), np.float64)
    for kind, cc in enumerate([c1q, c2q, c1k, c2k]):
        rot[:, kind] = cc.reshape(16, 128, 64).transpose(1, 0, 2)
    rot = rot.reshape(128, 4096)
    p = np.arange(128)[:, None]
    g = np.arange(1920)[None, :]
    rel = p + 1024 - g
    m2 = np.where((rel >= 0) & (rel <= 512), 0.0, -1e30)
    return _bf(rot), _bf(m2)


def l0_ffn_in_maps(inputs, x_full):
    c = host_constants()
    rot, m2 = l0_ffn_consts()
    qs = _f32(inputs["L0_ffn_q_scale"])
    ks = _f32(inputs["L0_ffn_k_scale"])
    perm = np.r_[32:64, 0:32]
    qsrows = np.concatenate(
        [np.concatenate([qs[l], qs[l][perm], ks[l], ks[l][perm]])
         for l in range(2)]).reshape(1, 512)
    g = _f32(inputs["L0_ffn_ln_g"])
    b = _f32(inputs["L0_ffn_ln_b"])
    gbf = np.zeros((128, 32), np.float32)
    for l in range(2):
        gbf[:, l * 16:l * 16 + 8] = g[l].reshape(8, 128).T
        gbf[:, l * 16 + 8:l * 16 + 16] = b[l].reshape(8, 128).T
    w1 = _f32(inputs["L0_ffn_w1"])
    b1 = _f32(inputs["L0_ffn_b1"])
    w2 = _f32(inputs["L0_ffn_w2"])
    w1a = np.zeros((2, 1024, 2816), np.float32)
    w1g = np.zeros((2, 1024, 2816), np.float32)
    w1a[:, :, 0:2730] = w1[:, :, 0:2730]
    w1g[:, :, 0:2730] = w1[:, :, 2730:5460]
    b1f = np.zeros((128, 2 * 2 * 22), np.float32)
    for l in range(2):
        ba = np.zeros(2816, np.float32)
        bg = np.zeros(2816, np.float32)
        ba[0:2730] = b1[l, 0:2730]
        bg[0:2730] = b1[l, 2730:5460]
        b1f[:, l * 44:l * 44 + 22] = ba.reshape(22, 128).T
        b1f[:, l * 44 + 22:l * 44 + 44] = bg.reshape(22, 128).T
    w2p = np.zeros((2, 2816, 1024), np.float32)
    w2p[:, 0:2730] = w2
    b2rows = _f32(inputs["L0_ffn_b2"]).reshape(1, 2048)
    base = {
        "wqkv": _f32(inputs["L0_ffn_wqkv"]),
        "qsrows": _f32(qsrows), "rot": rot,
        "wo": _f32(inputs["L0_ffn_wo"]),
        "gbf": gbf, "w1a": w1a, "w1g": w1g, "b1f": b1f, "w2p": w2p,
        "b2rows": b2rows, "m2": m2,
        "ident": c["ident"], "identf": c["identf"],
    }
    maps = []
    for core in range(8):
        m = dict(base)
        m["x_in"] = _f32(x_full[core])
        maps.append(m)
    return maps




def emit_L2_attn(P, wqkv_dram, bqkv_row_dram, wproj_dram, bproj_row_dram,
                 map_i32_dram, s4_row_dram):
    nc = P.nc
    qkv_tok = P.WSA[:, 0:NT * 1024]
    OHS = P.WSC[:, 0:16 * 512]
    OT2 = P.WSB[:, 0:NC * T]
    ws = P.WSC[:, 0:8192]
    bias_slab = P.WSF[:, 2048:3072]
    browq = P.WSF[0:1, 0:1024]
    rowA = P.WSF[0:1, 1024:3072]
    mrow = P.WSB[0:1, 0:4096].bitcast(F32)
    mirow = P.WSB[0:1, 4096:8192].bitcast(I32)
    sgrow = P.WSB[0:1, 8192:12288].bitcast(F32)
    kprow = P.WSB[0:1, 12288:16384].bitcast(F32)
    idxrow = P.WSB[0:1, 4096:8192].bitcast(F32)
    pen_all = P.sb("l2a_pen", [64, 32 * 64], BF16)
    mcols = P.sb("l2a_mcols", [128, 16], F32)
    jcol = P.sb("l2a_jcol", [128, 2], F32)
    thcol = P.sb("l2a_th", [64, 1], F32)
    onr = P.sb("l2a_onr", [1, 128], F32)[:, :]
    obf = P.sb("l2a_obf", [64, 1], BF16)[:, :]
    sexp = P.sb("l2a_sexp", [64, 2 * 64], BF16)
    stg = P.sb("l2a_stg", [128, 3 * 512], BF16)
    lcol = P.sb("l2a_lcol", [64, 2], F32)
    cols = P.stats[:, NT * 8: NT * 8 + 16]
    qmT_d = P.dram_tmp("l2a_qmT", [128, NC, T], BF16)
    kselT_d = P.dram_tmp("l2a_kselT", [128, NC, T], BF16)
    vsel_d = P.dram_tmp("l2a_vsel", [128, NT, 1024], BF16)
    P.V("memset", onr, 1.0)
    P.V("memset", obf, 1.0)
    P.dma(nc.sync, mirow[0:1, :], map_i32_dram)
    P.V("tensor_copy", mrow[0:1, :], mirow[0:1, :])
    P.V("tensor_scalar", mirow[0:1, :], mirow[0:1, :], -256, None,
        ALU.bitwise_and)
    P.V("tensor_copy", sgrow[0:1, :], mirow[0:1, :])
    sgrep = bass.AP(sgrow[0:1, :].tensor, sgrow[0:1, :].offset,
                    [list(sgrow[0:1, :].ap[0]), [64, 32], [0, 64]])
    P.G("iota", kprow[0:1, :].rearrange("o (n s) -> o n s", n=32),
        [[0, 32], [4, 64]], base=0, channel_multiplier=0,
        allow_small_or_imprecise_dtypes=True)
    P.V("tensor_tensor", kprow[0:1, :].rearrange("o (n s) -> o n s", n=32),
        kprow[0:1, :].rearrange("o (n s) -> o n s", n=32), sgrep, ALU.add)
    mask0 = rowA[0:1, 0:2048]
    P.V("tensor_scalar", mask0, kprow[0:1, :], 2048.0, None, ALU.is_lt)
    P.V("tensor_scalar", kprow[0:1, :], kprow[0:1, :], 2047.0, None, ALU.min)
    jobs = []
    for jt in range(NT):
        jobs.append((mrow[0:1, jt * 128:(jt + 1) * 128], mcols[:, jt:jt + 1]))
    P.pe_transpose_f32(jobs, rows=1)
    P.G("iota", jcol[:, 0:1], [[0, 1]], base=0, channel_multiplier=1,
        allow_small_or_imprecise_dtypes=True)
    for st_ in range(4):
        pid = P.PF[0:1, 0:512]
        for jt in range(NT):
            pb = P.PTF[:, :]
            P.mm(pb, onr, kprow[0:1, st_ * 512:(st_ + 1) * 512], start=True,
                 stop=True)
            P.V("tensor_scalar", jcol[:, 1:2], jcol[:, 0:1], float(jt * 128),
                None, ALU.add)
            ohf = stg[:, :].bitcast(F32)[:, 0:512]
            P.V("tensor_scalar", ohf, pb, jcol[:, 1:2], None, ALU.is_equal)
            P.mm(pid, mcols[:, jt:jt + 1], ohf, start=(jt == 0),
                 stop=(jt == NT - 1))
        P.V("tensor_copy", idxrow[0:1, st_ * 512:(st_ + 1) * 512], pid)
    P.V("tensor_scalar", idxrow[0:1, :], idxrow[0:1, :], 2047.0, None, ALU.min)
    P.V("tensor_tensor", idxrow[0:1, :], idxrow[0:1, :], mask0, ALU.mult)
    P.V("tensor_scalar", mask0, mask0, 1.0, -1.0, ALU.mult, ALU.add)
    P.V("tensor_tensor", idxrow[0:1, :], idxrow[0:1, :], mask0, ALU.add)
    P.V("tensor_scalar", mrow[0:1, :], mrow[0:1, :], 2047.0, None, ALU.min)

    def produce(src_lo):
        P.dma(nc.sync, browq, bqkv_row_dram[0:1, src_lo:src_lo + 1024])
        for half in range(2):
            pb = P.PP[half][:, :]
            P.mm(pb, onr, browq[0:1, half * 512:(half + 1) * 512], start=True,
                 stop=True)
            P.V("tensor_copy", bias_slab[:, half * 512:(half + 1) * 512], pb)
        for ph in range(2):
            P.dma(nc.gpsimd, ws[:, 0:4 * 1024].rearrange("p (c e) -> p c e", c=4),
                  wqkv_dram[ph * 512:(ph + 1) * 512, src_lo:src_lo + 1024]
                  .rearrange("(c p) e -> p c e", p=128))
            for tt in range(NT):
                for fs in range(2):
                    pq = P.PP[2 + (tt * 2 + fs) % 2][:, :]
                    for c in range(4):
                        P.mm(pq, P.xt_chunk(ph * 4 + c, tt * 128, 128),
                             ws[:, c * 1024 + fs * 512: c * 1024 + fs * 512 + 512],
                             start=(c == 0), stop=(c == 3))
                    qdst = qkv_tok[:, tt * 1024 + fs * 512:
                                   tt * 1024 + fs * 512 + 512]
                    if ph == 0:
                        P.V("tensor_tensor", qdst, pq,
                            bias_slab[:, fs * 512:(fs + 1) * 512], ALU.add)
                    else:
                        P.V("tensor_tensor", qdst, qdst, pq, ALU.add)

    def gather(rowsel, dst_dram, feat_major, scale):
        for st_ in range(4):
            for jt in range(NT):
                pb = P.PTF[:, :]
                P.mm(pb, onr, rowsel[0:1, st_ * 512:(st_ + 1) * 512],
                     start=True, stop=True)
                P.V("tensor_scalar", jcol[:, 1:2], jcol[:, 0:1],
                    float(jt * 128), None, ALU.add)
                oh = OHS[:, jt * 512:(jt + 1) * 512]
                P.V("tensor_scalar", oh, pb, jcol[:, 1:2], None, ALU.is_equal)
            if feat_major:
                for c in range(NC):
                    pg = P.PP[c % 2][:, :]
                    for jt in range(NT):
                        P.mm(pg, qkv_tok[:, jt * 1024 + c * 128:
                                         jt * 1024 + (c + 1) * 128],
                             OHS[:, jt * 512:(jt + 1) * 512],
                             start=(jt == 0), stop=(jt == NT - 1))
                    sg_ = stg[:, (c % 3) * 512:(c % 3) * 512 + 512]
                    P.V("tensor_scalar", sg_, pg, scale, None, ALU.mult)
                    P.dma(nc.sync, dst_dram[:, c, st_ * 512:(st_ + 1) * 512],
                          sg_)
            else:
                for q4 in range(4):
                    ot = st_ * 4 + q4
                    for dh in range(2):
                        pg = P.PP[(q4 * 2 + dh) % 2][:, :]
                        for jt in range(NT):
                            P.mm(pg, OHS[:, jt * 512 + q4 * 128:
                                         jt * 512 + (q4 + 1) * 128],
                                 qkv_tok[:, jt * 1024 + dh * 512:
                                         jt * 1024 + dh * 512 + 512],
                                 start=(jt == 0), stop=(jt == NT - 1))
                        sg_ = stg[:, dh * 512:(dh + 1) * 512]
                        P.V("tensor_copy", sg_, pg)
                        P.dma(nc.sync,
                              dst_dram[:, ot, dh * 512:dh * 512 + 512], sg_)

    produce(0)
    gather(mrow[0:1, :], qmT_d, True, 0.125)
    produce(1024)
    gather(idxrow[0:1, :], kselT_d, True, 1.0)
    produce(2048)
    gather(idxrow[0:1, :], vsel_d, False, 1.0)

    qmc = P.TMPB[:, 0:T]
    ksc = P.sb("l2a_ksc", [128, T], BF16)
    vsc2 = OHS[0:64, 0:4096]
    P.G("iota", thcol[0:64, 0:1], [[0, 1]], base=0, channel_multiplier=-4,
        allow_small_or_imprecise_dtypes=True)
    P.V("tensor_scalar", thcol[0:64, 0:1], thcol[0:64, 0:1], 2048.0, None,
        ALU.add)
    for n in range(32):
        psg = P.PTF[0:64, 0:64]
        P.mm(psg, onr[0:1, 0:64], sgrow[0:1, n * 64:(n + 1) * 64],
             start=True, stop=True)
        pen = pen_all[0:64, n * 64:(n + 1) * 64]
        P.V("tensor_scalar", pen, psg, thcol[0:64, 0:1], None, ALU.is_ge)
        P.V("tensor_scalar", pen, pen, -1e9, None, ALU.mult)
    for c in range(NC):
        P.dma(nc.sync, qmc, qmT_d[:, c])
        P.dma(nc.sync, ksc[:, :], kselT_d[:, c])
        P.dma(nc.sync,
              vsc2[0:64, :].rearrange("p (n b) -> p n b", n=32)[:, 0:32:2, :],
              vsel_d[0:64, :, c * 128:(c + 1) * 128])
        P.dma(nc.sync,
              vsc2[0:64, :].rearrange("p (n b) -> p n b", n=32)[:, 1:32:2, :],
              vsel_d[64:128, :, c * 128:(c + 1) * 128])
        for n in range(32):
            pen = pen_all[0:64, n * 64:(n + 1) * 64]
            for hh in range(2):
                eo = hh * 64
                pst = P.PP[hh][0:64, 0:64]
                P.mm(pst, ksc[eo:eo + 64, n * 64:(n + 1) * 64],
                     qmc[eo:eo + 64, n * 64:(n + 1) * 64], start=True,
                     stop=True)
                sT = P.WSF[0:64, 1024:1088]
                P.V("tensor_tensor", sT, pst, pen, ALU.add)
                P.V("tensor_reduce", cols[0:64, 1:2], sT, AX.X, ALU.max,
                    negate=True)
                se = sexp[0:64, hh * 64:(hh + 1) * 64]
                P.S("activation", se, sT, AF.Exp, bias=cols[0:64, 1:2])
                pls = P.PF[0:1, 0:64]
                P.mm(pls, obf, se, start=True, stop=True)
                P.V("tensor_copy", rowA[0:1, 0:64], pls)
                P.pe_transpose_f32([(rowA[0:1, 0:64], lcol[0:64, 0:1])],
                                   rows=1)
                P.V("tensor_scalar", lcol[0:64, 0:1], lcol[0:64, 0:1], 1e-6,
                    None, ALU.add)
                P.V("reciprocal", lcol[0:64, 1:2], lcol[0:64, 0:1])
                pav = P.PP[2 + hh][0:64, 0:64]
                vslice = vsc2[0:64, n * 128 + hh * 64: n * 128 + hh * 64 + 64]
                P.mm(pav, se, vslice, start=True, stop=True)
                osb = stg[0:64, 0:128]
                P.V("tensor_scalar", osb[0:64, hh * 64:(hh + 1) * 64], pav,
                    lcol[0:64, 1:2], None, ALU.mult)
            ptile = (P.PTA, P.PTB)[n % 2][:, 0:64]
            P.transpose(ptile, stg[0:64, 0:128], P.ident[0:64, 0:64])
            P.V("tensor_copy",
                OT2[:, c * T + n * 64: c * T + n * 64 + 64], ptile)
    for ph in range(2):
        P.dma(nc.gpsimd, ws[:, 0:4 * 1024].rearrange("p (c e) -> p c e", c=4),
              wproj_dram[ph * 512:(ph + 1) * 512]
              .rearrange("(c p) e -> p c e", p=128))
        for tt in range(NT):
            for dhh in range(2):
                po = P.PP[(tt * 2 + dhh) % 2][:, :]
                for c in range(4):
                    P.mm(po, OT2[:, (ph * 4 + c) * T + tt * 128:
                                 (ph * 4 + c) * T + (tt + 1) * 128],
                         ws[:, c * 1024 + dhh * 512: c * 1024 + dhh * 512 + 512],
                         start=(c == 0), stop=(c == 3))
                xdst = P.X[:, tt * D + dhh * 512: tt * D + dhh * 512 + 512]
                P.V("tensor_tensor", xdst, xdst, po, ALU.add)
    P.dma(nc.sync, browq, bproj_row_dram)
    for dhh in range(2):
        pb = P.PP[dhh][:, :]
        P.mm(pb, onr, browq[0:1, dhh * 512: dhh * 512 + 512], start=True,
             stop=True)
        bslab = bias_slab[:, dhh * 512:(dhh + 1) * 512]
        P.V("tensor_copy", bslab, pb)
        for tt in range(NT):
            xdst = P.X[:, tt * D + dhh * 512: tt * D + dhh * 512 + 512]
            P.V("tensor_tensor", xdst, xdst, bslab, ALU.add)
    emit_boundary_ln(P)


def build_l2_attn_prog():
    P = Prog("l2attn")
    x_in = P.dram_in("x_in", [T, D])
    wqkv = P.dram_in("wqkv", [1024, 3072])
    bqr = P.dram_in("bqr", [1, 3072])
    wproj = P.dram_in("wproj", [1024, 1024])
    bpr = P.dram_in("bpr", [1, 1024])
    mapi = P.dram_in("mapi", [1, 2048], I32)
    P.dram_in("ident", [128, 128], BF16)
    P.dram_in("identf", [128, 128], F32)
    x_out = P.dram_out("x_out", [T, D])
    P.begin()
    emit_load_stream(P, x_in)
    emit_L2_attn(P, wqkv, bqr, wproj, bpr, mapi, None)
    emit_store_stream(P, x_out)
    P.end()
    return P


def l2_attn_in_maps(inputs, x_full):
    c = host_constants()
    base = {
        "wqkv": _f32(inputs["L2_attn_Wqkv"]),
        "bqr": _f32(inputs["L2_attn_bqkv"]).reshape(1, 3072),
        "wproj": _f32(inputs["L2_attn_Wproj"]),
        "bpr": _f32(inputs["L2_attn_bproj"]).reshape(1, 1024),
        "mapi": np.ascontiguousarray(
            np.asarray(inputs["L2_attn_mapping"])[:2048].reshape(1, 2048),
            dtype=np.int32),
        "ident": c["ident"], "identf": c["identf"],
    }
    maps = []
    for core in range(8):
        m = dict(base)
        m["x_in"] = _f32(x_full[core])
        maps.append(m)
    return maps




def emit_L1_attn(P, wqt_dram, wkt_dram, wvt_dram, wot_dram, bof_dram,
                 bot_dram):
    nc = P.nc
    QT = P.WSA[:, 0:NC * T]
    KT = P.WSB[:, 0:NC * T]
    ws = P.WSC[:, 0:8192]
    vtc = P.sb("l1a_vtc", [128, 2048], BF16)[:, :]
    prodb8 = P.sb("l1a_prod", [128, 8 * 256], BF16)
    bof = P.sb("l1a_bof", [128, NC * 16], BF16)
    bot = P.sb("l1a_bot", [16, 1024], BF16)
    SSC = P.WSF[0:16, 0:2048]
    SMX = P.WSF[0:16, 2048:2560]
    Ps = P.sb("l1a_ps", [16, 2048], BF16)
    acc = P.WSF[:, 2560:3072].bitcast(BF16).bitcast(F32) if False else P.WSF[:, 2560:3072]
    vT_d = P.dram_tmp("l1a_vT", [128, NC, T], BF16)
    oT_d = P.dram_tmp("l1a_oT", [128, NC, T], BF16)
    P.dma(nc.sync, bof[:, :], bof_dram)
    P.dma(nc.sync, bot[0:16, :], bot_dram)

    def proj_feat(dst_sb, dst_dram, wt_dram):
        for ph in range(2):
            P.dma(nc.gpsimd, ws[:, 0:4 * 1024].rearrange("p (c e) -> p c e", c=4),
                  wt_dram[ph * 512:(ph + 1) * 512]
                  .rearrange("(c p) e -> p c e", p=128))
            for ec in range(NC):
                for ts in range(4):
                    pq = P.PP[(ec * 4 + ts) % 2][:, :]
                    for c in range(4):
                        P.mm(pq, ws[:, c * 1024 + ec * 128:
                                    c * 1024 + (ec + 1) * 128],
                             P.xt_chunk(ph * 4 + c, ts * 512, 512),
                             start=(c == 0), stop=(c == 3))
                    if dst_sb is not None:
                        ddst = dst_sb[:, ec * T + ts * 512: ec * T + ts * 512 + 512]
                        if ph == 0:
                            P.V("tensor_copy", ddst, pq)
                        else:
                            P.V("tensor_tensor", ddst, ddst, pq, ALU.add)
                    else:
                        ddst = vtc[:, ts * 512: ts * 512 + 512]
                        P.V("tensor_copy", ddst, pq)
                if dst_sb is None:
                    if ph == 0:
                        P.dma(nc.sync, dst_dram[:, ec], vtc[:, :])
                    else:
                        P.dma(nc.gpsimd, dst_dram[:, ec], vtc[:, :],
                              accum_op=ALU.add)

    proj_feat(QT, None, wqt_dram)
    proj_feat(KT, None, wkt_dram)
    proj_feat(None, vT_d, wvt_dram)

    for s in range(8):
        for sp in range(8):
            psc = P.PF[0:16, 0:256]
            for c in range(NC):
                pr = prodb8[:, c * 256:(c + 1) * 256]
                P.V("tensor_tensor", pr,
                    QT[:, c * T + s * 256:(c * T) + s * 256 + 256],
                    KT[:, c * T + sp * 256: c * T + sp * 256 + 256], ALU.mult)
                P.mm(psc, bof[:, c * 16:(c + 1) * 16], pr,
                     start=(c == 0), stop=(c == NC - 1))
            sscv = bass.AP(SSC[0:16, :].tensor, SSC[0:16, :].offset + sp,
                           [list(SSC[0:16, :].ap[0]), [8, 256]])
            P.V("tensor_scalar", sscv, psc, 0.125, None, ALU.mult)
        s3 = SSC[0:16, :].rearrange("p (t e) -> p t e", t=256)
        P.V("tensor_reduce", SMX[0:16, 0:256], s3, AX.X, ALU.max)
        mb = bass.AP(SMX[0:16, :].tensor, SMX[0:16, :].offset,
                     [list(SMX[0:16, :].ap[0]), [1, 256], [0, 8]])
        P.V("tensor_tensor", s3, s3, mb, ALU.subtract)
        P.S("activation", Ps[0:16, :], SSC[0:16, :], AF.Exp)
        p3 = Ps[0:16, :].rearrange("p (t e) -> p t e", t=256)
        P.V("tensor_reduce", SMX[0:16, 256:512], p3, AX.X, ALU.add)
        P.V("reciprocal", SMX[0:16, 256:512], SMX[0:16, 256:512])
        rb = bass.AP(SMX[0:16, :].tensor, SMX[0:16, :].offset + 256,
                     [list(SMX[0:16, :].ap[0]), [1, 256], [0, 8]])
        P.V("tensor_tensor", p3, p3, rb, ALU.mult)
        for c in range(NC):
            P.dma(nc.sync, vtc[:, :], vT_d[:, c])
            ab = acc[:, (c % 2) * 256:(c % 2) * 256 + 256]
            for sp in range(8):
                pb = (P.PTF, P.PG if False else P.PF)[0][:, 0:256] \
                    if False else (P.PTF, P.PF)[sp % 2][:, 0:256]
                psl = bass.AP(Ps[0:16, :].tensor, Ps[0:16, :].offset + sp,
                              [list(Ps[0:16, :].ap[0]), [8, 256]])
                P.mm(pb, bot[0:16, c * 128:(c + 1) * 128], psl, start=True,
                     stop=True)
                if sp == 0:
                    P.V("tensor_tensor", ab,
                        vtc[:, sp * 256:(sp + 1) * 256], pb, ALU.mult)
                else:
                    tt1 = P.TMPB[:, 0:256].bitcast(F32)[:, 0:128] \
                        if False else P.TMPB[:, 0:512].bitcast(F32)[:, 0:256]
                    P.V("tensor_tensor", tt1, vtc[:, sp * 256:(sp + 1) * 256],
                        pb, ALU.mult)
                    P.V("tensor_tensor", ab, ab, tt1, ALU.add)
            sgo = P.TMPB[:, 512:768]
            P.V("tensor_copy", sgo, ab)
            P.dma(nc.sync, oT_d[:, c, s * 256:(s + 1) * 256], sgo)
    for c in range(NC):
        P.dma(nc.sync, QT[:, c * T:(c + 1) * T], oT_d[:, c])
    for ph in range(2):
        P.dma(nc.gpsimd, ws[:, 0:4 * 1024].rearrange("p (c e) -> p c e", c=4),
              wot_dram[ph * 512:(ph + 1) * 512]
              .rearrange("(c p) e -> p c e", p=128))
        for tt in range(NT):
            for dhh in range(2):
                po = P.PP[(tt * 2 + dhh) % 2][:, :]
                for c in range(4):
                    P.mm(po, QT[:, (ph * 4 + c) * T + tt * 128:
                                 (ph * 4 + c) * T + (tt + 1) * 128],
                         ws[:, c * 1024 + dhh * 512: c * 1024 + dhh * 512 + 512],
                         start=(c == 0), stop=(c == 3))
                xdst = P.X[:, tt * D + dhh * 512: tt * D + dhh * 512 + 512]
                P.V("tensor_tensor", xdst, xdst, po, ALU.add)
    emit_boundary_ln(P)


def build_l1_attn_prog():
    P = Prog("l1attn")
    x_in = P.dram_in("x_in", [T, D])
    wqt = P.dram_in("wqt", [1024, 1024])
    wkt = P.dram_in("wkt", [1024, 1024])
    wvt = P.dram_in("wvt", [1024, 1024])
    wot = P.dram_in("wot", [1024, 1024])
    bof = P.dram_in("bof", [128, NC * 16], BF16)
    bot = P.dram_in("bot", [16, 1024], BF16)
    gw = P.dram_in("gw_t", [D, 8])
    P.dram_in("ident", [128, 128], BF16)
    P.dram_in("identf", [128, 128], F32)
    x_out = P.dram_out("x_out", [T, D])
    lg_out = P.dram_out("lg_out", [T, 8])
    P.begin()
    emit_load_stream(P, x_in)
    emit_L1_attn(P, wqt, wkt, wvt, wot, bof, bot)
    emit_store_stream(P, x_out)
    lgts = P.WSF[:, 0:NT * 8]
    emit_router_logits(P, gw, 8, lgts, "l1alg", wsf_overlay=True)
    tok = P.dma(P.nc.sync, lg_out.rearrange("(tt p) e -> p tt e", p=128),
                lgts[:, :].rearrange("p (tt e) -> p tt e", tt=NT))
    P.wait_tok(P.nc.sync, tok)
    P.end()
    return P


def l1_attn_in_maps(inputs, x_full):
    c = host_constants()
    W = np.asarray(inputs["L1_attn_in_proj_weight"])
    Wt = W.T
    H, hd = 16, 64
    idx = np.arange(3072).reshape(H, 3, hd)
    qcols = idx[:, 0, :].reshape(-1)
    kcols = idx[:, 1, :].reshape(-1)
    vcols = idx[:, 2, :].reshape(-1)
    wqt = _f32(Wt[:, qcols])
    wkt = _f32(Wt[:, kcols])
    wvt = _f32(Wt[:, vcols])
    wot = _f32(np.asarray(inputs["L1_attn_out_weight"]).T)
    d = np.arange(1024)
    bo = (d[:, None] // 64 == np.arange(16)[None, :]).astype(np.float32)
    bof = _bf(bo.reshape(8, 128, 16).transpose(1, 0, 2).reshape(128, 128))
    bot = _bf(bo.T)
    base = {"wqt": wqt, "wkt": wkt, "wvt": wvt, "wot": wot,
            "bof": bof, "bot": bot,
            "gw_t": _f32(np.asarray(inputs["L1_ffn_gate_weight"]).T),
            "ident": c["ident"], "identf": c["identf"]}
    maps = []
    for core in range(8):
        m = dict(base)
        xs = np.ascontiguousarray(
            np.asarray(x_full)[:, core * 256:(core + 1) * 256, :]
        ).reshape(T, D)
        m["x_in"] = _f32(xs)
        maps.append(m)
    return maps




_PROGS = {}


def _prog(name):
    if name not in _PROGS:
        _PROGS[name] = {
            "l0_attn": build_l0_attn_prog,
            "l0_ffn": build_l0_ffn_prog,
            "l1_attn": build_l1_attn_prog,
            "l1_logits": build_l1_logits_prog,
            "l1_ffn": build_l1_ffn_prog,
            "l2_attn": build_l2_attn_prog,
            "l2_ffn": build_l2_ffn_prog,
            "l3_attn": build_l3_attn_prog,
            "l3_ffn": build_l3_ffn_prog,
        }[name]()
    return _PROGS[name]


def _run(name, maps, outkey="x_out"):
    P = _prog(name)
    res = run_bass_kernel_spmd(P.nc, maps, core_ids=list(range(8)))
    return np.stack([res.results[c][outkey] for c in range(8)])


def kernel(**inputs):
    x = np.ascontiguousarray(np.asarray(inputs["L0_attn_x"], np.float32))
    x = _run("l0_attn", l0_attn_in_maps(inputs, x))
    x = _run("l0_ffn", l0_ffn_in_maps(inputs, x))
    P1 = _prog("l1_attn")
    res1 = run_bass_kernel_spmd(P1.nc, l1_attn_in_maps(inputs, x),
                                core_ids=list(range(8)))
    xp = np.stack([res1.results[c]["x_out"] for c in range(8)])
    lgp = np.stack([res1.results[c]["lg_out"] for c in range(8)])
    x = np.concatenate([xp[i].reshape(8, 256, D) for i in range(8)], axis=1)
    lg = np.concatenate([lgp[i].reshape(8, 256, 8) for i in range(8)], axis=1)
    x = _run("l1_ffn", l1_ffn_in_maps(inputs, x, lg))
    x = _run("l2_attn", l2_attn_in_maps(inputs, x))
    x = _run("l2_ffn", l2_ffn_in_maps(inputs, x))
    x = _run("l3_attn", l3_attn_in_maps(inputs, x))
    x = _run("l3_ffn", l3_ffn_in_maps(inputs, x))
    return np.ascontiguousarray(x, np.float32)


# version 04 (final)
# speedup vs baseline: 1.0202x; 1.0202x over previous
import math
import numpy as np
import ml_dtypes
import concourse.bass as bass
import concourse.mybir as mybir
from concourse.ap import AP
from concourse.bass_utils import run_bass_kernel_spmd

F32 = mybir.dt.float32
BF16 = mybir.dt.bfloat16
I32 = mybir.dt.int32
AF = mybir.ActivationFunctionType
ALU = mybir.AluOpType
AX = mybir.AxisListType

T = 2048
D = 1024
NT = 16
NC = 8
PAD = 8
XTW = PAD + T
BF = ml_dtypes.bfloat16


def _bf(x):
    return np.ascontiguousarray(x, dtype=BF)


def _f32(x):
    return np.ascontiguousarray(np.asarray(x), dtype=np.float32)


def host_constants():
    return {
        "ident": np.eye(128, dtype=BF),
        "identf": np.eye(128, dtype=np.float32),
    }


def _ap_span(ap):
    t = ap.tensor
    name = t.name
    kind = type(t).__name__
    if "PSum" in kind:
        return (name, 0, 1 << 40)
    esz = mybir.dt.sizeof(ap.dtype) if hasattr(mybir.dt, "sizeof") else {
        F32: 4, BF16: 2, I32: 4}.get(ap.dtype, 4)
    dims = list(ap.ap)
    is_dram = "DRam" in kind
    off = ap.offset
    if not is_dram and dims:
        pitch = abs(dims[0][0])
        if pitch > 0:
            off = off % pitch
    lo = off * esz
    reach = 0
    use = dims if is_dram else dims[1:]
    for step, cnt in use:
        if cnt > 1:
            reach += abs(step) * (cnt - 1)
    return (name, lo, lo + (reach + 1) * esz)


class Prog:

    def __init__(self, name="prog"):
        self.nc = bass.Bass("TRN2", target_bir_lowering=False)
        self.name = name
        self.inputs = {}
        self.outputs = {}
        self._sb = []
        self._psum = []

    def dram_in(self, name, shape, dtype=F32):
        ap = self.nc.dram_tensor(name, list(shape), dtype, kind="ExternalInput").ap()
        self.inputs[name] = ap
        return ap

    def dram_out(self, name, shape, dtype=F32):
        ap = self.nc.dram_tensor(name, list(shape), dtype, kind="ExternalOutput").ap()
        self.outputs[name] = ap
        return ap

    def dram_tmp(self, name, shape, dtype=F32, addr_space="Local"):
        return self.nc.dram_tensor(name, list(shape), dtype, kind="Internal",
                                   addr_space=addr_space).ap()

    def sb(self, name, shape, dtype):
        h = self.nc.sbuf_tensor(name, list(shape), dtype).__enter__()
        self._sb.append(h)
        return h

    def ps(self, name, shape, dtype=F32):
        h = self.nc.psum_tensor(name, list(shape), dtype).__enter__()
        self._psum.append(h)
        return h

    def begin(self):
        self.sems = {}
        qs = [f"q_{q}{i}" for q in ("sp", "gp", "act") for i in range(4)]
        for s in ["pe", "dve", "act", "pool"] + qs:
            self.sems[s] = [self.nc.semaphore(s).__enter__(), 0]
        self.X = self.sb("X", [128, NT * D], F32)
        self.XT = self.sb("XT", [128, NC * XTW], BF16)
        self.TMPB = self.sb("TMPB", [128, 2 * D], BF16)
        self.WSA = self.sb("WSA", [128, 16384], BF16)
        self.WSB = self.sb("WSB", [128, 16384], BF16)
        self.WSC = self.sb("WSC", [128, 8192], BF16)
        self.WSF = self.sb("WSF", [128, 3072], F32)
        self.ident = self.sb("identc", [128, 128], BF16)
        self.identf = self.sb("identfc", [128, 128], F32)
        self.stats = self.sb("statsb", [128, NT * 8 + 16], F32)
        self.PP = [self.ps(f"pp{i}", [128, 512], F32) for i in range(4)]
        self.PTA = self.ps("pta", [128, 1024], BF16)
        self.PTB = self.ps("ptb", [128, 1024], BF16)
        self.PF = self.ps("pfb", [128, 512], F32)
        self.PTF = self.ps("ptf", [128, 512], F32)
        self._mm_pend = {}
        self._wr = {}
        self._rd = {}
        self._waited = {}
        self._qstate = {}
        self._eng_sem = {"tensor": "pe", "vector": "dve", "scalar": "act",
                         "gpsimd": "pool"}
        self.block = self.nc.Block().__enter__()
        return self

    def end(self):
        self.block.__exit__(None, None, None)

    def inc(self, instr, sname, k=1):
        s = self.sems[sname]
        s[1] += k
        instr.then_inc(s[0], k)
        return s[1]

    def _deps_for(self, reads, writes, own_sem=None):
        deps = {}

        def scan(ap, lists):
            name, lo, hi = _ap_span(ap)
            for lst in lists:
                for (l, h, sem, val) in lst.get(name, ()):
                    if l < hi and lo < h and val > deps.get(sem, 0):
                        deps[sem] = val

        for ap in reads:
            scan(ap, (self._wr,))
        for ap in writes:
            scan(ap, (self._wr, self._rd))
        return deps

    def _apply_waits(self, eng, ename, deps):
        waited = self._waited.setdefault(ename, {})
        for sem, val in sorted(deps.items()):
            if waited.get(sem, 0) >= val:
                continue
            eng.wait_ge(self.sems[sem][0], val)
            waited[sem] = val

    def _record(self, reads, writes, sem, val):
        for ap in reads:
            name, lo, hi = _ap_span(ap)
            self._rd.setdefault(name, []).append((lo, hi, sem, val))
        for ap in writes:
            name, lo, hi = _ap_span(ap)
            wl = self._wr.setdefault(name, [])
            wl[:] = [e for e in wl if not (lo <= e[0] and e[1] <= hi)]
            wl.append((lo, hi, sem, val))
            rl = self._rd.get(name)
            if rl:
                rl[:] = [e for e in rl if not (lo <= e[0] and e[1] <= hi)]
        for d in (self._wr, self._rd):
            for ap in writes:
                name = _ap_span(ap)[0]
                l = d.get(name)
                if l and len(l) > 192:
                    best = {}
                    lo = min(e[0] for e in l)
                    hi = max(e[1] for e in l)
                    for (_, _, sem, val) in l:
                        best[sem] = max(best.get(sem, 0), val)
                    d[name] = [(lo, hi, sem, val) for sem, val in best.items()]

    def emit(self, ename, method, *args, xreads=(), xwrites=(), **kw):
        eng = getattr(self.nc, ename)
        esem = self._eng_sem[ename]
        writes = list(xwrites)
        reads = list(xreads)
        seen_out = False
        for a in args:
            if isinstance(a, AP):
                if not seen_out:
                    writes.append(a)
                    seen_out = True
                else:
                    reads.append(a)
        for k, v in kw.items():
            if isinstance(v, AP):
                if k in ("accum_out",):
                    writes.append(v)
                else:
                    reads.append(v)
        deps = self._deps_for(reads, writes)
        self._apply_waits(eng, ename, deps)
        getattr(eng, method)(*args, **kw)
        ins = eng
        raise RuntimeError("use _emit")

    def _emit(self, ename, method, args, kw, xreads=(), xwrites=()):
        eng = getattr(self.nc, ename)
        esem = self._eng_sem[ename]
        writes = list(xwrites)
        reads = list(xreads)
        seen_out = False
        for a in args:
            if isinstance(a, AP):
                if not seen_out:
                    writes.append(a)
                    seen_out = True
                else:
                    reads.append(a)
        for k, v in kw.items():
            if isinstance(v, AP):
                if k in ("accum_out",):
                    writes.append(v)
                else:
                    reads.append(v)
        deps = self._deps_for(reads, writes, own_sem=esem)
        self._apply_waits(eng, ename, deps)
        ins = getattr(eng, method)(*args, **kw)
        val = self.inc(ins, esem)
        self._record(reads, writes, esem, val)
        return (esem, val)

    def V(self, method, *args, **kw):
        return self._emit("vector", method, args, kw)

    def S(self, method, *args, **kw):
        return self._emit("scalar", method, args, kw)

    def G(self, method, *args, **kw):
        return self._emit("gpsimd", method, args, kw)

    def mm(self, out, lhsT, rhs, start, stop):
        eng = self.nc.tensor
        reads = [lhsT, rhs]
        deps = self._deps_for(reads, [out] if start else [], own_sem="pe")
        self._apply_waits(eng, "tensor", deps)
        ins = eng.matmul(out, lhsT, rhs, start=start, stop=stop)
        key = _ap_span(out)[0]
        pend = self._mm_pend.setdefault(key, [])
        pend.extend(reads)
        if stop:
            val = self.inc(ins, "pe")
            self._record(pend, [out], "pe", val)
            self._mm_pend[key] = []
            return ("pe", val)
        return None

    def transpose(self, out_psum, src, identity):
        eng = self.nc.tensor
        reads = [src, identity]
        deps = self._deps_for(reads, [out_psum], own_sem="pe")
        self._apply_waits(eng, "tensor", deps)
        ins = eng.transpose(out_psum, src, identity)
        val = self.inc(ins, "pe")
        self._record(reads, [out_psum], "pe", val)
        return ("pe", val)

    def _qname(self, qeng):
        return {id(self.nc.sync): "sp", id(self.nc.gpsimd): "gp",
                id(self.nc.scalar): "act"}[id(qeng)]

    def dma(self, qeng, out, in_, **kw):
        qn = self._qname(qeng)
        st = self._qstate.setdefault(qn, {"ctr": 0})
        sname = f"q_{qn}{st['ctr'] % 4}"
        st["ctr"] += 1
        deps = self._deps_for([in_], [out])
        prev = self.sems[sname][1]
        if prev > 0:
            deps[sname] = max(deps.get(sname, 0), prev)
        ename = f"q_{qn}"
        waited = self._waited.setdefault(ename, {})
        for sem, val in sorted(deps.items()):
            if waited.get(sem, 0) >= val:
                continue
            qeng.wait_ge(self.sems[sem][0], val)
            waited[sem] = val
        ins = qeng.dma_start(out=out, in_=in_, **kw)
        val = self.inc(ins, sname, 16)
        self._record([in_], [out], sname, val)
        return (sname, val)

    def wait_tok(self, eng, tok):
        if tok is not None:
            eng.wait_ge(self.sems[tok[0]][0], tok[1])

    def x_tile(self, tt):
        return self.X[:, tt * D:(tt + 1) * D]

    def tmpb(self, i):
        return self.TMPB[:, (i % 2) * D:(i % 2) * D + D]

    def xt_chunk(self, c, t0=0, tn=T):
        return self.XT[:, c * XTW + PAD + t0: c * XTW + PAD + t0 + tn]

    def pe_transpose(self, jobs):
        pend = []
        for i, (src, dst) in enumerate(jobs):
            bank = (self.PTA, self.PTB)[(i // 8) % 2]
            s = i % 8
            pt = bank[:, s * 128:(s + 1) * 128]
            self.transpose(pt, src, self.ident[:, :])
            pend.append((pt, dst))
            if len(pend) == 8:
                for (p, d) in pend:
                    self.V("tensor_copy", d, p)
                pend = []
        for (p, d) in pend:
            self.V("tensor_copy", d, p)

    def pe_transpose_f32(self, jobs, rows=128):
        for i, (src, dst) in enumerate(jobs):
            bank = (self.PTF, self.PF)[i % 2]
            ncols = src.shape[-1]
            pt = bank[0:ncols, 0:rows]
            self.transpose(pt, src, self.identf[0:rows, 0:rows])
            self.V("tensor_copy", dst, pt)


def emit_load_stream(P, x_dram, build_xt=True):
    nc = P.nc
    xr = x_dram.rearrange("(tt p) d -> p tt d", p=128)
    P.dma(nc.sync, P.X[:, :].rearrange("p (tt d) -> p tt d", tt=NT), xr)
    P.dma(nc.sync, P.ident[:, :], P.inputs["ident"])
    P.dma(nc.sync, P.identf[:, :], P.inputs["identf"])
    for c in range(NC):
        P.V("memset", P.XT[:, c * XTW: c * XTW + PAD], 0.0)
    if build_xt:
        emit_rebuild_xt(P)


def emit_rebuild_xt(P):
    for tt in range(NT):
        tb = P.tmpb(tt)
        P.V("tensor_copy", tb, P.x_tile(tt))
        jobs = []
        for c in range(NC):
            src = tb[:, c * 128:(c + 1) * 128]
            dst = P.XT[:, c * XTW + PAD + tt * 128: c * XTW + PAD + (tt + 1) * 128]
            jobs.append((src, dst))
        P.pe_transpose(jobs)


def emit_ln_stats(P):
    st = P.stats
    scr = NT * 8
    for tt in range(NT):
        xt = P.x_tile(tt)
        o = tt * 8
        P.V("bn_stats", st[:, scr:scr + 6], xt[:, 0:512])
        P.V("bn_stats", st[:, scr + 6:scr + 12], xt[:, 512:1024])
        P.V("bn_aggr", st[:, o:o + 2], st[:, scr:scr + 12])
        P.V("tensor_scalar", st[:, o + 2:o + 3], st[:, o + 1:o + 2], 1e-5, None,
            ALU.add)
    for tt in range(NT):
        o = tt * 8
        P.S("activation", st[:, o + 3:o + 4], st[:, o + 2:o + 3], AF.Sqrt)
    for tt in range(NT):
        o = tt * 8
        P.V("reciprocal", st[:, o + 4:o + 5], st[:, o + 3:o + 4])
        P.V("tensor_tensor", st[:, o + 6:o + 7], st[:, o:o + 1],
            st[:, o + 4:o + 5], ALU.mult)
        P.V("tensor_scalar", st[:, o + 5:o + 6], st[:, o + 6:o + 7], -1.0, None,
            ALU.mult)


def emit_boundary_ln(P, rebuild=True):
    st = P.stats
    emit_ln_stats(P)
    for tt in range(NT):
        o = tt * 8
        P.V("tensor_scalar", P.x_tile(tt), P.x_tile(tt),
            st[:, o + 4:o + 5], st[:, o + 5:o + 6], ALU.mult, ALU.add)
    if rebuild:
        emit_rebuild_xt(P)


def emit_store_stream(P, x_out_dram):
    nc = P.nc
    xr = x_out_dram.rearrange("(tt p) d -> p tt d", p=128)
    tok = P.dma(nc.sync, xr, P.X[:, :].rearrange("p (tt d) -> p tt d", tt=NT))
    P.wait_tok(nc.sync, tok)
    return tok


def emit_router_logits(P, rw_dram, E, lgts, tag, pre_scale=None, rwf_hook=None,
                       wsf_overlay=False):
    nc = P.nc
    if wsf_overlay:
        rwf = P.WSF[:, 640:640 + NC * E]
        rwh = P.WSF[:, 704:704 + NC * E // 2].bitcast(BF16)
        rwl = P.WSF[:, 736:736 + NC * E // 2].bitcast(BF16)
        rowbuf = P.WSF[0:E, 128:640]
    else:
        rwf = P.sb(f"rwf_{tag}", [128, NC * E], F32)
        rwh = P.sb(f"rwh_{tag}", [128, NC * E], BF16)
        rwl = P.sb(f"rwl_{tag}", [128, NC * E], BF16)
        rowbuf = P.sb(f"rowb_{tag}", [E, 512], F32)
    xlotf = P.WSA[:, 0:NC * T]

    P.dma(nc.sync, rwf[:, :].rearrange("p (c e) -> p c e", c=NC),
          rw_dram.rearrange("(c p) e -> p c e", p=128))
    if rwf_hook is not None:
        rwf_hook(rwf)
    if pre_scale is not None:
        for c in range(NC):
            P.V("tensor_scalar", rwf[:, c * E:(c + 1) * E],
                rwf[:, c * E:(c + 1) * E], pre_scale[:, c:c + 1], None, ALU.mult)
    P.V("tensor_copy", rwh[:, :], rwf[:, :])
    P.V("tensor_tensor", rwl[:, :], rwf[:, :], rwh[:, :], ALU.subtract)

    for tt in range(NT):
        tb = P.tmpb(tt)
        P.V("tensor_copy", tb, P.x_tile(tt))
        P.V("tensor_tensor", tb, P.x_tile(tt), tb, ALU.subtract)
        jobs = []
        for c in range(NC):
            jobs.append((tb[:, c * 128:(c + 1) * 128],
                         xlotf[:, c * T + tt * 128: c * T + (tt + 1) * 128]))
        P.pe_transpose(jobs)

    plv = P.PF[0:E, 0:512]
    for ts in range(T // 512):
        for c in range(NC):
            hiT = P.xt_chunk(c, ts * 512, 512)
            loT = xlotf[:, c * T + ts * 512: c * T + ts * 512 + 512]
            rh = rwh[:, c * E:(c + 1) * E]
            rl = rwl[:, c * E:(c + 1) * E]
            P.mm(plv, rh, hiT, start=(c == 0), stop=False)
            P.mm(plv, rh, loT, start=False, stop=False)
            P.mm(plv, rl, hiT, start=False, stop=(c == NC - 1))
        P.V("tensor_copy", rowbuf[:, :], plv)
        jobs = []
        for q in range(4):
            tt = ts * 4 + q
            jobs.append((rowbuf[0:E, q * 128:(q + 1) * 128],
                         lgts[:, tt * E:(tt + 1) * E]))
        P.pe_transpose_f32(jobs, rows=E)


def emit_dense_moe(P, sel, w1_dram, w2_dram, E, I, tag, hidden_act=AF.Silu,
                   up_off=None, b1_sb=None, xt=None):
    nc = P.nc
    xt = xt or P.xt_chunk
    NIC = I // 128
    gate_up = up_off is not None
    nreg = 4 if gate_up else 2
    wq = P.WSB[:, 0:nreg * NC * 256 * 2]
    w2s = P.WSC[:, 0:8 * 1024]
    HT = P.WSA[:, 0:8 * T]
    sg = P.sb(f"{tag}_sg", [128, 512], BF16)
    NSEC = I // 1024

    for e in range(E):
        for sec in range(NSEC):
            P.dma(nc.gpsimd, w2s.rearrange("p (i n) -> p i n", i=8),
                  w2_dram[e, sec * 1024:(sec + 1) * 1024]
                  .rearrange("(i p) n -> p i n", p=128))
            for q in range(2):
                qq = sec * 2 + q
                b = qq % 2
                wqv = wq[:, b * nreg * NC * 256:(b + 1) * nreg * NC * 256]
                g0 = 512 * qq
                srcs = [g0, g0 + 256]
                if gate_up:
                    srcs += [up_off + g0, up_off + g0 + 256]
                for r, base in enumerate(srcs):
                    P.dma(nc.gpsimd,
                          wqv[:, r * NC * 256:(r + 1) * NC * 256]
                          .rearrange("p (c n) -> p c n", c=NC),
                          w1_dram[e, :, base:base + 256]
                          .rearrange("(c p) n -> p c n", p=128))
                for ic in range(4):
                    reg = ic // 2
                    off = (ic % 2) * 128
                    for ts in range(4):
                        it = (ic * 4 + ts) % 2
                        pg = P.PP[2 * it][:, :]
                        pu = P.PP[2 * it + 1][:, :]
                        for c in range(NC):
                            gw = wqv[:, reg * NC * 256 + c * 256 + off:
                                     reg * NC * 256 + c * 256 + off + 128]
                            hiT = xt(c, ts * 512, 512)
                            P.mm(pg, gw, hiT, start=(c == 0), stop=(c == NC - 1))
                            if gate_up:
                                uw = wqv[:, (2 + reg) * NC * 256 + c * 256 + off:
                                         (2 + reg) * NC * 256 + c * 256 + off + 128]
                                P.mm(pu, uw, hiT, start=(c == 0),
                                     stop=(c == NC - 1))
                        hc = q * 4 + ic
                        ichunk_g = sec * 8 + hc
                        hdst = HT[:, hc * T + ts * 512: hc * T + ts * 512 + 512]
                        bias = (b1_sb[:, e * NIC + ichunk_g: e * NIC + ichunk_g + 1]
                                if b1_sb is not None else 0.0)
                        if gate_up:
                            P.S("activation", sg[:, :], pg, hidden_act, bias=bias)
                            P.V("tensor_tensor", hdst, sg[:, :], pu, ALU.mult)
                        else:
                            P.S("activation", hdst, pg, hidden_act, bias=bias)
            for tt in range(NT):
                for dh in range(2):
                    it = (tt * 2 + dh) % 2
                    py = P.PP[2 * it][:, :]
                    for hc in range(8):
                        lh = HT[:, hc * T + tt * 128: hc * T + (tt + 1) * 128]
                        rw2 = w2s[:, hc * 1024 + dh * 512:
                                  hc * 1024 + dh * 512 + 512]
                        P.mm(py, lh, rw2, start=(hc == 0), stop=(hc == 7))
                    xdst = P.X[:, tt * D + dh * 512: tt * D + dh * 512 + 512]
                    if sel is not None:
                        P.V("scalar_tensor_tensor", xdst, py,
                            sel[:, tt * E + e: tt * E + e + 1], xdst,
                            ALU.mult, ALU.add)
                    else:
                        P.V("tensor_tensor", xdst, xdst, py, ALU.add)


def emit_L3_ffn(P, rw_dram, w13_dram, w2_dram):
    E = 8
    lgts = P.sb("l3f_lgts", [128, NT * E], F32)
    sel = P.sb("l3f_sel", [128, NT * E], F32)
    probs = P.sb("l3f_probs", [128, NT * E], F32)
    scr = P.sb("l3f_scr", [128, NT * 8], F32)
    emit_router_logits(P, rw_dram, E, lgts, "l3f")

    for tt in range(NT):
        lg = lgts[:, tt * E:(tt + 1) * E]
        pr = probs[:, tt * E:(tt + 1) * E]
        t8 = scr[:, tt * 8: tt * 8 + 8]
        P.V("tensor_reduce", t8[:, 0:1], lg, AX.X, ALU.max, negate=True)
        P.S("activation", pr, lg, AF.Exp, bias=t8[:, 0:1], scale=1.0,
            accum_out=t8[:, 1:2])
        P.V("reciprocal", t8[:, 2:3], t8[:, 1:2])
        P.V("tensor_scalar", pr, pr, t8[:, 2:3], None, ALU.mult)
        P.V("max", t8[:, 0:8], pr)
        P.V("tensor_tensor", t8[:, 6:7], t8[:, 0:1], t8[:, 1:2], ALU.add)
        P.V("tensor_scalar", t8[:, 6:7], t8[:, 6:7], 1e-6, None, ALU.add)
        P.V("reciprocal", t8[:, 7:8], t8[:, 6:7])
        sl = sel[:, tt * E:(tt + 1) * E]
        sc1 = lg
        P.V("tensor_scalar", sc1, pr, t8[:, 0:1], None, ALU.is_equal)
        P.V("tensor_scalar", sc1, sc1, t8[:, 0:1], None, ALU.mult)
        P.V("tensor_scalar", sl, pr, t8[:, 1:2], None, ALU.is_equal)
        P.V("tensor_scalar", sl, sl, t8[:, 1:2], None, ALU.mult)
        P.V("tensor_tensor", sl, sl, sc1, ALU.add)
        P.V("tensor_scalar", sl, sl, t8[:, 7:8], None, ALU.mult)

    emit_dense_moe(P, sel, w13_dram, w2_dram, E=8, I=2048, tag="l3f", up_off=2048)
    emit_boundary_ln(P)


def build_l3_ffn_prog():
    P = Prog("l3ffn")
    x_in = P.dram_in("x_in", [T, D])
    rw = P.dram_in("rw_t", [D, 8])
    w13 = P.dram_in("w13", [8, 1024, 4096])
    w2 = P.dram_in("w2", [8, 2048, 1024])
    P.dram_in("ident", [128, 128], BF16)
    P.dram_in("identf", [128, 128], F32)
    x_out = P.dram_out("x_out", [T, D])
    P.begin()
    emit_load_stream(P, x_in)
    emit_L3_ffn(P, rw, w13, w2)
    emit_store_stream(P, x_out)
    P.end()
    return P


def l3_ffn_in_maps(inputs, x_full):
    c = host_constants()
    rw_t = _f32(np.asarray(inputs["L3_ffn_router_w"]).T)
    w13 = _f32(inputs["L3_ffn_w13"])
    w2 = _f32(inputs["L3_ffn_w2"])
    maps = []
    for core in range(8):
        maps.append({
            "x_in": _f32(x_full[core]),
            "rw_t": rw_t, "w13": w13, "w2": w2,
            "ident": c["ident"], "identf": c["identf"],
        })
    return maps




def emit_kth_bisect(P, alg, K, kcol, ones_row, ones_col, kA, kB, iters=44):
    nc = P.nc
    sc = P.sb("kth_sc", [1, 16], F32)
    scb = P.sb("kth_scb", [128, 4], F32)
    lo, hi, mid = sc[0:1, 0:1], sc[0:1, 1:2], sc[0:1, 2:3]
    cnt, pp, qq, t1 = sc[0:1, 3:4], sc[0:1, 4:5], sc[0:1, 5:6], sc[0:1, 6:7]
    P.V("tensor_reduce", kA[:, 0:1], alg[:, :], AX.X, ALU.max)
    P.V("tensor_reduce", kA[:, 1:2], alg[:, :], AX.X, ALU.min)
    P.pe_transpose_f32([(kA[:, 0:1], kB[0:1, 0:128]),
                        (kA[:, 1:2], kB[0:1, 128:256])], rows=128)
    P.V("tensor_reduce", hi, kB[0:1, 0:128], AX.X, ALU.max)
    P.V("tensor_reduce", lo, kB[0:1, 128:256], AX.X, ALU.min)
    P.V("tensor_scalar", hi, hi, 1e-3, None, ALU.add)
    P.V("tensor_scalar", lo, lo, -1e-3, None, ALU.add)
    for i in range(iters):
        P.V("tensor_tensor", mid, lo, hi, ALU.add)
        P.V("tensor_scalar", mid, mid, 0.5, None, ALU.mult)
        pb = (P.PF, P.PTF)[i % 2][:, 0:1]
        P.mm(pb, ones_row, mid, start=True, stop=True)
        P.V("tensor_copy", scb[:, 0:1], pb)
        P.V("tensor_scalar", kA[:, :], alg[:, :], scb[:, 0:1], 0.0, ALU.is_lt,
            ALU.add, accum_out=kB[:, 0:1])
        ps = (P.PTF, P.PF)[i % 2][0:1, 0:1]
        P.mm(ps, ones_col, kB[:, 0:1], start=True, stop=True)
        P.V("tensor_copy", cnt, ps)
        P.V("tensor_scalar", pp, cnt, float(K), None, ALU.is_ge)
        P.V("tensor_scalar", qq, pp, -1.0, 1.0, ALU.mult, ALU.add)
        P.V("tensor_tensor", t1, mid, hi, ALU.subtract)
        P.V("tensor_tensor", t1, t1, pp, ALU.mult)
        P.V("tensor_tensor", hi, hi, t1, ALU.add)
        P.V("tensor_tensor", t1, mid, lo, ALU.subtract)
        P.V("tensor_tensor", t1, t1, qq, ALU.mult)
        P.V("tensor_tensor", lo, lo, t1, ALU.add)
    pb = P.PF[:, 0:1]
    P.mm(pb, ones_row, hi, start=True, stop=True)
    P.V("tensor_copy", scb[:, 1:2], pb)
    P.V("tensor_scalar", kA[:, :], alg[:, :], scb[:, 1:2], None, ALU.is_lt)
    P.V("tensor_tensor", kB[:, :], kA[:, :], alg[:, :], ALU.mult)
    P.V("tensor_scalar", kA[:, :], kA[:, :], 1e30, -1e30, ALU.mult, ALU.add)
    P.V("tensor_tensor", kB[:, :], kB[:, :], kA[:, :], ALU.add)
    P.V("tensor_reduce", kA[:, 0:1], kB[:, :], AX.X, ALU.max)
    P.pe_transpose_f32([(kA[:, 0:1], kB[0:1, 0:128])], rows=128)
    P.V("tensor_reduce", mid, kB[0:1, 0:128], AX.X, ALU.max)
    pb = P.PTF[:, 0:1]
    P.mm(pb, ones_row, mid, start=True, stop=True)
    P.V("tensor_copy", kcol, pb)


def emit_L1_ffn(P, all_lg_dram, own_lg_dram, w1t_dram, b1f_dram, w2t_dram,
                b2_dram):
    nc = P.nc
    E, I = 8, 4096
    alg = P.WSF[:, 0:1024]
    kA = P.WSF[:, 1024:2048]
    kB = P.WSF[:, 2048:3072]
    kcol = P.sb("l1f_kcol", [128, 1], F32)
    lgown = P.sb("l1f_lgo", [128, NT * E], F32)
    gates = P.sb("l1f_gates", [128, NT * E], F32)
    gatesT = P.WSF[0:8, 0:T]
    b1f = P.sb("l1f_b1f", [128, E * 32], F32)
    b2sb = P.WSF[0:8, 2048:3072]
    ones_row = P.sb("l1f_or", [1, 128], F32)[:, :]
    ones_col = P.sb("l1f_oc", [128, 1], F32)[:, :]
    kcol = kcol[:, :]
    P.V("memset", ones_row, 1.0)
    P.V("memset", ones_col, 1.0)
    P.dma(nc.sync, alg[:, :].rearrange("p (a e) -> p a e", a=128),
          all_lg_dram.rearrange("(a p) e -> p a e", p=128))
    P.dma(nc.sync, lgown[:, :].rearrange("p (tt e) -> p tt e", tt=NT),
          own_lg_dram.rearrange("(tt p) e -> p tt e", p=128))
    P.dma(nc.sync, b1f[:, :], b1f_dram)
    emit_kth_bisect(P, alg, 24576, kcol, ones_row, ones_col, kA, kB)
    P.dma(nc.sync, b2sb, b2_dram)
    P.V("tensor_scalar", gates[:, :], lgown[:, :], kcol[:, 0:1], None, ALU.is_ge)
    P.V("tensor_tensor", gates[:, :], gates[:, :], lgown[:, :], ALU.mult)
    jobs = []
    for tt in range(NT):
        jobs.append((gates[:, tt * E:(tt + 1) * E],
                     gatesT[0:E, tt * 128:(tt + 1) * 128]))
    P.pe_transpose_f32(jobs, rows=128)
    emit_dense_moe(P, gates, w1t_dram, w2t_dram, E=8, I=4096, tag="l1f",
                   hidden_act=AF.Gelu_apprx_tanh, up_off=None, b1_sb=b1f)
    for tt in range(NT):
        for dh in range(2):
            py = (P.PF, P.PTF)[(tt * 2 + dh) % 2][:, :]
            P.mm(py, gatesT[0:E, tt * 128:(tt + 1) * 128],
                 b2sb[0:E, dh * 512:(dh + 1) * 512], start=True, stop=True)
            xdst = P.X[:, tt * D + dh * 512: tt * D + dh * 512 + 512]
            P.V("tensor_tensor", xdst, xdst, py, ALU.add)
    emit_boundary_ln(P)


def build_l1_logits_prog():
    P = Prog("l1lg")
    x_in = P.dram_in("x_in", [T, D])
    gw = P.dram_in("gw_t", [D, 8])
    P.dram_in("ident", [128, 128], BF16)
    P.dram_in("identf", [128, 128], F32)
    lg_out = P.dram_out("lg_out", [T, 8])
    P.begin()
    emit_load_stream(P, x_in)
    lgts = P.sb("l1lg_lg", [128, NT * 8], F32)
    emit_router_logits(P, gw, 8, lgts, "l1lg")
    tok = P.dma(P.nc.sync, lg_out.rearrange("(tt p) e -> p tt e", p=128),
                lgts[:, :].rearrange("p (tt e) -> p tt e", tt=NT))
    P.wait_tok(P.nc.sync, tok)
    P.end()
    return P


def build_l1_ffn_prog():
    P = Prog("l1ffn")
    x_in = P.dram_in("x_in", [T, D])
    all_lg = P.dram_in("all_lg", [8 * T, 8])
    own_lg = P.dram_in("own_lg", [T, 8])
    w1t = P.dram_in("w1t", [8, 1024, 4096])
    b1f = P.dram_in("b1f", [128, 8 * 32])
    w2t = P.dram_in("w2t", [8, 4096, 1024])
    b2 = P.dram_in("b2", [8, 1024])
    P.dram_in("ident", [128, 128], BF16)
    P.dram_in("identf", [128, 128], F32)
    x_out = P.dram_out("x_out", [T, D])
    P.begin()
    emit_load_stream(P, x_in)
    emit_L1_ffn(P, all_lg, own_lg, w1t, b1f, w2t, b2)
    emit_store_stream(P, x_out)
    P.end()
    return P


def l1_logits_in_maps(inputs, x_full):
    c = host_constants()
    gw_t = _f32(np.asarray(inputs["L1_ffn_gate_weight"]).T)
    return [{"x_in": _f32(x_full[core]), "gw_t": gw_t,
             "ident": c["ident"], "identf": c["identf"]} for core in range(8)]


def l1_ffn_in_maps(inputs, x_full, all_logits):
    c = host_constants()
    w1 = np.asarray(inputs["L1_ffn_w1"])
    w2 = np.asarray(inputs["L1_ffn_w2"])
    b1 = np.asarray(inputs["L1_ffn_b1"])
    w1t = _f32(w1.transpose(0, 2, 1))
    w2t = _f32(w2.transpose(0, 2, 1))
    b1f = _f32(b1.reshape(8, 32, 128).transpose(2, 0, 1).reshape(128, 256))
    b2 = _f32(inputs["L1_ffn_b2"])
    maps = []
    for core in range(8):
        maps.append({
            "x_in": _f32(x_full[core]),
            "all_lg": _f32(all_logits.reshape(8 * T, 8)),
            "own_lg": _f32(all_logits[core]),
            "w1t": w1t, "b1f": b1f, "w2t": w2t, "b2": b2,
            "ident": c["ident"], "identf": c["identf"],
        })
    return maps




def emit_l2_expert(P, gate_src, up_src, down_src, xt, selrow_bf, sg, wsbst,
                   ones_bf):
    nc = P.nc
    wq = P.WSB[:, 0:4 * NC * 256]
    w2s = P.WSC[:, 0:8 * 1024]
    HT = P.WSB[:, 4 * NC * 256: 4 * NC * 256 + 4 * T]
    for sec in range(4):
        P.dma(nc.gpsimd, w2s[:, 0:4 * 1024].rearrange("p (i n) -> p i n", i=4),
              down_src[sec * 512:(sec + 1) * 512]
              .rearrange("(i p) n -> p i n", p=128))
        wqv = wq
        g0 = 512 * sec
        for r, (srcd, base) in enumerate(
                [(gate_src, g0), (gate_src, g0 + 256),
                 (up_src, g0), (up_src, g0 + 256)]):
            P.dma(nc.gpsimd,
                  wqv[:, r * NC * 256:(r + 1) * NC * 256]
                  .rearrange("p (c n) -> p c n", c=NC),
                  srcd[:, base:base + 256]
                  .rearrange("(c p) n -> p c n", p=128))
        for ts in range(4):
            if selrow_bf is not None:
                pb = P.PF[:, :]
                P.mm(pb, ones_bf, selrow_bf[0:1, ts * 512:(ts + 1) * 512],
                     start=True, stop=True)
                P.V("tensor_copy", wsbst[:, :], pb)
            for ic in range(4):
                reg = ic // 2
                off = (ic % 2) * 128
                it = ic % 2
                pg = P.PP[2 * it][:, :]
                pu = P.PP[2 * it + 1][:, :]
                for c in range(NC):
                    gw = wqv[:, reg * NC * 256 + c * 256 + off:
                             reg * NC * 256 + c * 256 + off + 128]
                    uw = wqv[:, (2 + reg) * NC * 256 + c * 256 + off:
                             (2 + reg) * NC * 256 + c * 256 + off + 128]
                    hiT = xt(c, ts * 512, 512)
                    P.mm(pg, gw, hiT, start=(c == 0), stop=(c == NC - 1))
                    P.mm(pu, uw, hiT, start=(c == 0), stop=(c == NC - 1))
                hdst = HT[:, ic * T + ts * 512: ic * T + ts * 512 + 512]
                if selrow_bf is not None:
                    P.V("tensor_tensor", sg[:, :], pg, wsbst[:, :], ALU.mult)
                    P.S("activation", sg[:, :], sg[:, :], AF.Silu)
                    P.V("tensor_tensor", hdst, pu, wsbst[:, :], ALU.mult)
                    P.V("tensor_tensor", hdst, hdst, sg[:, :], ALU.mult)
                else:
                    P.S("activation", sg[:, :], pg, AF.Silu)
                    P.V("tensor_tensor", hdst, sg[:, :], pu, ALU.mult)
        for tt in range(NT):
            for dh in range(2):
                it = (tt * 2 + dh) % 2
                py = P.PP[2 * it][:, :]
                for hc in range(4):
                    lh = HT[:, hc * T + tt * 128: hc * T + (tt + 1) * 128]
                    rw2 = w2s[:, hc * 1024 + dh * 512: hc * 1024 + dh * 512 + 512]
                    P.mm(py, lh, rw2, start=(hc == 0), stop=(hc == 3))
                xdst = P.X[:, tt * D + dh * 512: tt * D + dh * 512 + 512]
                P.V("tensor_tensor", xdst, xdst, py, ALU.add)


def emit_L2_ffn(P, rw_dram, wrow_dram, brow_dram, wfeat_dram, bfeat_dram,
                gate_dram, up_dram, down_dram, shg_dram, shu_dram, shd_dram):
    nc = P.nc
    E = 8
    lgts = P.sb("l2f_lgts", [128, NT * E], F32)
    sel = P.sb("l2f_sel", [128, NT * E], F32)
    scr = P.sb("l2f_scr", [128, NT * 4], F32)
    csb = P.sb("l2f_csb", [128, E], F32)
    bvb = P.sb("l2f_bvb", [128, E], F32)
    wfeat = P.sb("l2f_wf", [128, NC], F32)
    bfeat = P.sb("l2f_bf", [128, NC], F32)
    sg = P.sb("l2f_sg", [128, 512], BF16)
    wsbst = P.sb("l2f_wsb", [128, 512], BF16)
    sm1 = P.sb("l2f_sm1", [1, 8], F32)
    cs_row = P.sb("l2f_cs", [1, 8], F32)
    rwf2 = P.sb("l2f_rwf2", [128, NC * E], F32)
    onescol = P.sb("l2f_oc", [128, 1], F32)[:, :]
    ones_row = P.sb("l2f_or", [1, 128], F32)[:, :]
    ones_bf = P.sb("l2f_orb", [1, 128], BF16)[:, :]
    P.V("memset", onescol, 1.0)
    P.V("memset", ones_row, 1.0)
    P.V("memset", ones_bf, 1.0)
    P.dma(nc.sync, wfeat[:, :], wfeat_dram)
    P.dma(nc.sync, bfeat[:, :], bfeat_dram)
    wrow_sb = P.WSF[0:1, 0:1024]
    brow_sb = P.WSF[0:1, 1024:2048]
    P.dma(nc.sync, wrow_sb, wrow_dram)
    P.dma(nc.sync, brow_sb, brow_dram)
    emit_ln_stats(P)

    def hook(rwf):
        pb = P.PF[0:1, 0:E]
        for c in range(NC):
            P.mm(pb, bfeat[:, c:c + 1], rwf[:, c * E:(c + 1) * E],
                 start=(c == 0), stop=(c == NC - 1))
        P.V("tensor_copy", sm1[0:1, 0:E], pb)

    emit_router_logits(P, rw_dram, E, lgts, "l2f", pre_scale=wfeat,
                       rwf_hook=hook)
    P.dma(nc.sync, rwf2[:, :].rearrange("p (c e) -> p c e", c=NC),
          rw_dram.rearrange("(c p) e -> p c e", p=128))
    for c in range(NC):
        P.V("tensor_scalar", rwf2[:, c * E:(c + 1) * E],
            rwf2[:, c * E:(c + 1) * E], wfeat[:, c:c + 1], None, ALU.mult)
    pcs = P.PTF[0:1, 0:E]
    for c in range(NC):
        P.mm(pcs, onescol, rwf2[:, c * E:(c + 1) * E],
             start=(c == 0), stop=(c == NC - 1))
    P.V("tensor_copy", cs_row[0:1, 0:E], pcs)
    pb = P.PF[:, 0:E]
    P.mm(pb, ones_row, cs_row[0:1, 0:E], start=True, stop=True)
    P.V("tensor_copy", csb[:, :], pb)
    pb2 = P.PTF[:, 0:E]
    P.mm(pb2, ones_row, sm1[0:1, 0:E], start=True, stop=True)
    P.V("tensor_copy", bvb[:, :], pb2)

    st = P.stats
    for tt in range(NT):
        o = tt * 8
        lg = lgts[:, tt * E:(tt + 1) * E]
        P.V("tensor_scalar", lg, lg, st[:, o + 4:o + 5], None, ALU.mult)
        P.V("scalar_tensor_tensor", lg, csb[:, :], st[:, o + 5:o + 6], lg,
            ALU.mult, ALU.add)
        P.V("tensor_tensor", lg, lg, bvb[:, :], ALU.add)
        P.V("tensor_reduce", scr[:, tt * 4:tt * 4 + 1], lg, AX.X, ALU.max)
        P.S("activation", scr[:, tt * 4 + 1:tt * 4 + 2],
            scr[:, tt * 4:tt * 4 + 1], AF.Sigmoid)
        sl = sel[:, tt * E:(tt + 1) * E]
        P.V("tensor_scalar", sl, lg, scr[:, tt * 4:tt * 4 + 1], None,
            ALU.is_equal)
        P.V("tensor_scalar", sl, sl, scr[:, tt * 4 + 1:tt * 4 + 2], None,
            ALU.mult)

    wb = P.WSF[:, 0:1024]
    bb = P.WSF[:, 1024:2048]
    for half in range(2):
        pbq = P.PP[half][:, :]
        P.mm(pbq, ones_row, wrow_sb[0:1, half * 512:(half + 1) * 512],
             start=True, stop=True)
        pbb = P.PP[2 + half][:, :]
        P.mm(pbb, ones_row, brow_sb[0:1, half * 512:(half + 1) * 512],
             start=True, stop=True)
        P.V("tensor_copy", wb[:, half * 512:(half + 1) * 512], pbq)
        P.V("tensor_copy", bb[:, half * 512:(half + 1) * 512], pbb)

    xnT = P.WSA[:, 0:NC * T]
    for tt in range(NT):
        tb = P.tmpb(tt)
        o = tt * 8
        P.V("tensor_scalar", tb, P.x_tile(tt), st[:, o + 4:o + 5],
            st[:, o + 5:o + 6], ALU.mult, ALU.add)
        P.V("tensor_tensor", tb, tb, wb, ALU.mult)
        P.V("tensor_tensor", tb, tb, bb, ALU.add)
        jobs = []
        for c in range(NC):
            jobs.append((tb[:, c * 128:(c + 1) * 128],
                         xnT[:, c * T + tt * 128: c * T + (tt + 1) * 128]))
        P.pe_transpose(jobs)

    def xt(c, t0, tn):
        return xnT[:, c * T + t0: c * T + t0 + tn]

    for tt in range(NT):
        P.V("tensor_scalar", P.x_tile(tt), P.x_tile(tt), 2.0, None, ALU.mult)

    selrow_bf = P.TMPB[0:1, 0:T]
    for e in range(E):
        jobs = []
        for tt in range(NT):
            jobs.append((sel[:, tt * E + e: tt * E + e + 1],
                         selrow_bf[0:1, tt * 128:(tt + 1) * 128]))
        P.pe_transpose_f32(jobs, rows=128)
        emit_l2_expert(P, gate_dram[e], up_dram[e], down_dram[e], xt,
                       selrow_bf, sg, wsbst, ones_bf)
    emit_l2_expert(P, shg_dram, shu_dram, shd_dram, xt, None, sg, wsbst,
                   ones_bf)
    emit_boundary_ln(P)


def build_l2_ffn_prog():
    P = Prog("l2ffn")
    x_in = P.dram_in("x_in", [T, D])
    rw = P.dram_in("rw", [D, 8])
    wrow = P.dram_in("wrow", [1, D])
    brow = P.dram_in("brow", [1, D])
    wfeat = P.dram_in("wfeat", [128, NC])
    bfeat = P.dram_in("bfeat", [128, NC])
    gate = P.dram_in("gate_w", [8, 1024, 2048])
    up = P.dram_in("up_w", [8, 1024, 2048])
    down = P.dram_in("down_w", [8, 2048, 1024])
    shg = P.dram_in("sh_gate", [1024, 2048])
    shu = P.dram_in("sh_up", [1024, 2048])
    shd = P.dram_in("sh_down", [2048, 1024])
    P.dram_in("ident", [128, 128], BF16)
    P.dram_in("identf", [128, 128], F32)
    x_out = P.dram_out("x_out", [T, D])
    P.begin()
    emit_load_stream(P, x_in)
    emit_L2_ffn(P, rw, wrow, brow, wfeat, bfeat, gate, up, down, shg, shu, shd)
    emit_store_stream(P, x_out)
    P.end()
    return P


def l2_ffn_in_maps(inputs, x_full):
    c = host_constants()
    lw = _f32(inputs["L2_ffn_ln_w"])
    lb = _f32(inputs["L2_ffn_ln_b"])
    maps = []
    for core in range(8):
        maps.append({
            "x_in": _f32(x_full[core]),
            "rw": _f32(inputs["L2_ffn_router_w"]),
            "wrow": lw.reshape(1, D), "brow": lb.reshape(1, D),
            "wfeat": _f32(lw.reshape(NC, 128).T),
            "bfeat": _f32(lb.reshape(NC, 128).T),
            "gate_w": _f32(inputs["L2_ffn_gate_w"]),
            "up_w": _f32(inputs["L2_ffn_up_w"]),
            "down_w": _f32(inputs["L2_ffn_down_w"]),
            "sh_gate": _f32(inputs["L2_ffn_sh_gate"]),
            "sh_up": _f32(inputs["L2_ffn_sh_up"]),
            "sh_down": _f32(inputs["L2_ffn_sh_down"]),
            "ident": c["ident"], "identf": c["identf"],
        })
    return maps




def emit_L3_attn(P, wup_dram, bup_dram, convw_dram, convb_dram, wdel_dram,
                 bdel_dram, wB_dram, bB_dram, wC_dram, bC_dram, alog_dram,
                 dskip_dram, wdown_dram, bdownrow_dram):
    nc = P.nc
    N = 64
    XSf = P.WSA[:, 0:4096].bitcast(F32)
    DLT = P.WSA[:, 4096:8192].bitcast(F32)
    LPT = P.WSA[:, 8192:10244]
    XSb = P.WSA[:, 10244:12292]
    WT = P.WSA[:, 12292:14340]
    GT = P.WSA[:, 12292:14340]
    SRT = P.WSC[:, 0:2048]
    BT = P.WSC[:, 2048:4096]
    CT = P.WSC[:, 4096:6144]
    YSB = P.WSC[:, 6144:8192]
    cvs = P.WSF[:, 0:2048]
    wdsb = P.sb("l3a_wd", [128, 1024], BF16)
    wupsb = P.sb("l3a_wup", [128, 2048], BF16)
    wdelsb = P.sb("l3a_wdel", [128, 128], BF16)
    wBsb = P.sb("l3a_wB", [128, 64], BF16)
    wCsb = P.sb("l3a_wC", [128, 64], BF16)
    smalls = P.sb("l3a_small", [128, 16], F32)
    aL = P.sb("l3a_alog", [128, N], F32)
    acols = P.sb("l3a_acols", [128, N], F32)
    bBc = P.sb("l3a_bB", [64, 2], F32)
    selJ = P.sb("l3a_selJ", [64, 128], F32)
    selE = P.sb("l3a_selE", [64, 128], BF16)
    bdrow = P.sb("l3a_bdrow", [1, 1024], F32)
    bdb = P.WSF[:, 2048:3072]
    onr = P.sb("l3a_or", [1, 128], F32)[:, :]
    P.V("memset", onr, 1.0)

    P.dma(nc.gpsimd, wupsb[:, :].rearrange("p (c u) -> p c u", c=NC),
          wup_dram.rearrange("(c p) u -> p c u", p=128))
    P.dma(nc.sync, smalls[:, 0:2], bup_dram)
    P.dma(nc.sync, smalls[:, 2:5], convw_dram)
    P.dma(nc.sync, smalls[:, 5:6], convb_dram)
    P.dma(nc.gpsimd, wdelsb[:, :], wdel_dram)
    P.dma(nc.sync, smalls[:, 6:7], bdel_dram)
    P.dma(nc.gpsimd, wBsb[:, :], wB_dram)
    P.dma(nc.sync, bBc[:, 0:1], bB_dram)
    P.dma(nc.gpsimd, wCsb[:, :], wC_dram)
    P.dma(nc.sync, bBc[:, 1:2], bC_dram)
    P.dma(nc.sync, aL[:, :], alog_dram)
    P.dma(nc.sync, smalls[:, 7:8], dskip_dram)
    P.dma(nc.gpsimd, wdsb[:, :], wdown_dram)
    P.dma(nc.sync, bdrow[:, :], bdownrow_dram)
    P.S("activation", acols[:, :], aL[:, :], AF.Exp)
    P.V("tensor_scalar", acols[:, :], acols[:, :], -1.0, None, ALU.mult)
    for half in range(2):
        pb = P.PP[half][:, :]
        P.mm(pb, onr, bdrow[0:1, half * 512:(half + 1) * 512], start=True,
             stop=True)
        P.V("tensor_copy", bdb.bitcast(F32)[:, half * 512:(half + 1) * 512]
            if False else bdb[:, half * 512:(half + 1) * 512], pb)

    P.V("memset", LPT[:, 0:4], 0.0)
    for half, dst in ((0, None), (1, None)):
        for ts in range(4):
            pu = P.PP[2 + (ts % 2)][:, :]
            for c in range(NC):
                P.mm(pu, wupsb[:, c * 256 + half * 128: c * 256 + half * 128 + 128],
                     P.xt_chunk(c, ts * 512, 512),
                     start=(c == 0), stop=(c == NC - 1))
            if half == 0:
                P.V("tensor_scalar", LPT[:, 4 + ts * 512: 4 + ts * 512 + 512],
                    pu, smalls[:, 0:1], None, ALU.add)
            else:
                P.S("activation", SRT[:, ts * 512:(ts + 1) * 512], pu, AF.Silu,
                    bias=smalls[:, 1:2])

    P.V("tensor_scalar", cvs[:, :], LPT[:, 2:2 + 2048], smalls[:, 2:3], None,
        ALU.mult)
    P.V("scalar_tensor_tensor", cvs[:, :], LPT[:, 3:3 + 2048], smalls[:, 3:4],
        cvs[:, :], ALU.mult, ALU.add)
    P.V("scalar_tensor_tensor", cvs[:, :], LPT[:, 4:4 + 2048], smalls[:, 4:5],
        cvs[:, :], ALU.mult, ALU.add)
    P.S("activation", XSf[:, :], cvs[:, :], AF.Silu, bias=smalls[:, 5:6])
    P.V("tensor_copy", XSb[:, :], XSf[:, :])

    for ts in range(4):
        pz = P.PP[ts % 2][:, :]
        P.mm(pz, wdelsb[:, :], XSb[:, ts * 512:(ts + 1) * 512], start=True,
             stop=True)
        z = cvs[:, ts * 512:(ts + 1) * 512]
        P.V("tensor_scalar", z, pz, smalls[:, 6:7], None, ALU.add)
        za = P.PP[2 + ts % 2]
        dz = DLT[:, ts * 512:(ts + 1) * 512]
        P.S("activation", dz, z, AF.Abs)
        P.S("activation", dz, dz, AF.Exp, scale=-1.0)
        P.S("activation", dz, dz, AF.Ln, bias=1.0)
        P.V("tensor_scalar", z, z, 0.0, None, ALU.max)
        P.V("tensor_tensor", dz, dz, z, ALU.add)
    for ts in range(4):
        pb_ = P.PP[ts % 2][0:64, :]
        P.mm(pb_, wBsb[:, :], XSb[:, ts * 512:(ts + 1) * 512], start=True,
             stop=True)
        P.V("tensor_scalar", BT[0:64, ts * 512:(ts + 1) * 512], pb_,
            bBc[:, 0:1], None, ALU.add)
        pc_ = P.PP[2 + ts % 2][0:64, :]
        P.mm(pc_, wCsb[:, :], XSb[:, ts * 512:(ts + 1) * 512], start=True,
             stop=True)
        P.V("tensor_scalar", CT[0:64, ts * 512:(ts + 1) * 512], pc_,
            bBc[:, 1:2], None, ALU.add)
    P.V("tensor_tensor", WT[:, :], DLT[:, :], XSf[:, :], ALU.mult)

    for n in range(N):
        o = (n % 2) * 8192
        dAn = P.WSB[:, o:o + 2048]
        dBn = P.WSB[:, o + 2048:o + 4096]
        hn = P.WSB[:, o + 4096:o + 6144]
        un = P.WSB[:, o + 6144:o + 8192]
        P.G("iota", selJ[:, :], [[0, 128]], base=-n, channel_multiplier=1,
            allow_small_or_imprecise_dtypes=True)
        P.V("tensor_scalar", selE[:, :], selJ[:, :], 0.0, None, ALU.is_equal)
        P.S("activation", dAn, DLT[:, :], AF.Exp, scale=acols[:, n:n + 1])
        for ts in range(4):
            pbb = (P.PF, P.PTF)[ts % 2][:, :]
            P.mm(pbb, selE[:, :], BT[0:64, ts * 512:(ts + 1) * 512],
                 start=True, stop=True)
            P.V("tensor_tensor", dBn[:, ts * 512:(ts + 1) * 512],
                WT[:, ts * 512:(ts + 1) * 512], pbb, ALU.mult)
        P.V("tensor_tensor_scan", hn, dAn, dBn, 0.0, ALU.mult, ALU.add)
        for ts in range(4):
            pcc = (P.PTF, P.PF)[ts % 2][:, :]
            P.mm(pcc, selE[:, :], CT[0:64, ts * 512:(ts + 1) * 512],
                 start=True, stop=True)
            P.V("tensor_tensor", un[:, ts * 512:(ts + 1) * 512],
                hn[:, ts * 512:(ts + 1) * 512], pcc, ALU.mult)
        for ts in range(4):
            P.mm(P.PP[ts][:, :], P.ident[:, :], un[:, ts * 512:(ts + 1) * 512],
                 start=(n == 0), stop=(n == N - 1))
    for ts in range(4):
        P.V("scalar_tensor_tensor", YSB[:, ts * 512:(ts + 1) * 512],
            XSf[:, ts * 512:(ts + 1) * 512], smalls[:, 7:8], P.PP[ts][:, :],
            ALU.mult, ALU.add)
    P.V("tensor_tensor", GT[:, :], YSB[:, :], SRT[:, :], ALU.mult)
    for tt in range(NT):
        for dh in range(2):
            po = P.PP[(tt * 2 + dh) % 4][:, :]
            P.mm(po, GT[:, tt * 128:(tt + 1) * 128],
                 wdsb[:, dh * 512:(dh + 1) * 512], start=True, stop=True)
            xdst = P.X[:, tt * D + dh * 512: tt * D + dh * 512 + 512]
            P.V("tensor_tensor", xdst, xdst, po, ALU.add)
            P.V("tensor_tensor", xdst, xdst,
                bdb[:, dh * 512:(dh + 1) * 512], ALU.add)
    emit_boundary_ln(P)


def build_l3_attn_prog():
    P = Prog("l3attn")
    x_in = P.dram_in("x_in", [T, D])
    wup = P.dram_in("wup", [1024, 256])
    bup = P.dram_in("bup", [128, 2])
    convw = P.dram_in("convw", [128, 3])
    convb = P.dram_in("convb", [128, 1])
    wdel = P.dram_in("wdel", [128, 128])
    bdel = P.dram_in("bdel", [128, 1])
    wB = P.dram_in("wB", [128, 64])
    bB = P.dram_in("bB", [64, 1])
    wC = P.dram_in("wC", [128, 64])
    bC = P.dram_in("bC", [64, 1])
    alog = P.dram_in("alog", [128, 64])
    dskip = P.dram_in("dskip", [128, 1])
    wdown = P.dram_in("wdown", [128, 1024])
    bdown = P.dram_in("bdown", [1, 1024])
    P.dram_in("ident", [128, 128], BF16)
    P.dram_in("identf", [128, 128], F32)
    x_out = P.dram_out("x_out", [T, D])
    P.begin()
    emit_load_stream(P, x_in)
    emit_L3_attn(P, wup, bup, convw, convb, wdel, bdel, wB, bB, wC, bC, alog,
                 dskip, wdown, bdown)
    emit_store_stream(P, x_out)
    P.end()
    return P


def l3_attn_in_maps(inputs, x_full):
    c = host_constants()
    g = lambda k: _f32(inputs[k])
    base = {
        "wup": g("L3_attn_W_up"),
        "bup": _f32(np.stack([np.asarray(inputs["L3_attn_b_up"])[0:128],
                              np.asarray(inputs["L3_attn_b_up"])[128:256]], 1)),
        "convw": g("L3_attn_conv_w"),
        "convb": _f32(np.asarray(inputs["L3_attn_conv_b"]).reshape(128, 1)),
        "wdel": g("L3_attn_W_delta"),
        "bdel": _f32(np.asarray(inputs["L3_attn_b_delta"]).reshape(128, 1)),
        "wB": g("L3_attn_W_B"),
        "bB": _f32(np.asarray(inputs["L3_attn_b_B"]).reshape(64, 1)),
        "wC": g("L3_attn_W_C"),
        "bC": _f32(np.asarray(inputs["L3_attn_b_C"]).reshape(64, 1)),
        "alog": g("L3_attn_A_log"),
        "dskip": _f32(np.asarray(inputs["L3_attn_D_skip"]).reshape(128, 1)),
        "wdown": g("L3_attn_W_down"),
        "bdown": _f32(np.asarray(inputs["L3_attn_b_down"]).reshape(1, 1024)),
        "ident": c["ident"], "identf": c["identf"],
    }
    maps = []
    for core in range(8):
        m = dict(base)
        m["x_in"] = _f32(x_full[core])
        maps.append(m)
    return maps




def _alibi_slopes16():
    import math
    start = 2.0 ** (-2.0 ** (-(math.log2(16) - 3)))
    return [start * start ** i for i in range(16)]


def emit_L0_attn(P, wqt, bqt, aqt, wkt, bkt, akt, wvt, bvt, avt, wpt,
                 prec_dram, msk_dram):
    nc = P.nc
    H, dh = 16, 64
    QT = P.WSA[:, 0:NC * T]
    KT = P.WSB[:, 0:NC * T]
    VH = P.WSC[:, 0:16 * 512]
    Ssb = P.WSF[:, 0:2048]
    prec = P.WSF[:, 2048:2560]
    mskw = P.sb("l0a_mskw", [128, 896], BF16)
    ws = P.sb("l0a_ws", [128, 4096], BF16)
    lora1 = P.TMPB[0:8, 0:T]
    lora1v = P.sb("l0a_lora1v", [8, T], BF16)
    amt = P.WSF[0:8, 2560:3072].bitcast(BF16)
    cols = P.stats[:, NT * 8: NT * 8 + 16]
    pts = P.stats[:, 0:128].bitcast(BF16)
    bmt2 = P.sb("l0a_bmt2", [128, 3 * 64], BF16)
    P.dma(nc.sync, prec, prec_dram)
    P.dma(nc.sync, mskw[:, :], msk_dram)
    for i, bd in enumerate([bqt, bkt, bvt]):
        P.dma(nc.gpsimd, bmt2[:, i * 64:(i + 1) * 64]
              .rearrange("p (c e) -> p c e", c=NC),
              bd.rearrange("(c p) e -> p c e", p=128))
    slopes = _alibi_slopes16()

    def lora_mid(i, dstb):
        for ts in range(4):
            pl = (P.PF, P.PTF)[ts % 2][0:8, :]
            for c in range(NC):
                P.mm(pl, bmt2[:, i * 64 + c * 8: i * 64 + (c + 1) * 8],
                     P.xt_chunk(c, ts * 512, 512), start=(c == 0),
                     stop=(c == NC - 1))
            P.V("tensor_copy", dstb[0:8, ts * 512: ts * 512 + 512], pl)

    def load_amt(at_dram):
        P.dma(nc.gpsimd, amt[0:8, :], at_dram)
        P.V("tensor_scalar", amt[0:8, :], amt[0:8, :], 2.0, None, ALU.mult)

    def proj_feat(dst, wt_dram, at_dram, i):
        load_amt(at_dram)
        lora_mid(i, lora1)
        for ph in range(2):
            P.dma(nc.gpsimd, ws[:, :].rearrange("p (c e) -> p c e", c=4),
                  wt_dram[ph * 512:(ph + 1) * 512]
                  .rearrange("(c p) e -> p c e", p=128))
            for ec in range(NC):
                for ts in range(4):
                    pq = P.PP[(ec * 4 + ts) % 2][:, :]
                    for c in range(4):
                        P.mm(pq, ws[:, c * 1024 + ec * 128:
                                    c * 1024 + (ec + 1) * 128],
                             P.xt_chunk(ph * 4 + c, ts * 512, 512),
                             start=(c == 0), stop=(ph == 0 and c == 3))
                    ddst = dst[:, ec * T + ts * 512: ec * T + ts * 512 + 512]
                    if ph == 0:
                        P.V("tensor_copy", ddst, pq)
                    else:
                        P.mm(pq, amt[0:8, ec * 128:(ec + 1) * 128],
                             lora1[0:8, ts * 512: ts * 512 + 512],
                             start=False, stop=True)
                        P.V("tensor_tensor", ddst, ddst, pq, ALU.add)

    proj_feat(QT, wqt, aqt, 0)
    proj_feat(KT, wkt, akt, 1)
    load_amt(avt)
    lora_mid(2, lora1v)

    for hg in range(2):
        for ph in range(2):
            P.dma(nc.gpsimd, ws[:, 0:2048].rearrange("p (c e) -> p c e", c=4),
                  wvt[ph * 512:(ph + 1) * 512, hg * 512:(hg + 1) * 512]
                  .rearrange("(c p) e -> p c e", p=128))
            for tt in range(NT):
                pv = P.PP[2 + tt % 2][:, :]
                for c in range(4):
                    P.mm(pv, P.xt_chunk(ph * 4 + c, tt * 128, 128),
                         ws[:, c * 512:(c + 1) * 512], start=(c == 0),
                         stop=(ph == 0 and c == 3))
                vdst = VH[:, tt * 512: tt * 512 + 512]
                if ph == 0:
                    P.V("tensor_copy", vdst, pv)
                else:
                    P.mm(pv, lora1v[0:8, tt * 128:(tt + 1) * 128],
                         amt[0:8, hg * 512:(hg + 1) * 512],
                         start=False, stop=True)
                    P.V("tensor_tensor", vdst, vdst, pv, ALU.add)
        for hh in range(8):
            h = hg * 8 + hh
            ec = h // 2
            eo = (h % 2) * 64
            slope = slopes[h]
            for tq in range(NT):
                nstr = tq // 4 + 1
                diag = tq % 4
                par = (hh * NT + tq) % 2
                Ssb_p = Ssb if par == 0 else ws[:, :].bitcast(F32)[:, 0:2048]
                Tm_p = P.TMPB[:, 0:2048]
                co = par * 8
                for st_ in range(nstr):
                    ps = (P.PP[st_ % 2] if par == 0
                          else (P.PF, P.PTF)[st_ % 2])[:, 0:512]
                    qsl = QT[eo:eo + 64,
                             ec * T + tq * 128: ec * T + (tq + 1) * 128]
                    ksl = KT[eo:eo + 64,
                             ec * T + st_ * 512: ec * T + st_ * 512 + 512]
                    P.mm(ps, qsl, ksl, start=True, stop=True)
                    sdst = Ssb_p[:, st_ * 512:(st_ + 1) * 512]
                    cst = slope * (tq * 128 - st_ * 512)
                    P.V("tensor_scalar", sdst, ps, 0.125, cst, ALU.mult,
                        ALU.add)
                    P.V("scalar_tensor_tensor", sdst, prec, slope, sdst,
                        ALU.mult, ALU.add)
                    if st_ == nstr - 1:
                        P.V("tensor_tensor", sdst, sdst,
                            mskw[:, 384 - 128 * diag: 896 - 128 * diag],
                            ALU.add)
                klen = nstr * 512
                P.V("tensor_reduce", cols[:, co:co + 1], Ssb_p[:, 0:klen],
                    AX.X, ALU.max, negate=True)
                P.S("activation", Tm_p[:, 0:klen], Ssb_p[:, 0:klen], AF.Exp,
                    bias=cols[:, co:co + 1],
                    accum_out=cols[:, co + 1:co + 2])
                P.V("reciprocal", cols[:, co + 2:co + 3],
                    cols[:, co + 1:co + 2])
                pav = P.PP[2 + par][0:64, 0:128]
                P.V("tensor_scalar", Tm_p[:, 0:klen], Tm_p[:, 0:klen],
                    cols[:, co + 2:co + 3], None, ALU.mult)
                nkt = tq + 1
                for kt in range(nkt):
                    ptile = (P.PTA, P.PTB)[kt % 2][:, 0:128]
                    P.transpose(ptile, Tm_p[:, kt * 128:(kt + 1) * 128],
                                P.ident[:, :])
                    stg = pts[:, (kt % 2) * 128:(kt % 2) * 128 + 128]
                    P.V("tensor_copy", stg, ptile)
                    P.mm(pav, VH[:, kt * 512 + hh * 64: kt * 512 + hh * 64 + 64],
                         stg, start=(kt == 0), stop=(kt == nkt - 1))
                P.V("tensor_copy",
                    QT[eo:eo + 64, ec * T + tq * 128: ec * T + (tq + 1) * 128],
                    pav)
    for ph in range(2):
        P.dma(nc.gpsimd, ws[:, :].rearrange("p (c e) -> p c e", c=4),
              wpt[ph * 512:(ph + 1) * 512].rearrange("(c p) e -> p c e", p=128))
        for tt in range(NT):
            for dhh in range(2):
                po = P.PP[(tt * 2 + dhh) % 2][:, :]
                for c in range(4):
                    P.mm(po, QT[:, (ph * 4 + c) * T + tt * 128:
                                (ph * 4 + c) * T + (tt + 1) * 128],
                         ws[:, c * 1024 + dhh * 512: c * 1024 + dhh * 512 + 512],
                         start=(c == 0), stop=(c == 3))
                xdst = P.X[:, tt * D + dhh * 512: tt * D + dhh * 512 + 512]
                P.V("tensor_tensor", xdst, xdst, po, ALU.add)
    emit_boundary_ln(P)


def build_l0_attn_prog():
    P = Prog("l0attn")
    x_in = P.dram_in("x_in", [T, D])
    names = ["wqt", "bqt", "aqt", "wkt", "bkt", "akt", "wvt", "bvt", "avt", "wpt"]
    shapes = [[1024, 1024], [1024, 8], [8, 1024]] * 3 + [[1024, 1024]]
    aps = {}
    for nm, sh in zip(names, shapes):
        aps[nm] = P.dram_in(nm, sh)
    prec = P.dram_in("prec", [128, 512])
    msk = P.dram_in("msk", [128, 896], BF16)
    P.dram_in("ident", [128, 128], BF16)
    P.dram_in("identf", [128, 128], F32)
    x_out = P.dram_out("x_out", [T, D])
    P.begin()
    emit_load_stream(P, x_in)
    emit_L0_attn(P, aps["wqt"], aps["bqt"], aps["aqt"], aps["wkt"], aps["bkt"],
                 aps["akt"], aps["wvt"], aps["bvt"], aps["avt"], aps["wpt"],
                 prec, msk)
    emit_store_stream(P, x_out)
    P.end()
    return P


def l0_attn_consts():
    p = np.arange(128)[:, None]
    f = np.arange(512)[None, :]
    prec = (p - f).astype(np.float32)
    g = np.arange(896)[None, :]
    mskw = np.where(p + 384 - g >= 0, 0.0, -1e30)
    return prec, np.ascontiguousarray(mskw, dtype=BF)


def l0_attn_in_maps(inputs, x_full):
    c = host_constants()
    prec, msk = l0_attn_consts()
    base = {
        "wqt": _f32(np.asarray(inputs["L0_attn_Wq"]).T),
        "bqt": _f32(np.asarray(inputs["L0_attn_Bq"]).T),
        "aqt": _f32(np.asarray(inputs["L0_attn_Aq"]).T),
        "wkt": _f32(np.asarray(inputs["L0_attn_Wk"]).T),
        "bkt": _f32(np.asarray(inputs["L0_attn_Bk"]).T),
        "akt": _f32(np.asarray(inputs["L0_attn_Ak"]).T),
        "wvt": _f32(np.asarray(inputs["L0_attn_Wv"]).T),
        "bvt": _f32(np.asarray(inputs["L0_attn_Bv"]).T),
        "avt": _f32(np.asarray(inputs["L0_attn_Av"]).T),
        "wpt": _f32(np.asarray(inputs["L0_attn_Wp"]).T),
        "prec": prec, "msk": msk,
        "ident": c["ident"], "identf": c["identf"],
    }
    maps = []
    for core in range(8):
        m = dict(base)
        m["x_in"] = _f32(x_full[core])
        maps.append(m)
    return maps




def emit_L0_ffn(P, x_in_dram, wqkv_dram, qsrows_dram, rot_dram, wo_dram,
                gbf_dram, w1a_dram, w1g_dram, b1f_dram, w2_dram, b2rows_dram,
                m2_dram):
    nc = P.nc
    HD = 8
    NU = 22
    QK = P.WSA[:, 0:NT * 1024]
    VTOK = P.WSA[:, 0:NT * 512]
    OT = P.WSA[:, NT * 512:NT * 512 + 4 * T]
    XNT = P.WSA[:, 0:NC * T]
    QKT = P.WSB[:, 0:8 * T]
    HMT = P.WSB[:, 0:4 * T]
    RBT = P.WSB[:, 4 * T:6 * T]
    Ssb = P.WSF[:, 0:1024]
    SqS = P.WSF[:, 1024:1536]
    h1S = P.WSF[:, 1536:2048]
    ws = P.WSC[:, 0:8192]
    rot = P.sb("l0f_rot", [128, 4 * 1024], BF16)
    m2 = P.sb("l0f_m2", [128, 1920], BF16)
    qsb = P.sb("l0f_qsb", [128, 4 * 64], F32)
    qsrows = P.WSF[0:1, 1536:2048]
    b2rows = P.WSF[0:1, 0:1024]
    gbf = P.sb("l0f_gbf", [128, 2 * 16], F32)
    b1f = P.sb("l0f_b1f", [128, 2 * 2 * NU], F32)
    dtile = P.sb("l0f_dt", [128, 2 * 64], BF16)[:, :]
    ocb = P.sb("l0f_ocb", [128, 1], BF16)[:, :]
    onr = P.sb("l0f_onr", [1, 128], F32)[:, :]
    cols = P.stats[:, NT * 8: NT * 8 + 16]
    pts = P.stats[:, 0:128].bitcast(BF16)
    gsb = P.TMPB[:, 1024:1536]
    P.V("memset", ocb, 1.0)
    P.V("memset", onr, 1.0)
    P.dma(nc.sync, rot[:, :], rot_dram)
    P.dma(nc.sync, m2[:, :], m2_dram)
    P.dma(nc.sync, gbf[:, :], gbf_dram)
    P.dma(nc.sync, b1f[:, :], b1f_dram)

    def xt_shift(c, t0, tn):
        if c < 4:
            return P.xt_chunk(c, t0, tn)
        return P.XT[:, c * XTW + PAD + t0 - 1: c * XTW + PAD + t0 - 1 + tn]

    for l in range(2):
        P.dma(nc.sync, qsrows[0:1, 0:256], qsrows_dram[0:1, l * 256:(l + 1) * 256])
        for j in range(4):
            pb = P.PP[j % 2][:, 0:64]
            P.mm(pb, onr, qsrows[0:1, j * 64:(j + 1) * 64],
                 start=True, stop=True)
            P.V("tensor_copy", qsb[:, j * 64:(j + 1) * 64], pb)
        for ph in range(2):
            P.dma(nc.gpsimd, ws[:, 0:4 * 1024].rearrange("p (c e) -> p c e", c=4),
                  wqkv_dram[l, ph * 512:(ph + 1) * 512, 0:1024]
                  .rearrange("(c p) e -> p c e", p=128))
            for tt in range(NT):
                for fs in range(2):
                    pq = P.PP[(tt * 2 + fs) % 2][:, :]
                    for c in range(4):
                        P.mm(pq, xt_shift(ph * 4 + c, tt * 128, 128),
                             ws[:, c * 1024 + fs * 512: c * 1024 + fs * 512 + 512],
                             start=(c == 0), stop=(c == 3))
                    qdst = QK[:, tt * 1024 + fs * 512: tt * 1024 + fs * 512 + 512]
                    if ph == 0:
                        P.V("tensor_copy", qdst, pq)
                    else:
                        P.V("tensor_tensor", qdst, qdst, pq, ALU.add)
        for tt in range(NT):
            for pj in range(2):
                seg = QK[:, tt * 1024 + pj * 512: tt * 1024 + pj * 512 + 512]
                seg3 = seg.rearrange("p (h d) -> p h d", h=8)
                P.S("activation", SqS[:, 0:512], seg, AF.Square)
                P.V("tensor_reduce", cols[:, 0:8],
                    SqS[:, 0:512].rearrange("p (h d) -> p h d", h=8), AX.X,
                    ALU.add)
                P.S("activation", cols[:, 8:16], cols[:, 0:8], AF.Sqrt)
                P.V("tensor_scalar", cols[:, 8:16], cols[:, 8:16], 1e-12, None,
                    ALU.max)
                P.V("reciprocal", cols[:, 0:8], cols[:, 8:16])
                sview = bass.AP(cols.tensor, cols.offset,
                                [list(cols.ap[0]), [1, 8], [0, 64]])
                P.V("tensor_tensor", seg3, seg3, sview, ALU.mult)
                c1 = rot[:, (2 * pj) * 1024 + tt * 64:
                         (2 * pj) * 1024 + tt * 64 + 64]
                c2 = rot[:, (2 * pj + 1) * 1024 + tt * 64:
                         (2 * pj + 1) * 1024 + tt * 64 + 64]
                P.V("tensor_tensor", dtile[:, 0:64], c1,
                    qsb[:, (2 * pj) * 64:(2 * pj) * 64 + 64], ALU.mult)
                P.V("tensor_tensor", dtile[:, 64:128], c2,
                    qsb[:, (2 * pj + 1) * 64:(2 * pj + 1) * 64 + 64], ALU.mult)
                d1b = bass.AP(dtile.tensor, dtile.offset,
                              [list(dtile.ap[0]), [0, 8], [1, 64]])
                d2b4 = bass.AP(dtile.tensor, dtile.offset + 64,
                               [list(dtile.ap[0]), [0, 8], [32, 2], [1, 32]])
                aswap = bass.AP(seg.tensor, seg.offset + 32,
                                [list(seg.ap[0]), [64, 8], [-32, 2], [1, 32]])
                tb = P.tmpb(0)[:, 0:512]
                tb3 = tb.rearrange("p (h d) -> p h d", h=8)
                P.V("tensor_tensor", tb3, seg3, d1b, ALU.mult)
                tc = P.tmpb(1)[:, 0:512]
                tc4 = tc.rearrange("p (h e d) -> p h e d", h=8, e=2)
                P.V("tensor_tensor", tc4, aswap, d2b4, ALU.mult)
                P.V("tensor_tensor", seg, tb, tc, ALU.add)
        jobs = []
        for tt in range(NT):
            for c in range(8):
                jobs.append((QK[:, tt * 1024 + c * 128: tt * 1024 + (c + 1) * 128],
                             QKT[:, c * T + tt * 128: c * T + (tt + 1) * 128]))
        P.pe_transpose(jobs)
        for ph in range(2):
            P.dma(nc.gpsimd, ws[:, 0:4 * 512].rearrange("p (c e) -> p c e", c=4),
                  wqkv_dram[l, ph * 512:(ph + 1) * 512, 1024:1536]
                  .rearrange("(c p) e -> p c e", p=128))
            for tt in range(NT):
                pv = P.PP[2 + tt % 2][:, :]
                for c in range(4):
                    P.mm(pv, xt_shift(ph * 4 + c, tt * 128, 128),
                         ws[:, c * 512:(c + 1) * 512],
                         start=(c == 0), stop=(c == 3))
                vdst = VTOK[:, tt * 512:(tt + 1) * 512]
                if ph == 0:
                    P.V("tensor_copy", vdst, pv)
                else:
                    P.V("tensor_tensor", vdst, vdst, pv, ALU.add)
        for h in range(HD):
            qc = h // 2
            eo = (h % 2) * 64
            for w in range(4):
                for q4 in range(4):
                    tq = w * 4 + q4
                    strips = [s_ for s_ in range(2)
                              if not (w == 0 and s_ == 0)]
                    co = (q4 % 2) * 8
                    par = q4 % 2
                    Ssb_p = (Ssb if par == 0
                             else ws[:, 0:2048].bitcast(F32)[:, 0:1024])
                    for s_ in strips:
                        kt0 = (w - 1 + s_) * 4
                        ps = (P.PP[s_ % 2] if par == 0
                              else (P.PF, P.PTF)[s_ % 2])[:, 0:512]
                        qsl = QKT[eo:eo + 64,
                                  qc * T + tq * 128:qc * T + (tq + 1) * 128]
                        ksl = QKT[eo:eo + 64,
                                  (4 + qc) * T + kt0 * 128:
                                  (4 + qc) * T + kt0 * 128 + 512]
                        P.mm(ps, qsl, ksl, start=True, stop=True)
                        sdst = Ssb_p[:, s_ * 512:(s_ + 1) * 512]
                        c2m = q4 * 128 + 512 - s_ * 512
                        P.V("tensor_scalar", sdst, ps, 8.0, None, ALU.mult)
                        P.V("tensor_tensor", sdst, sdst,
                            m2[:, 1024 - c2m:1536 - c2m], ALU.add)
                    s0 = strips[0] * 512
                    klen = len(strips) * 512
                    P.V("tensor_reduce", cols[:, co:co + 1],
                        Ssb_p[:, s0:s0 + klen], AX.X, ALU.max, negate=True)
                    P.S("activation", P.TMPB[:, s0:s0 + klen],
                        Ssb_p[:, s0:s0 + klen], AF.Exp,
                        bias=cols[:, co:co + 1],
                        accum_out=cols[:, co + 1:co + 2])
                    P.V("reciprocal", cols[:, co + 2:co + 3],
                        cols[:, co + 1:co + 2])
                    P.V("tensor_scalar", P.TMPB[:, s0:s0 + klen],
                        P.TMPB[:, s0:s0 + klen], cols[:, co + 2:co + 3], None,
                        ALU.mult)
                    pav = P.PP[2 + (q4 % 2)][0:64, 0:128]
                    kts = []
                    for s_ in strips:
                        for k4 in range(4):
                            kts.append(((w - 1 + s_) * 4 + k4,
                                        s_ * 512 + k4 * 128))
                    for ki, (ktile, pcol) in enumerate(kts):
                        ptile = (P.PTA, P.PTB)[ki % 2][:, 0:128]
                        P.transpose(ptile, P.TMPB[:, pcol:pcol + 128],
                                    P.ident[:, :])
                        stg = pts[:, (ki % 2) * 128:(ki % 2) * 128 + 128]
                        P.V("tensor_copy", stg, ptile)
                        P.mm(pav, VTOK[:, ktile * 512 + h * 64:
                                       ktile * 512 + h * 64 + 64],
                             stg, start=(ki == 0), stop=(ki == len(kts) - 1))
                    P.V("tensor_copy",
                        OT[eo:eo + 64,
                           qc * T + tq * 128: qc * T + (tq + 1) * 128], pav)
        P.dma(nc.gpsimd, ws[:, 0:4 * 1024].rearrange("p (c e) -> p c e", c=4),
              wo_dram[l].rearrange("(c p) e -> p c e", p=128))
        for tt in range(NT):
            for dhh in range(2):
                po = P.PP[(tt * 2 + dhh) % 2][:, :]
                for c in range(4):
                    P.mm(po, OT[:, c * T + tt * 128: c * T + (tt + 1) * 128],
                         ws[:, c * 1024 + dhh * 512: c * 1024 + dhh * 512 + 512],
                         start=(c == 0), stop=(c == 3))
                xdst = P.X[:, tt * D + dhh * 512: tt * D + dhh * 512 + 512]
                P.V("tensor_tensor", xdst, xdst, po, ALU.add)
        emit_rebuild_xt(P)
        for st_ in range(4):
            pm = P.PF[0:1, :]
            for c in range(NC):
                P.mm(pm, ocb, xt_shift(c, st_ * 512, 512), start=(c == 0),
                     stop=(c == NC - 1))
            sacc = Ssb[0:1, 512:1024]
            P.V("memset", sacc, 0.0)
            for c in range(NC):
                sqb = P.tmpb(c)[:, 0:512]
                P.S("activation", sqb, xt_shift(c, st_ * 512, 512), AF.Square)
                pq2 = P.PTF[0:1, :] if c % 2 else P.PG[0:1, :] \
                    if False else P.PTF[0:1, :]
                P.mm(pq2, ocb, sqb, start=True, stop=True)
                P.V("tensor_tensor", sacc, sacc, pq2, ALU.add)
            mrow = h1S[0:1, 0:512]
            P.V("tensor_scalar", mrow, pm, 1.0 / 1024, None, ALU.mult)
            vrow = SqS[0:1, 0:512]
            P.V("tensor_scalar", vrow, sacc, 1.0 / 1024, None, ALU.mult)
            t2 = Ssb[0:1, 0:512]
            P.V("tensor_tensor", t2, mrow, mrow, ALU.mult)
            P.V("tensor_tensor", vrow, vrow, t2, ALU.subtract)
            P.V("tensor_scalar", vrow, vrow, 1e-5, None, ALU.add)
            P.S("activation", vrow, vrow, AF.Sqrt)
            P.V("reciprocal", vrow, vrow)
            rrow = P.TMPB[0:1, 0:512]
            nrow = P.TMPB[0:1, 512:1024]
            P.V("tensor_copy", rrow, vrow)
            P.V("tensor_tensor", t2, mrow, vrow, ALU.mult)
            P.V("tensor_scalar", t2, t2, -1.0, None, ALU.mult)
            P.V("tensor_copy", nrow, t2)
            P.V("memset", pts[0:1, 0:128], 1.0)
            pb = P.PP[st_ % 2][:, :]
            P.mm(pb, pts[0:1, 0:128], rrow, start=True, stop=True)
            P.V("tensor_copy", RBT[:, st_ * 512: st_ * 512 + 512], pb)
            pb2 = P.PP[2 + st_ % 2][:, :]
            P.mm(pb2, pts[0:1, 0:128], nrow, start=True, stop=True)
            P.V("tensor_copy", RBT[:, T + st_ * 512: T + st_ * 512 + 512], pb2)
        for c in range(NC):
            gcol = gbf[:, l * 16 + c: l * 16 + c + 1]
            bcol = gbf[:, l * 16 + 8 + c: l * 16 + 8 + c + 1]
            xc = XNT[:, c * T:(c + 1) * T]
            P.V("tensor_tensor", xc, xt_shift(c, 0, T), RBT[:, 0:T], ALU.mult)
            P.V("tensor_scalar", xc, xc, gcol, None, ALU.mult)
            P.V("scalar_tensor_tensor", xc, RBT[:, T:2 * T], gcol, xc,
                ALU.mult, ALU.add)
            P.V("tensor_scalar", xc, xc, bcol, None, ALU.add)
        for sec in range(6):
            u0 = sec * 4
            nch = min(4, NU - u0)
            if nch <= 0:
                break
            wsa_ = ws[:, 0:NC * nch * 128]
            wsg_ = ws[:, 4096:4096 + NC * nch * 128]
            P.dma(nc.gpsimd, wsa_.rearrange("p (c j) -> p c j", c=NC),
                  w1a_dram[l, :, u0 * 128:(u0 + nch) * 128]
                  .rearrange("(c p) j -> p c j", p=128))
            P.dma(nc.gpsimd, wsg_.rearrange("p (c j) -> p c j", c=NC),
                  w1g_dram[l, :, u0 * 128:(u0 + nch) * 128]
                  .rearrange("(c p) j -> p c j", p=128))
            for j in range(nch):
                for ts in range(4):
                    it = (j * 4 + ts) % 2
                    pa = P.PP[2 * it][:, :]
                    pg = P.PP[2 * it + 1][:, :]
                    for c in range(NC):
                        P.mm(pa, wsa_[:, c * nch * 128 + j * 128:
                                      c * nch * 128 + (j + 1) * 128],
                             XNT[:, c * T + ts * 512: c * T + ts * 512 + 512],
                             start=(c == 0), stop=(c == NC - 1))
                        P.mm(pg, wsg_[:, c * nch * 128 + j * 128:
                                      c * nch * 128 + (j + 1) * 128],
                             XNT[:, c * T + ts * 512: c * T + ts * 512 + 512],
                             start=(c == 0), stop=(c == NC - 1))
                    jj = u0 + j
                    acol = b1f[:, l * 2 * NU + jj: l * 2 * NU + jj + 1]
                    gcol2 = b1f[:, l * 2 * NU + NU + jj: l * 2 * NU + NU + jj + 1]
                    P.S("activation", gsb[:, :], pg, AF.Gelu, bias=gcol2)
                    P.V("tensor_scalar", h1S[:, 0:512], pa, acol, None, ALU.add)
                    P.V("tensor_tensor",
                        HMT[:, j * T + ts * 512: j * T + ts * 512 + 512],
                        h1S[:, 0:512], gsb[:, :], ALU.mult)
            P.dma(nc.gpsimd,
                  ws[:, 0:nch * 1024].rearrange("p (c n) -> p c n", c=nch),
                  w2_dram[l, u0 * 128:(u0 + nch) * 128]
                  .rearrange("(c p) n -> p c n", p=128))
            for tt in range(NT):
                for dhh in range(2):
                    pd = P.PP[2 + (tt * 2 + dhh) % 2][:, :]
                    for j in range(nch):
                        P.mm(pd, HMT[:, j * T + tt * 128: j * T + (tt + 1) * 128],
                             ws[:, j * 1024 + dhh * 512: j * 1024 + dhh * 512 + 512],
                             start=(j == 0), stop=(j == nch - 1))
                    xdst = P.X[:, tt * D + dhh * 512: tt * D + dhh * 512 + 512]
                    P.V("tensor_tensor", xdst, xdst, pd, ALU.add)
        P.dma(nc.sync, b2rows, b2rows_dram[0:1, l * 1024:(l + 1) * 1024])
        for dhh in range(2):
            pb = P.PP[dhh][:, :]
            P.mm(pb, onr, b2rows[0:1, dhh * 512: dhh * 512 + 512], start=True,
                 stop=True)
            bslab = P.WSF[:, 2048 + dhh * 512: 2048 + dhh * 512 + 512]
            P.V("tensor_copy", bslab, pb)
            for tt in range(NT):
                xdst = P.X[:, tt * D + dhh * 512: tt * D + dhh * 512 + 512]
                P.V("tensor_tensor", xdst, xdst, bslab, ALU.add)
        emit_rebuild_xt(P)
    P.dma(nc.gpsimd, P.X[:, :].rearrange("p (tt d) -> p tt d", tt=NT),
          x_in_dram.rearrange("(tt p) d -> p tt d", p=128),
          accum_op=ALU.add)
    emit_boundary_ln(P)


def build_l0_ffn_prog():
    P = Prog("l0ffn")
    x_in = P.dram_in("x_in", [T, D])
    wqkv = P.dram_in("wqkv", [2, 1024, 1536])
    qsrows = P.dram_in("qsrows", [1, 512])
    rot = P.dram_in("rot", [128, 4096], BF16)
    wo = P.dram_in("wo", [2, 512, 1024])
    gbf = P.dram_in("gbf", [128, 32])
    w1a = P.dram_in("w1a", [2, 1024, 2816])
    w1g = P.dram_in("w1g", [2, 1024, 2816])
    b1f = P.dram_in("b1f", [128, 88])
    w2p = P.dram_in("w2p", [2, 2816, 1024])
    b2rows = P.dram_in("b2rows", [1, 2048])
    m2 = P.dram_in("m2", [128, 1920], BF16)
    P.dram_in("ident", [128, 128], BF16)
    P.dram_in("identf", [128, 128], F32)
    x_out = P.dram_out("x_out", [T, D])
    P.begin()
    emit_load_stream(P, x_in)
    emit_L0_ffn(P, x_in, wqkv, qsrows, rot, wo, gbf, w1a, w1g, b1f, w2p,
                b2rows, m2)
    emit_store_stream(P, x_out)
    P.end()
    return P


def l0_ffn_consts():
    DIM_HEAD, WINDOW, n = 64, 512, 2048
    half_idx = np.arange(0, DIM_HEAD, 2, dtype=np.float64)
    inv_freq = 1.0 / 10000.0 ** (half_idx / DIM_HEAD)
    t = np.arange(n, dtype=np.float64)
    freqs = t[:, None] * inv_freq[None, :]
    freqs = np.concatenate([freqs, freqs], axis=-1)
    base = (half_idx + 0.4 * DIM_HEAD) / (1.4 * DIM_HEAD)
    power = (t - n // 2) / float(WINDOW // 2)
    scale = base[None, :] ** power[:, None]
    scale = np.concatenate([scale, scale], axis=-1)
    cos, sin = np.cos(freqs), np.sin(freqs)
    sgn = np.where(np.arange(64) < 32, -1.0, 1.0)[None, :]
    c1q = cos * scale
    c2q = sin * scale * sgn
    c1k = cos / scale
    c2k = sin / scale * sgn
    rot = np.zeros((128, 4, 16, 64), np.float64)
    for kind, cc in enumerate([c1q, c2q, c1k, c2k]):
        rot[:, kind] = cc.reshape(16, 128, 64).transpose(1, 0, 2)
    rot = rot.reshape(128, 4096)
    p = np.arange(128)[:, None]
    g = np.arange(1920)[None, :]
    rel = p + 1024 - g
    m2 = np.where((rel >= 0) & (rel <= 512), 0.0, -1e30)
    return _bf(rot), _bf(m2)


def l0_ffn_in_maps(inputs, x_full):
    c = host_constants()
    rot, m2 = l0_ffn_consts()
    qs = _f32(inputs["L0_ffn_q_scale"])
    ks = _f32(inputs["L0_ffn_k_scale"])
    perm = np.r_[32:64, 0:32]
    qsrows = np.concatenate(
        [np.concatenate([qs[l], qs[l][perm], ks[l], ks[l][perm]])
         for l in range(2)]).reshape(1, 512)
    g = _f32(inputs["L0_ffn_ln_g"])
    b = _f32(inputs["L0_ffn_ln_b"])
    gbf = np.zeros((128, 32), np.float32)
    for l in range(2):
        gbf[:, l * 16:l * 16 + 8] = g[l].reshape(8, 128).T
        gbf[:, l * 16 + 8:l * 16 + 16] = b[l].reshape(8, 128).T
    w1 = _f32(inputs["L0_ffn_w1"])
    b1 = _f32(inputs["L0_ffn_b1"])
    w2 = _f32(inputs["L0_ffn_w2"])
    w1a = np.zeros((2, 1024, 2816), np.float32)
    w1g = np.zeros((2, 1024, 2816), np.float32)
    w1a[:, :, 0:2730] = w1[:, :, 0:2730]
    w1g[:, :, 0:2730] = w1[:, :, 2730:5460]
    b1f = np.zeros((128, 2 * 2 * 22), np.float32)
    for l in range(2):
        ba = np.zeros(2816, np.float32)
        bg = np.zeros(2816, np.float32)
        ba[0:2730] = b1[l, 0:2730]
        bg[0:2730] = b1[l, 2730:5460]
        b1f[:, l * 44:l * 44 + 22] = ba.reshape(22, 128).T
        b1f[:, l * 44 + 22:l * 44 + 44] = bg.reshape(22, 128).T
    w2p = np.zeros((2, 2816, 1024), np.float32)
    w2p[:, 0:2730] = w2
    b2rows = _f32(inputs["L0_ffn_b2"]).reshape(1, 2048)
    base = {
        "wqkv": _f32(inputs["L0_ffn_wqkv"]),
        "qsrows": _f32(qsrows), "rot": rot,
        "wo": _f32(inputs["L0_ffn_wo"]),
        "gbf": gbf, "w1a": w1a, "w1g": w1g, "b1f": b1f, "w2p": w2p,
        "b2rows": b2rows, "m2": m2,
        "ident": c["ident"], "identf": c["identf"],
    }
    maps = []
    for core in range(8):
        m = dict(base)
        m["x_in"] = _f32(x_full[core])
        maps.append(m)
    return maps




def emit_L2_attn(P, wqkv_dram, bqkv_row_dram, wproj_dram, bproj_row_dram,
                 map_i32_dram, s4_row_dram):
    nc = P.nc
    qkv_tok = P.WSA[:, 0:NT * 1024]
    OHS = P.WSC[:, 0:16 * 512]
    OT2 = P.WSB[:, 0:NC * T]
    ws = P.WSC[:, 0:8192]
    bias_slab = P.WSF[:, 2048:3072]
    browq = P.WSF[0:1, 0:1024]
    rowA = P.WSF[0:1, 1024:3072]
    mrow = P.WSB[0:1, 0:4096].bitcast(F32)
    mirow = P.WSB[0:1, 4096:8192].bitcast(I32)
    sgrow = P.WSB[0:1, 8192:12288].bitcast(F32)
    kprow = P.WSB[0:1, 12288:16384].bitcast(F32)
    idxrow = P.WSB[0:1, 4096:8192].bitcast(F32)
    pen_all = P.sb("l2a_pen", [64, 32 * 64], BF16)
    mcols = P.sb("l2a_mcols", [128, 16], F32)
    jcol = P.sb("l2a_jcol", [128, 2], F32)
    thcol = P.sb("l2a_th", [64, 1], F32)
    onr = P.sb("l2a_onr", [1, 128], F32)[:, :]
    obf = P.sb("l2a_obf", [64, 1], BF16)[:, :]
    sexp = P.sb("l2a_sexp", [64, 2 * 64], BF16)
    stg = P.sb("l2a_stg", [128, 3 * 512], BF16)
    lcol = P.sb("l2a_lcol", [64, 2], F32)
    cols = P.stats[:, NT * 8: NT * 8 + 16]
    qmT_d = P.dram_tmp("l2a_qmT", [128, NC, T], BF16)
    kselT_d = P.dram_tmp("l2a_kselT", [128, NC, T], BF16)
    vsel_d = P.dram_tmp("l2a_vsel", [128, NT, 1024], BF16)
    P.V("memset", onr, 1.0)
    P.V("memset", obf, 1.0)
    P.dma(nc.sync, mirow[0:1, :], map_i32_dram)
    P.V("tensor_copy", mrow[0:1, :], mirow[0:1, :])
    P.V("tensor_scalar", mirow[0:1, :], mirow[0:1, :], -256, None,
        ALU.bitwise_and)
    P.V("tensor_copy", sgrow[0:1, :], mirow[0:1, :])
    sgrep = bass.AP(sgrow[0:1, :].tensor, sgrow[0:1, :].offset,
                    [list(sgrow[0:1, :].ap[0]), [64, 32], [0, 64]])
    P.G("iota", kprow[0:1, :].rearrange("o (n s) -> o n s", n=32),
        [[0, 32], [4, 64]], base=0, channel_multiplier=0,
        allow_small_or_imprecise_dtypes=True)
    P.V("tensor_tensor", kprow[0:1, :].rearrange("o (n s) -> o n s", n=32),
        kprow[0:1, :].rearrange("o (n s) -> o n s", n=32), sgrep, ALU.add)
    mask0 = rowA[0:1, 0:2048]
    P.V("tensor_scalar", mask0, kprow[0:1, :], 2048.0, None, ALU.is_lt)
    P.V("tensor_scalar", kprow[0:1, :], kprow[0:1, :], 2047.0, None, ALU.min)
    jobs = []
    for jt in range(NT):
        jobs.append((mrow[0:1, jt * 128:(jt + 1) * 128], mcols[:, jt:jt + 1]))
    P.pe_transpose_f32(jobs, rows=1)
    P.G("iota", jcol[:, 0:1], [[0, 1]], base=0, channel_multiplier=1,
        allow_small_or_imprecise_dtypes=True)
    for st_ in range(4):
        pid = P.PF[0:1, 0:512]
        for jt in range(NT):
            pb = P.PTF[:, :]
            P.mm(pb, onr, kprow[0:1, st_ * 512:(st_ + 1) * 512], start=True,
                 stop=True)
            P.V("tensor_scalar", jcol[:, 1:2], jcol[:, 0:1], float(jt * 128),
                None, ALU.add)
            ohf = stg[:, :].bitcast(F32)[:, 0:512]
            P.V("tensor_scalar", ohf, pb, jcol[:, 1:2], None, ALU.is_equal)
            P.mm(pid, mcols[:, jt:jt + 1], ohf, start=(jt == 0),
                 stop=(jt == NT - 1))
        P.V("tensor_copy", idxrow[0:1, st_ * 512:(st_ + 1) * 512], pid)
    P.V("tensor_scalar", idxrow[0:1, :], idxrow[0:1, :], 2047.0, None, ALU.min)
    P.V("tensor_tensor", idxrow[0:1, :], idxrow[0:1, :], mask0, ALU.mult)
    P.V("tensor_scalar", mask0, mask0, 1.0, -1.0, ALU.mult, ALU.add)
    P.V("tensor_tensor", idxrow[0:1, :], idxrow[0:1, :], mask0, ALU.add)
    P.V("tensor_scalar", mrow[0:1, :], mrow[0:1, :], 2047.0, None, ALU.min)

    def produce(src_lo):
        P.dma(nc.sync, browq, bqkv_row_dram[0:1, src_lo:src_lo + 1024])
        for half in range(2):
            pb = P.PP[half][:, :]
            P.mm(pb, onr, browq[0:1, half * 512:(half + 1) * 512], start=True,
                 stop=True)
            P.V("tensor_copy", bias_slab[:, half * 512:(half + 1) * 512], pb)
        for ph in range(2):
            P.dma(nc.gpsimd, ws[:, 0:4 * 1024].rearrange("p (c e) -> p c e", c=4),
                  wqkv_dram[ph * 512:(ph + 1) * 512, src_lo:src_lo + 1024]
                  .rearrange("(c p) e -> p c e", p=128))
            for tt in range(NT):
                for fs in range(2):
                    pq = P.PP[2 + (tt * 2 + fs) % 2][:, :]
                    for c in range(4):
                        P.mm(pq, P.xt_chunk(ph * 4 + c, tt * 128, 128),
                             ws[:, c * 1024 + fs * 512: c * 1024 + fs * 512 + 512],
                             start=(c == 0), stop=(c == 3))
                    qdst = qkv_tok[:, tt * 1024 + fs * 512:
                                   tt * 1024 + fs * 512 + 512]
                    if ph == 0:
                        P.V("tensor_tensor", qdst, pq,
                            bias_slab[:, fs * 512:(fs + 1) * 512], ALU.add)
                    else:
                        P.V("tensor_tensor", qdst, qdst, pq, ALU.add)

    def gather(rowsel, dst_dram, feat_major, scale):
        for st_ in range(4):
            for jt in range(NT):
                pb = P.PTF[:, :]
                P.mm(pb, onr, rowsel[0:1, st_ * 512:(st_ + 1) * 512],
                     start=True, stop=True)
                P.V("tensor_scalar", jcol[:, 1:2], jcol[:, 0:1],
                    float(jt * 128), None, ALU.add)
                oh = OHS[:, jt * 512:(jt + 1) * 512]
                P.V("tensor_scalar", oh, pb, jcol[:, 1:2], None, ALU.is_equal)
            if feat_major:
                for c in range(NC):
                    pg = P.PP[c % 2][:, :]
                    for jt in range(NT):
                        P.mm(pg, qkv_tok[:, jt * 1024 + c * 128:
                                         jt * 1024 + (c + 1) * 128],
                             OHS[:, jt * 512:(jt + 1) * 512],
                             start=(jt == 0), stop=(jt == NT - 1))
                    sg_ = stg[:, (c % 3) * 512:(c % 3) * 512 + 512]
                    P.V("tensor_scalar", sg_, pg, scale, None, ALU.mult)
                    P.dma(nc.sync, dst_dram[:, c, st_ * 512:(st_ + 1) * 512],
                          sg_)
            else:
                for q4 in range(4):
                    ot = st_ * 4 + q4
                    for dh in range(2):
                        pg = P.PP[(q4 * 2 + dh) % 2][:, :]
                        for jt in range(NT):
                            P.mm(pg, OHS[:, jt * 512 + q4 * 128:
                                         jt * 512 + (q4 + 1) * 128],
                                 qkv_tok[:, jt * 1024 + dh * 512:
                                         jt * 1024 + dh * 512 + 512],
                                 start=(jt == 0), stop=(jt == NT - 1))
                        sg_ = stg[:, dh * 512:(dh + 1) * 512]
                        P.V("tensor_copy", sg_, pg)
                        P.dma(nc.sync,
                              dst_dram[:, ot, dh * 512:dh * 512 + 512], sg_)

    produce(0)
    gather(mrow[0:1, :], qmT_d, True, 0.125)
    produce(1024)
    gather(idxrow[0:1, :], kselT_d, True, 1.0)
    produce(2048)
    gather(idxrow[0:1, :], vsel_d, False, 1.0)

    qmc = P.TMPB[:, 0:T]
    ksc = P.sb("l2a_ksc", [128, T], BF16)
    vsc2 = OHS[0:64, 0:4096]
    P.G("iota", thcol[0:64, 0:1], [[0, 1]], base=0, channel_multiplier=-4,
        allow_small_or_imprecise_dtypes=True)
    P.V("tensor_scalar", thcol[0:64, 0:1], thcol[0:64, 0:1], 2048.0, None,
        ALU.add)
    for n in range(32):
        psg = P.PTF[0:64, 0:64]
        P.mm(psg, onr[0:1, 0:64], sgrow[0:1, n * 64:(n + 1) * 64],
             start=True, stop=True)
        pen = pen_all[0:64, n * 64:(n + 1) * 64]
        P.V("tensor_scalar", pen, psg, thcol[0:64, 0:1], None, ALU.is_ge)
        P.V("tensor_scalar", pen, pen, -1e9, None, ALU.mult)
    for c in range(NC):
        P.dma(nc.sync, qmc, qmT_d[:, c])
        P.dma(nc.sync, ksc[:, :], kselT_d[:, c])
        P.dma(nc.sync,
              vsc2[0:64, :].rearrange("p (n b) -> p n b", n=32)[:, 0:32:2, :],
              vsel_d[0:64, :, c * 128:(c + 1) * 128])
        P.dma(nc.sync,
              vsc2[0:64, :].rearrange("p (n b) -> p n b", n=32)[:, 1:32:2, :],
              vsel_d[64:128, :, c * 128:(c + 1) * 128])
        for n in range(32):
            pen = pen_all[0:64, n * 64:(n + 1) * 64]
            for hh in range(2):
                eo = hh * 64
                pst = P.PP[hh][0:64, 0:64]
                P.mm(pst, ksc[eo:eo + 64, n * 64:(n + 1) * 64],
                     qmc[eo:eo + 64, n * 64:(n + 1) * 64], start=True,
                     stop=True)
                sT = P.WSF[0:64, 1024:1088]
                P.V("tensor_tensor", sT, pst, pen, ALU.add)
                P.V("tensor_reduce", cols[0:64, 1:2], sT, AX.X, ALU.max,
                    negate=True)
                se = sexp[0:64, hh * 64:(hh + 1) * 64]
                P.S("activation", se, sT, AF.Exp, bias=cols[0:64, 1:2])
                pls = P.PF[0:1, 0:64]
                P.mm(pls, obf, se, start=True, stop=True)
                P.V("tensor_copy", rowA[0:1, 0:64], pls)
                P.pe_transpose_f32([(rowA[0:1, 0:64], lcol[0:64, 0:1])],
                                   rows=1)
                P.V("tensor_scalar", lcol[0:64, 0:1], lcol[0:64, 0:1], 1e-6,
                    None, ALU.add)
                P.V("reciprocal", lcol[0:64, 1:2], lcol[0:64, 0:1])
                pav = P.PP[2 + hh][0:64, 0:64]
                vslice = vsc2[0:64, n * 128 + hh * 64: n * 128 + hh * 64 + 64]
                P.mm(pav, se, vslice, start=True, stop=True)
                osb = stg[0:64, 0:128]
                P.V("tensor_scalar", osb[0:64, hh * 64:(hh + 1) * 64], pav,
                    lcol[0:64, 1:2], None, ALU.mult)
            ptile = (P.PTA, P.PTB)[n % 2][:, 0:64]
            P.transpose(ptile, stg[0:64, 0:128], P.ident[0:64, 0:64])
            P.V("tensor_copy",
                OT2[:, c * T + n * 64: c * T + n * 64 + 64], ptile)
    for ph in range(2):
        P.dma(nc.gpsimd, ws[:, 0:4 * 1024].rearrange("p (c e) -> p c e", c=4),
              wproj_dram[ph * 512:(ph + 1) * 512]
              .rearrange("(c p) e -> p c e", p=128))
        for tt in range(NT):
            for dhh in range(2):
                po = P.PP[(tt * 2 + dhh) % 2][:, :]
                for c in range(4):
                    P.mm(po, OT2[:, (ph * 4 + c) * T + tt * 128:
                                 (ph * 4 + c) * T + (tt + 1) * 128],
                         ws[:, c * 1024 + dhh * 512: c * 1024 + dhh * 512 + 512],
                         start=(c == 0), stop=(c == 3))
                xdst = P.X[:, tt * D + dhh * 512: tt * D + dhh * 512 + 512]
                P.V("tensor_tensor", xdst, xdst, po, ALU.add)
    P.dma(nc.sync, browq, bproj_row_dram)
    for dhh in range(2):
        pb = P.PP[dhh][:, :]
        P.mm(pb, onr, browq[0:1, dhh * 512: dhh * 512 + 512], start=True,
             stop=True)
        bslab = bias_slab[:, dhh * 512:(dhh + 1) * 512]
        P.V("tensor_copy", bslab, pb)
        for tt in range(NT):
            xdst = P.X[:, tt * D + dhh * 512: tt * D + dhh * 512 + 512]
            P.V("tensor_tensor", xdst, xdst, bslab, ALU.add)
    emit_boundary_ln(P)


def build_l2_attn_prog():
    P = Prog("l2attn")
    x_in = P.dram_in("x_in", [T, D])
    wqkv = P.dram_in("wqkv", [1024, 3072])
    bqr = P.dram_in("bqr", [1, 3072])
    wproj = P.dram_in("wproj", [1024, 1024])
    bpr = P.dram_in("bpr", [1, 1024])
    mapi = P.dram_in("mapi", [1, 2048], I32)
    P.dram_in("ident", [128, 128], BF16)
    P.dram_in("identf", [128, 128], F32)
    x_out = P.dram_out("x_out", [T, D])
    P.begin()
    emit_load_stream(P, x_in)
    emit_L2_attn(P, wqkv, bqr, wproj, bpr, mapi, None)
    emit_store_stream(P, x_out)
    P.end()
    return P


def l2_attn_in_maps(inputs, x_full):
    c = host_constants()
    base = {
        "wqkv": _f32(inputs["L2_attn_Wqkv"]),
        "bqr": _f32(inputs["L2_attn_bqkv"]).reshape(1, 3072),
        "wproj": _f32(inputs["L2_attn_Wproj"]),
        "bpr": _f32(inputs["L2_attn_bproj"]).reshape(1, 1024),
        "mapi": np.ascontiguousarray(
            np.asarray(inputs["L2_attn_mapping"])[:2048].reshape(1, 2048),
            dtype=np.int32),
        "ident": c["ident"], "identf": c["identf"],
    }
    maps = []
    for core in range(8):
        m = dict(base)
        m["x_in"] = _f32(x_full[core])
        maps.append(m)
    return maps




def emit_L1_attn(P, wqt_dram, wkt_dram, wvt_dram, wot_dram, bof_dram,
                 bot_dram):
    nc = P.nc
    QT = P.WSA[:, 0:NC * T]
    KT = P.WSB[:, 0:NC * T]
    ws = P.WSC[:, 0:8192]
    vtc = P.sb("l1a_vtc", [128, 2048], BF16)[:, :]
    prodb8 = P.sb("l1a_prod", [128, 8 * 256], BF16)
    bof = P.sb("l1a_bof", [128, NC * 16], BF16)
    bot = P.sb("l1a_bot", [16, 1024], BF16)
    SSC = P.WSF[0:16, 0:2048]
    SMX = P.WSF[0:16, 2048:2560]
    Ps = P.sb("l1a_ps", [16, 2048], BF16)
    acc = P.WSF[:, 2560:3072].bitcast(BF16).bitcast(F32) if False else P.WSF[:, 2560:3072]
    vT_d = P.dram_tmp("l1a_vT", [128, NC, T], BF16)
    oT_d = P.dram_tmp("l1a_oT", [128, NC, T], BF16)
    P.dma(nc.sync, bof[:, :], bof_dram)
    P.dma(nc.sync, bot[0:16, :], bot_dram)

    def proj_feat(dst_sb, dst_dram, wt_dram):
        for ph in range(2):
            P.dma(nc.gpsimd, ws[:, 0:4 * 1024].rearrange("p (c e) -> p c e", c=4),
                  wt_dram[ph * 512:(ph + 1) * 512]
                  .rearrange("(c p) e -> p c e", p=128))
            for ec in range(NC):
                for ts in range(4):
                    pq = P.PP[(ec * 4 + ts) % 2][:, :]
                    for c in range(4):
                        P.mm(pq, ws[:, c * 1024 + ec * 128:
                                    c * 1024 + (ec + 1) * 128],
                             P.xt_chunk(ph * 4 + c, ts * 512, 512),
                             start=(c == 0), stop=(c == 3))
                    if dst_sb is not None:
                        ddst = dst_sb[:, ec * T + ts * 512: ec * T + ts * 512 + 512]
                        if ph == 0:
                            P.V("tensor_copy", ddst, pq)
                        else:
                            P.V("tensor_tensor", ddst, ddst, pq, ALU.add)
                    else:
                        ddst = vtc[:, ts * 512: ts * 512 + 512]
                        P.V("tensor_copy", ddst, pq)
                if dst_sb is None:
                    if ph == 0:
                        P.dma(nc.sync, dst_dram[:, ec], vtc[:, :])
                    else:
                        P.dma(nc.gpsimd, dst_dram[:, ec], vtc[:, :],
                              accum_op=ALU.add)

    proj_feat(QT, None, wqt_dram)
    proj_feat(KT, None, wkt_dram)
    proj_feat(None, vT_d, wvt_dram)

    for s in range(8):
        for sp in range(8):
            psc = P.PF[0:16, 0:256]
            for c in range(NC):
                pr = prodb8[:, c * 256:(c + 1) * 256]
                P.V("tensor_tensor", pr,
                    QT[:, c * T + s * 256:(c * T) + s * 256 + 256],
                    KT[:, c * T + sp * 256: c * T + sp * 256 + 256], ALU.mult)
                P.mm(psc, bof[:, c * 16:(c + 1) * 16], pr,
                     start=(c == 0), stop=(c == NC - 1))
            sscv = bass.AP(SSC[0:16, :].tensor, SSC[0:16, :].offset + sp,
                           [list(SSC[0:16, :].ap[0]), [8, 256]])
            P.V("tensor_scalar", sscv, psc, 0.125, None, ALU.mult)
        s3 = SSC[0:16, :].rearrange("p (t e) -> p t e", t=256)
        P.V("tensor_reduce", SMX[0:16, 0:256], s3, AX.X, ALU.max)
        mb = bass.AP(SMX[0:16, :].tensor, SMX[0:16, :].offset,
                     [list(SMX[0:16, :].ap[0]), [1, 256], [0, 8]])
        P.V("tensor_tensor", s3, s3, mb, ALU.subtract)
        P.S("activation", Ps[0:16, :], SSC[0:16, :], AF.Exp)
        p3 = Ps[0:16, :].rearrange("p (t e) -> p t e", t=256)
        P.V("tensor_reduce", SMX[0:16, 256:512], p3, AX.X, ALU.add)
        P.V("reciprocal", SMX[0:16, 256:512], SMX[0:16, 256:512])
        rb = bass.AP(SMX[0:16, :].tensor, SMX[0:16, :].offset + 256,
                     [list(SMX[0:16, :].ap[0]), [1, 256], [0, 8]])
        P.V("tensor_tensor", p3, p3, rb, ALU.mult)
        for c in range(NC):
            P.dma(nc.sync, vtc[:, :], vT_d[:, c])
            ab = acc[:, (c % 2) * 256:(c % 2) * 256 + 256]
            for sp in range(8):
                pb = (P.PTF, P.PG if False else P.PF)[0][:, 0:256] \
                    if False else (P.PTF, P.PF)[sp % 2][:, 0:256]
                psl = bass.AP(Ps[0:16, :].tensor, Ps[0:16, :].offset + sp,
                              [list(Ps[0:16, :].ap[0]), [8, 256]])
                P.mm(pb, bot[0:16, c * 128:(c + 1) * 128], psl, start=True,
                     stop=True)
                if sp == 0:
                    P.V("tensor_tensor", ab,
                        vtc[:, sp * 256:(sp + 1) * 256], pb, ALU.mult)
                else:
                    tt1 = P.TMPB[:, 0:256].bitcast(F32)[:, 0:128] \
                        if False else P.TMPB[:, 0:512].bitcast(F32)[:, 0:256]
                    P.V("tensor_tensor", tt1, vtc[:, sp * 256:(sp + 1) * 256],
                        pb, ALU.mult)
                    P.V("tensor_tensor", ab, ab, tt1, ALU.add)
            sgo = P.TMPB[:, 512:768]
            P.V("tensor_copy", sgo, ab)
            P.dma(nc.sync, oT_d[:, c, s * 256:(s + 1) * 256], sgo)
    for c in range(NC):
        P.dma(nc.sync, QT[:, c * T:(c + 1) * T], oT_d[:, c])
    for ph in range(2):
        P.dma(nc.gpsimd, ws[:, 0:4 * 1024].rearrange("p (c e) -> p c e", c=4),
              wot_dram[ph * 512:(ph + 1) * 512]
              .rearrange("(c p) e -> p c e", p=128))
        for tt in range(NT):
            for dhh in range(2):
                po = P.PP[(tt * 2 + dhh) % 2][:, :]
                for c in range(4):
                    P.mm(po, QT[:, (ph * 4 + c) * T + tt * 128:
                                 (ph * 4 + c) * T + (tt + 1) * 128],
                         ws[:, c * 1024 + dhh * 512: c * 1024 + dhh * 512 + 512],
                         start=(c == 0), stop=(c == 3))
                xdst = P.X[:, tt * D + dhh * 512: tt * D + dhh * 512 + 512]
                P.V("tensor_tensor", xdst, xdst, po, ALU.add)
    emit_boundary_ln(P)


def build_l1_attn_prog():
    P = Prog("l1attn")
    x_in = P.dram_in("x_in", [T, D])
    wqt = P.dram_in("wqt", [1024, 1024])
    wkt = P.dram_in("wkt", [1024, 1024])
    wvt = P.dram_in("wvt", [1024, 1024])
    wot = P.dram_in("wot", [1024, 1024])
    bof = P.dram_in("bof", [128, NC * 16], BF16)
    bot = P.dram_in("bot", [16, 1024], BF16)
    gw = P.dram_in("gw_t", [D, 8])
    P.dram_in("ident", [128, 128], BF16)
    P.dram_in("identf", [128, 128], F32)
    x_out = P.dram_out("x_out", [T, D])
    lg_out = P.dram_out("lg_out", [T, 8])
    P.begin()
    emit_load_stream(P, x_in)
    emit_L1_attn(P, wqt, wkt, wvt, wot, bof, bot)
    emit_store_stream(P, x_out)
    lgts = P.WSF[:, 0:NT * 8]
    emit_router_logits(P, gw, 8, lgts, "l1alg", wsf_overlay=True)
    tok = P.dma(P.nc.sync, lg_out.rearrange("(tt p) e -> p tt e", p=128),
                lgts[:, :].rearrange("p (tt e) -> p tt e", tt=NT))
    P.wait_tok(P.nc.sync, tok)
    P.end()
    return P


def l1_attn_in_maps(inputs, x_full):
    c = host_constants()
    W = np.asarray(inputs["L1_attn_in_proj_weight"])
    Wt = W.T
    H, hd = 16, 64
    idx = np.arange(3072).reshape(H, 3, hd)
    qcols = idx[:, 0, :].reshape(-1)
    kcols = idx[:, 1, :].reshape(-1)
    vcols = idx[:, 2, :].reshape(-1)
    wqt = _f32(Wt[:, qcols])
    wkt = _f32(Wt[:, kcols])
    wvt = _f32(Wt[:, vcols])
    wot = _f32(np.asarray(inputs["L1_attn_out_weight"]).T)
    d = np.arange(1024)
    bo = (d[:, None] // 64 == np.arange(16)[None, :]).astype(np.float32)
    bof = _bf(bo.reshape(8, 128, 16).transpose(1, 0, 2).reshape(128, 128))
    bot = _bf(bo.T)
    base = {"wqt": wqt, "wkt": wkt, "wvt": wvt, "wot": wot,
            "bof": bof, "bot": bot,
            "gw_t": _f32(np.asarray(inputs["L1_ffn_gate_weight"]).T),
            "ident": c["ident"], "identf": c["identf"]}
    maps = []
    for core in range(8):
        m = dict(base)
        xs = np.ascontiguousarray(
            np.asarray(x_full)[:, core * 256:(core + 1) * 256, :]
        ).reshape(T, D)
        m["x_in"] = _f32(xs)
        maps.append(m)
    return maps




_PROGS = {}


def _prog(name):
    if name not in _PROGS:
        _PROGS[name] = {
            "l0_attn": build_l0_attn_prog,
            "l0_ffn": build_l0_ffn_prog,
            "l1_attn": build_l1_attn_prog,
            "l1_logits": build_l1_logits_prog,
            "l1_ffn": build_l1_ffn_prog,
            "l2_attn": build_l2_attn_prog,
            "l2_ffn": build_l2_ffn_prog,
            "l3_attn": build_l3_attn_prog,
            "l3_ffn": build_l3_ffn_prog,
        }[name]()
    return _PROGS[name]


def _run(name, maps, outkey="x_out"):
    P = _prog(name)
    res = run_bass_kernel_spmd(P.nc, maps, core_ids=list(range(8)))
    return np.stack([res.results[c][outkey] for c in range(8)])


def kernel(**inputs):
    x = np.ascontiguousarray(np.asarray(inputs["L0_attn_x"], np.float32))
    x = _run("l0_attn", l0_attn_in_maps(inputs, x))
    x = _run("l0_ffn", l0_ffn_in_maps(inputs, x))
    P1 = _prog("l1_attn")
    res1 = run_bass_kernel_spmd(P1.nc, l1_attn_in_maps(inputs, x),
                                core_ids=list(range(8)))
    xp = np.stack([res1.results[c]["x_out"] for c in range(8)])
    lgp = np.stack([res1.results[c]["lg_out"] for c in range(8)])
    x = np.concatenate([xp[i].reshape(8, 256, D) for i in range(8)], axis=1)
    lg = np.concatenate([lgp[i].reshape(8, 256, 8) for i in range(8)], axis=1)
    x = _run("l1_ffn", l1_ffn_in_maps(inputs, x, lg))
    x = _run("l2_attn", l2_attn_in_maps(inputs, x))
    x = _run("l2_ffn", l2_ffn_in_maps(inputs, x))
    x = _run("l3_attn", l3_attn_in_maps(inputs, x))
    x = _run("l3_ffn", l3_ffn_in_maps(inputs, x))
    return np.ascontiguousarray(x, np.float32)
